# version 05
# baseline (speedup 1.0000x reference)
import numpy as np
import concourse.bass as bass
import concourse.mybir as mybir
from concourse.bass_utils import run_bass_kernel_spmd

F32 = mybir.dt.float32
BF16 = mybir.dt.bfloat16
AF = mybir.ActivationFunctionType
ALU = mybir.AluOpType
AX = mybir.AxisListType

MAGIC = float(3 * 2 ** 22)
NCORES = 8
T, D, H, DH, DR, LAT, FF = 8192, 2048, 16, 128, 64, 512, 8192
NT = T // 128
NOWN = NT // NCORES
SCALE = float(np.float32(1.0 / np.sqrt(DH + DR)))
LN10K = float(np.log(10000.0))

STAGE = 4
DEBUG_DUMPS = False

ENGS = ("sync", "gpsimd", "scalar", "vector", "tensor")


class Step:
    __slots__ = ("eng", "fn", "dma", "waits", "mark", "cum", "tag", "eidx")

    def __init__(self, eng, fn, dma, tag):
        self.eng, self.fn, self.dma, self.tag = eng, fn, dma, tag
        self.waits = []
        self.mark = False
        self.cum = 0
        self.eidx = 0

    @property
    def queue(self):
        return ("q_" + self.eng) if self.dma else self.eng


class Res:
    __slots__ = ("w", "rs", "name")

    def __init__(self, name=""):
        self.w = None
        self.rs = []
        self.name = name


class Buf:
    __slots__ = ("ap", "res")

    def __init__(self, ap, name=""):
        self.ap = ap
        self.res = Res(name)

    def __getitem__(self, key):
        return self.ap[key]


class KB:
    def __init__(self, nc):
        self.nc = nc
        self.steps = []
        self.sems = {}

    def do(self, eng, fn, reads=(), writes=(), extra=(), dma=False, tag=""):
        st = Step(eng, fn, dma, tag)
        seen = {}

        def addw(w, raw):
            if w is None:
                return
            if id(w) in seen:
                if raw and not seen[id(w)][1]:
                    seen[id(w)][1] = True
                return
            ent = [w, raw]
            seen[id(w)] = ent
            st.waits.append(ent)

        for r in reads:
            addw(r.res.w if isinstance(r, Buf) else r.w, True)
        for r in writes:
            rr = r.res if isinstance(r, Buf) else r
            addw(rr.w, False)
            for s in rr.rs:
                addw(s, False)
        for w in extra:
            addw(w, False)
        self.steps.append(st)
        for r in reads:
            rr = r.res if isinstance(r, Buf) else r
            rr.rs.append(st)
        for r in writes:
            rr = r.res if isinstance(r, Buf) else r
            rr.w = st
            rr.rs = []
        return st

    def barrier(self):
        firsts = []
        last_by_q = {}
        for st in self.steps:
            last_by_q[st.queue] = st
        dma_lasts = [s for q, s in last_by_q.items() if q.startswith("q_")]
        for eng in ENGS:
            firsts.append(self.do(eng, lambda e: e.nop(), extra=dma_lasts,
                                   tag="bar1"))
        for eng in ENGS:
            self.do(eng, lambda e: e.nop(), extra=firsts, tag="bar2")

    FENCE_WINDOW = 10

    def emit(self, block):
        eidx = {}
        for st in self.steps:
            eidx[st.eng] = eidx.get(st.eng, 0) + 1
            st.eidx = eidx[st.eng]
        for st in self.steps:
            if st.dma:
                st.mark = True
            for w, raw in st.waits:
                if w.dma or w.eng != st.eng:
                    w.mark = True
                elif raw and st.eidx - w.eidx <= self.FENCE_WINDOW:
                    w.mark = True
        cum = {}
        for st in self.steps:
            q = st.queue
            if st.mark:
                cum[q] = cum.get(q, 0) + (16 if st.dma else 1)
            st.cum = cum.get(q, 0)
        for q in set(s.queue for s in self.steps):
            self.sems[q] = self.nc.alloc_semaphore(q)

        def mk_body(eng):
            def body(e):
                water = {}
                for st in self.steps:
                    if st.eng != eng:
                        continue
                    wmap = {}
                    for w, raw in st.waits:
                        if (not w.dma) and w.eng == eng and (
                                (not raw) or st.eidx - w.eidx > KB.FENCE_WINDOW):
                            continue
                        assert w.mark and w.cum > 0
                        q = w.queue
                        if wmap.get(q, 0) < w.cum:
                            wmap[q] = w.cum
                    for q, v in sorted(wmap.items()):
                        if water.get(q, 0) >= v:
                            continue
                        e.wait_ge(self.sems[q], v)
                        water[q] = v
                    inst = st.fn(e)
                    if st.tag:
                        inst.annotate(st.tag)
                    if st.mark:
                        inst.then_inc(self.sems[st.queue],
                                      16 if st.dma else 1)
            return body

        block.sync(mk_body("sync"))
        block.gpsimd(mk_body("gpsimd"))
        block.scalar(mk_body("scalar"))
        block.vector(mk_body("vector"))
        block.tensor(mk_body("tensor"))


class Arena:

    _n = 0

    def __init__(self, nc, nbytes):
        Arena._n += 1
        self.t = nc.alloc_sbuf_tensor(f"arena{Arena._n}", [128, nbytes // 4], F32)
        self.nbytes = nbytes
        self.cur = 0

    def reset(self):
        self.cur = 0

    def sub(self, name, shape, dtype):
        assert len(shape) == 2
        esz = 2 if dtype == BF16 else 4
        nb = shape[1] * esz
        off = (self.cur + 31) & ~31
        self.cur = off + nb
        assert self.cur <= self.nbytes, f"arena overflow at {name}: {self.cur}"
        base = self.t.ap()[:, off // 4:(off + nb + 3) // 4]
        ap = base.bitcast(dtype) if dtype == BF16 else base
        ap = ap[0:shape[0], 0:shape[1]]
        return Buf(ap, name)


def build_program(stage=STAGE, debug=DEBUG_DUMPS):
    nc = bass.Bass("TRN2", target_bir_lowering=False,
                   detect_race_conditions=False)

    dt = lambda n, s, d=F32, k="ExternalInput": nc.dram_tensor(n, s, d, kind=k).ap()
    x_full = dt("x_full", [T, D])
    x_own = dt("x_own", [NOWN * 128, D])
    cofs = dt("cofs", [128, 1])
    qn_in = dt("qn", [128, 1])
    kn_in = dt("kn", [128, 1])
    subln_in = dt("subln", [1, FF])
    w_in = {}
    WEIGHTS = [
        ("wqc", D, D), ("wqr", D, H * DR), ("wkvd", D, LAT),
        ("wku", LAT, D), ("wvu", LAT, D), ("wkr", LAT, H * DR),
        ("wout", D, D), ("wg", D, FF), ("wu", D, FF), ("wd", FF, D),
    ]
    for nm, K, N in WEIGHTS:
        w_in[nm] = dt(nm + "T", [K, N])
    tern = {nm: dt("t_" + nm, [K, N], BF16, "Internal")
            for nm, K, N in WEIGHTS}
    Kc_hbm = dt("kc_hbm", [T, D], BF16, "Internal")
    Kr_hbm = dt("kr_hbm", [H, DR, T], BF16, "Internal")
    V_hbm = dt("v_hbm", [T, D], BF16, "Internal")
    Qc_hbm = dt("qc_hbm", [D, NOWN * 128], BF16, "Internal")
    Qr_hbm = dt("qr_hbm", [H, DR, NOWN * 128], BF16, "Internal")
    y_out = dt("y", [NOWN * 128, D], F32, "ExternalOutput")
    dbg = {}
    if debug:
        dbg["tern"] = dt("dbg_tern", [D, LAT], F32, "ExternalOutput")
        dbg["hws"] = dt("dbg_hws", [128, 16], F32, "ExternalOutput")
        dbg["wacc"] = dt("dbg_wacc", [128, 16], F32, "ExternalOutput")
        dbg["trig"] = dt("dbg_trig", [128, 4 * 64], F32, "ExternalOutput")
        dbg["krf"] = dt("dbg_krf", [128, H * DR], F32, "ExternalOutput")
        dbg["krb"] = dt("dbg_krb", [128, H * DR], F32, "ExternalOutput")
        dbg["wpart"] = dt("dbg_wpart", [128, 16], F32, "ExternalOutput")
        if stage >= 1:
            dbg["kc"] = dt("dbg_kc", [T, D], F32, "ExternalOutput")
            dbg["kr"] = dt("dbg_kr", [H, DR, T], F32, "ExternalOutput")
            dbg["v"] = dt("dbg_v", [T, D], F32, "ExternalOutput")
            dbg["xq"] = dt("dbg_xq", [128, D], F32, "ExternalOutput")
            dbg["latq"] = dt("dbg_latq", [128, LAT], F32, "ExternalOutput")
            dbg["kcraw"] = dt("dbg_kcraw", [128, D], F32, "ExternalOutput")
            dbg["invs"] = dt("dbg_invs", [128, 4], F32, "ExternalOutput")
        if stage >= 2:
            dbg["qc"] = dt("dbg_qc", [D, NOWN * 128], F32, "ExternalOutput")
            dbg["qr"] = dt("dbg_qr", [H, DR, NOWN * 128], F32, "ExternalOutput")
        if stage >= 3:
            dbg["ao"] = dt("dbg_ao", [128, NOWN * D], F32, "ExternalOutput")
            dbg["mneg"] = dt("dbg_mneg", [128, NOWN], F32, "ExternalOutput")
            dbg["pt"] = dt("dbg_pt", [128, 256], F32, "ExternalOutput")
            dbg["sct"] = dt("dbg_sct", [128, 256], F32, "ExternalOutput")

    WI = {nm: i for i, (nm, _, _) in enumerate(WEIGHTS)}

    with (
        nc.psum_tensor("ps0", [128, 512], F32) as ps0_t,
        nc.psum_tensor("ps1", [128, 512], F32) as ps1_t,
        nc.psum_tensor("ps2", [128, 512], F32) as ps2_t,
        nc.psum_tensor("ps3", [128, 512], F32) as ps3_t,
        nc.psum_tensor("ps4", [128, 512], F32) as ps4_t,
        nc.psum_tensor("ps5", [128, 512], F32) as ps5_t,
        nc.psum_tensor("ps6", [128, 512], F32) as ps6_t,
        nc.psum_tensor("ps7", [128, 512], F32) as ps7_t,
        nc.Block() as block,
    ):
        PS = [Buf(t.ap(), f"ps{i}") for i, t in
              enumerate([ps0_t, ps1_t, ps2_t, ps3_t, ps4_t, ps5_t, ps6_t, ps7_t])]
        PSB = [Buf(t.ap().bitcast(BF16), f"psb{i}") for i, t in
               enumerate([ps0_t, ps1_t])]
        PSB[0].res = PS[0].res
        PSB[1].res = PS[1].res

        kb = KB(nc)

        persist = Arena(nc, 44 * 1024)
        C = {}
        const_vals = dict(
            magic=MAGIC, eps5=1e-5, eps6=1e-6, c127=127.0,
            i127=1.0 / 127.0, i128=1.0 / 128.0, i2048=1.0 / 2048.0,
            i8192=1.0 / 8192.0, half=0.5, neg1=-1.0, tiny=1e-12,
            zero=0.0, one=1.0, big=3.0e38, i2pi=float(1.0 / (2 * np.pi)),
            n2pi=float(-2 * np.pi), hpi=float(np.pi / 2),
            lnb=float(-LN10K / 32.0), scl=SCALE,
            pip=3.141592502593994, pin=-3.141592502593994,
        )
        for k in const_vals:
            C[k] = persist.sub("c_" + k, [128, 1], F32)
        ident_b = persist.sub("ident_b", [128, 128], BF16)
        ident_f = persist.sub("ident_f", [128, 128], F32)
        ones_col_b = persist.sub("ones_col_b", [128, 1], BF16)
        ones_col_f = persist.sub("ones_col_f", [128, 1], F32)
        ones_row_f = persist.sub("ones_row_f", [128, 128], F32)
        Mmask = persist.sub("Mmask", [128, 32 * 128], BF16)
        g_col = persist.sub("g_col", [128, 1], F32)
        cofs_sb = persist.sub("cofs_sb", [128, 1], F32)
        hws = persist.sub("hws", [128, 16], F32)
        hneg = persist.sub("hneg", [128, 16], F32)
        alt = persist.sub("alt", [128, 64], F32)
        IF32 = persist.sub("IF32", [128, 32], F32)
        attn_out = persist.sub("attn_out", [128, NOWN * D], BF16)
        tv = persist.sub("tv", [128, 1], F32)
        th = persist.sub("th", [128, 32], F32)
        th2 = persist.sub("th2", [128, 32], F32)
        trig32 = persist.sub("trig32", [128, 32], F32)
        arena_cb = [persist.sub(f"arena_cb{i}", [128, 1], F32) for i in range(2)]

        arena = Arena(nc, 162 * 1024)

        def memset_step(buf, val, eng="vector"):
            kb.do(eng, (lambda e, a=buf.ap, v=val: e.memset(a, v)),
                  writes=[buf])

        def quant(src_buf, src_ap, xq_buf, invs_buf, invs_col, n,
                  mx, s, u_scr, extra_reads=()):
            v = kb.do
            mxa = mx.ap
            v("vector", (lambda e, sa=src_ap: e.tensor_reduce(
                mxa, sa, AX.X, ALU.max, apply_absolute_value=True)),
              reads=[src_buf, *extra_reads], writes=[mx])
            v("vector", lambda e: e.tensor_scalar(
                mxa, mxa, C["eps5"].ap, None, ALU.max),
              reads=[mx, C["eps5"]], writes=[mx])
            v("vector", (lambda e, ia=invs_buf.ap[:, invs_col:invs_col + 1]:
                         e.tensor_scalar(ia, mxa, C["i127"].ap, None, ALU.mult)),
              reads=[mx, C["i127"]], writes=[invs_buf])
            v("vector", lambda e: e.reciprocal(s.ap, mxa),
              reads=[mx], writes=[s])
            v("vector", lambda e: e.tensor_scalar(
                s.ap, s.ap, C["c127"].ap, None, ALU.mult),
              reads=[s, C["c127"]], writes=[s])
            ua = u_scr.ap[:, 0:n]
            v("scalar", (lambda e, sa=src_ap: e.activation(
                ua, sa, AF.Identity, bias=C["magic"].ap, scale=s.ap)),
              reads=[src_buf, s, C["magic"], *extra_reads], writes=[u_scr])
            v("vector", (lambda e, xa=xq_buf.ap[:, 0:n]: e.tensor_scalar(
                xa, ua, C["magic"].ap, None, ALU.subtract)),
              reads=[u_scr, C["magic"]], writes=[xq_buf])

        def transposes(src_buf, src_ap_fn, dst_buf, dst_col0, nchunks,
                       scale_col=None, scale_buf=None):
            for i in range(nchunks):
                bank = PSB[(i // 8) % 2]
                slot = i % 8
                kb.do("tensor", (lambda e, sa=src_ap_fn(i), pa=bank.ap[:, slot * 128:(slot + 1) * 128]:
                                 e.transpose(pa, sa, ident_b.ap)),
                      reads=[src_buf, ident_b], writes=[bank], tag="tr")
                if scale_col is not None:
                    sc = scale_buf.ap[:, scale_col:scale_col + 1]
                    kb.do("vector", (lambda e, pa=bank.ap[:, slot * 128:(slot + 1) * 128],
                                     da=dst_buf.ap[:, dst_col0 + i * 128: dst_col0 + (i + 1) * 128],
                                     s=sc:
                                     e.tensor_scalar(da, pa, s, None, ALU.mult)),
                          reads=[bank, scale_buf], writes=[dst_buf], tag="trev")
                else:
                    kb.do("vector", (lambda e, pa=bank.ap[:, slot * 128:(slot + 1) * 128],
                                     da=dst_buf.ap[:, dst_col0 + i * 128: dst_col0 + (i + 1) * 128]:
                                     e.tensor_copy(da, pa)),
                          reads=[bank], writes=[dst_buf], tag="trev")

        arena.reset()
        mm_f32 = arena.sub("mm_f32", [128, 32 * 128], F32)

        v = kb.do
        for k, val in const_vals.items():
            v("vector", (lambda e, a=C[k].ap, x=val: e.memset(a, x)),
              writes=[C[k]], tag="const")
        v("gpsimd", lambda e: e.memset(ident_b.ap, 1.0), writes=[ident_b])
        v("gpsimd", lambda e: e.affine_select(
            ident_b.ap, ident_b.ap, pattern=[[-1, 128]],
            compare_op=ALU.is_ge, fill=0.0, base=0, channel_multiplier=1),
          reads=[ident_b], writes=[ident_b])
        v("gpsimd", lambda e: e.affine_select(
            ident_b.ap, ident_b.ap, pattern=[[1, 128]],
            compare_op=ALU.is_ge, fill=0.0, base=0, channel_multiplier=-1),
          reads=[ident_b], writes=[ident_b])
        v("gpsimd", lambda e: e.memset(ident_f.ap, 1.0), writes=[ident_f])
        v("gpsimd", lambda e: e.affine_select(
            ident_f.ap, ident_f.ap, pattern=[[-1, 128]],
            compare_op=ALU.is_ge, fill=0.0, base=0, channel_multiplier=1),
          reads=[ident_f], writes=[ident_f])
        v("gpsimd", lambda e: e.affine_select(
            ident_f.ap, ident_f.ap, pattern=[[1, 128]],
            compare_op=ALU.is_ge, fill=0.0, base=0, channel_multiplier=-1),
          reads=[ident_f], writes=[ident_f])
        v("vector", lambda e: e.memset(ones_col_b.ap, 1.0), writes=[ones_col_b])
        v("vector", lambda e: e.memset(ones_col_f.ap, 1.0), writes=[ones_col_f])
        v("vector", lambda e: e.memset(ones_row_f.ap, 1.0), writes=[ones_row_f])
        v("sync", lambda e: e.dma_start(out=cofs_sb.ap, in_=cofs),
          writes=[cofs_sb], dma=True)
        qn_sb = persist.sub("qn_sb", [128, 1], F32)
        kn_sb = persist.sub("kn_sb", [128, 1], F32)
        v("sync", lambda e: e.dma_start(out=qn_sb.ap, in_=qn_in),
          writes=[qn_sb], dma=True)
        v("sync", lambda e: e.dma_start(out=kn_sb.ap, in_=kn_in),
          writes=[kn_sb], dma=True)
        v("vector", lambda e: e.tensor_tensor(g_col.ap, qn_sb.ap, kn_sb.ap,
                                              ALU.mult),
          reads=[qn_sb, kn_sb], writes=[g_col])
        v("vector", lambda e: e.memset(hws.ap, 0.0), writes=[hws])
        v("vector", lambda e: e.memset(hneg.ap, 0.0), writes=[hneg])
        v("gpsimd", lambda e: e.iota(
            mm_f32.ap, pattern=[[128, 32], [-1, 128]], base=0,
            channel_multiplier=1, allow_small_or_imprecise_dtypes=True),
          writes=[mm_f32])
        v("vector", lambda e: e.tensor_scalar(
            mm_f32.ap, mm_f32.ap, cofs_sb.ap, None, ALU.subtract),
          reads=[mm_f32, cofs_sb], writes=[mm_f32])
        v("vector", lambda e: e.tensor_scalar(
            Mmask.ap, mm_f32.ap, C["zero"].ap, None, ALU.is_le),
          reads=[mm_f32, C["zero"]], writes=[Mmask])
        v("gpsimd", lambda e: e.iota(
            alt.ap, pattern=[[32, 2], [0, 32]], base=0, channel_multiplier=0,
            allow_small_or_imprecise_dtypes=True),
          writes=[alt])
        v("gpsimd", lambda e: e.iota(
            alt.ap, pattern=[[1, 64]], base=0, channel_multiplier=0,
            allow_small_or_imprecise_dtypes=True),
          writes=[alt])
        alt_h = persist.sub("alt_h", [128, 64], F32)
        v("vector", lambda e: e.tensor_scalar(
            alt_h.ap, alt.ap, C["half"].ap, C["magic"].ap, ALU.mult, ALU.add),
          reads=[alt, C["half"], C["magic"]], writes=[alt_h])
        v("vector", lambda e: e.tensor_scalar(
            alt_h.ap, alt_h.ap, C["magic"].ap, None, ALU.subtract),
          reads=[alt_h, C["magic"]], writes=[alt_h])
        v("vector", lambda e: e.tensor_scalar(
            alt_h.ap, alt.ap, C["half"].ap, None, ALU.mult),
          reads=[alt, C["half"]], writes=[alt_h])
        qtr = persist.sub("qtr", [128, 1], F32)
        v("vector", lambda e: e.memset(qtr.ap, 0.25), writes=[qtr])
        v("vector", lambda e: e.tensor_scalar(
            alt_h.ap, alt_h.ap, qtr.ap, C["magic"].ap, ALU.subtract, ALU.add),
          reads=[alt_h, qtr, C["magic"]], writes=[alt_h])
        v("vector", lambda e: e.tensor_scalar(
            alt_h.ap, alt_h.ap, C["magic"].ap, None, ALU.subtract),
          reads=[alt_h, C["magic"]], writes=[alt_h])
        two = persist.sub("two", [128, 1], F32)
        v("vector", lambda e: e.memset(two.ap, 2.0), writes=[two])
        v("vector", lambda e: e.tensor_scalar(
            alt_h.ap, alt_h.ap, two.ap, None, ALU.mult),
          reads=[alt_h, two], writes=[alt_h])
        v("vector", lambda e: e.tensor_tensor(
            alt.ap, alt.ap, alt_h.ap, ALU.subtract),
          reads=[alt, alt_h], writes=[alt])
        v("vector", lambda e: e.tensor_scalar(
            alt.ap, alt.ap, two.ap, C["one"].ap, ALU.mult, ALU.subtract),
          reads=[alt, two, C["one"]], writes=[alt])
        v("gpsimd", lambda e: e.iota(
            IF32.ap, pattern=[[1, 32]], base=0, channel_multiplier=0,
            allow_small_or_imprecise_dtypes=True),
          writes=[IF32])
        v("scalar", lambda e: e.activation(
            IF32.ap, IF32.ap, AF.Exp, scale=C["lnb"].ap),
          reads=[IF32, C["lnb"]], writes=[IF32])

        wtile = [arena.sub(f"wtile{i}", [128, 2048], F32) for i in range(2)]
        wout_b = [arena.sub(f"wtern{i}", [128, 2048], BF16) for i in range(2)]
        sg1 = arena.sub("sg1", [128, 2048], BF16)
        wacc = arena.sub("wacc", [128, 1], F32)
        wpart = arena.sub("wpart", [128, 1], F32)
        wsc = arena.sub("wsc", [1, 1], F32)

        def w_tiles(nm):
            K, N = [(k, n) for w, k, n in WEIGHTS if w == nm][0]
            ncol = min(N, 2048)
            out = []
            for kc in range(K // 128):
                for ncc in range(N // ncol):
                    out.append((kc, ncc, ncol))
            return out, K, N

        def w_pipeline(weights, wtile, wout_b, sg1, wacc, wpart, wsc, ar, wbank):
            for nm, K, N in weights:
                tiles, _, _ = w_tiles(nm)
                memset_step(wacc, 0.0)
                wi = WI[nm]
                for i, (kc, ncc, ncol) in enumerate(tiles):
                    b = wtile[i % 2]
                    srcw = w_in[nm][kc * 128:(kc + 1) * 128,
                                    ncc * ncol:(ncc + 1) * ncol]
                    kb.do("sync", (lambda e, d=b.ap[:, 0:ncol], s=srcw:
                                   e.dma_start(out=d, in_=s)),
                          writes=[b], dma=True, tag=f"w1ld_{nm}")
                    kb.do("vector", (lambda e, s=b.ap[:, 0:ncol]:
                                     e.tensor_reduce(wpart.ap, s, AX.X, ALU.add,
                                                     apply_absolute_value=True)),
                          reads=[b], writes=[wpart])
                    kb.do("vector", lambda e: e.tensor_tensor(
                        wacc.ap, wacc.ap, wpart.ap, ALU.add),
                        reads=[wacc, wpart], writes=[wacc])
                    yield
                kb.do("tensor", (lambda e, pb=wbank.ap: e.matmul(
                    pb[0:1, 0:1], ones_col_f.ap, wacc.ap,
                    start=True, stop=True)),
                    reads=[ones_col_f, wacc], writes=[wbank])
                inv_numel = float(np.float32(1.0 / (K * N)))
                cnum = ar.sub(f"cnum_{nm}", [1, 1], F32)
                kb.do("vector", (lambda e, a2=cnum.ap, x=inv_numel: e.memset(a2, x)),
                      writes=[cnum])
                kb.do("vector", (lambda e, cn=cnum.ap[0:1, :], pb=wbank.ap:
                                 e.tensor_scalar(
                    wsc.ap, pb[0:1, 0:1], cn, None, ALU.mult)),
                    reads=[wbank, cnum], writes=[wsc])
                kb.do("vector", lambda e: e.tensor_scalar(
                    wsc.ap, wsc.ap, C["eps5"].ap[0:1, :], None, ALU.max),
                    reads=[wsc, C["eps5"]], writes=[wsc])
                kb.do("vector", lambda e: e.tensor_scalar(
                    wsc.ap, wsc.ap, C["half"].ap[0:1, :], None, ALU.mult),
                    reads=[wsc, C["half"]], writes=[wsc])
                kb.do("tensor", (lambda e, pb=wbank.ap: e.matmul(
                    pb[:, 0:1], ones_row_f.ap[0:1, :], wsc.ap,
                    start=True, stop=True)),
                    reads=[ones_row_f, wsc], writes=[wbank])
                kb.do("vector", (lambda e, d=hws.ap[:, wi:wi + 1],
                                 pb=wbank.ap:
                                 e.tensor_copy(d, pb[:, 0:1])),
                      reads=[wbank], writes=[hws])
                kb.do("vector", (lambda e, d=hneg.ap[:, wi:wi + 1],
                                 s=hws.ap[:, wi:wi + 1]:
                                 e.tensor_scalar(d, s, C["neg1"].ap, None, ALU.mult)),
                      reads=[hws, C["neg1"]], writes=[hneg])
                yield
                for i, (kc, ncc, ncol) in enumerate(tiles):
                    b = wtile[i % 2]
                    ob = wout_b[i % 2]
                    srcw = w_in[nm][kc * 128:(kc + 1) * 128,
                                    ncc * ncol:(ncc + 1) * ncol]
                    dst = tern[nm][kc * 128:(kc + 1) * 128,
                                   ncc * ncol:(ncc + 1) * ncol]
                    kb.do("sync", (lambda e, d=b.ap[:, 0:ncol], s=srcw:
                                   e.dma_start(out=d, in_=s)),
                          writes=[b], dma=True, tag=f"w2ld_{nm}")
                    kb.do("scalar", (lambda e, s=b.ap[:, 0:ncol],
                                     d=sg1.ap[:, 0:ncol],
                                     hn=hneg.ap[:, wi:wi + 1]:
                                     e.activation(d, s, AF.Sign, bias=hn)),
                          reads=[b, hneg], writes=[sg1])
                    kb.do("scalar", (lambda e, s=b.ap[:, 0:ncol],
                                     d=ob.ap[:, 0:ncol],
                                     hp=hws.ap[:, wi:wi + 1]:
                                     e.activation(d, s, AF.Sign, bias=hp)),
                          reads=[b, hws], writes=[ob])
                    kb.do("vector", (lambda e, d=ob.ap[:, 0:ncol],
                                     s1=sg1.ap[:, 0:ncol]:
                                     e.tensor_tensor(d, d, s1, ALU.add)),
                          reads=[ob, sg1], writes=[ob])
                    kb.do("gpsimd", (lambda e, s=ob.ap[:, 0:ncol], d=dst:
                                     e.dma_start(out=d, in_=s)),
                          reads=[ob], dma=True, tag=f"w2st_{nm}")
                    yield

        W_EARLY = [w for w in WEIGHTS if w[0] in
                   ("wqc", "wqr", "wkvd", "wku", "wvu", "wkr")]
        W_LATE = [w for w in WEIGHTS if w[0] in ("wout", "wg", "wu", "wd")]
        for _ in w_pipeline(W_EARLY, wtile, wout_b, sg1, wacc, wpart, wsc,
                            arena, PS[6]):
            pass

        if debug:
            kb.do("gpsimd", (lambda e: e.dma_start(out=dbg["hws"], in_=hws.ap)),
                  reads=[hws], dma=True)
            for ch in range(16):
                bb = wout_b[0]
                kb.do("sync", (lambda e, d=bb.ap[:, 0:512],
                               s=tern["wkvd"][ch * 128:(ch + 1) * 128, :]:
                               e.dma_start(out=d, in_=s)),
                      writes=[bb], dma=True)
                kb.do("gpsimd", (lambda e, s=bb.ap[:, 0:512],
                                 d=dbg["tern"][ch * 128:(ch + 1) * 128, :]:
                                 e.dma_start(out=d, in_=s)),
                      reads=[bb], dma=True)

        kb.barrier()

        arena.reset()
        wkvd_sb = arena.sub("wkvd_sb", [128, 16 * LAT], BF16)
        wku_sb = arena.sub("wku_sb", [128, 4 * D], BF16)
        wvu_sb = arena.sub("wvu_sb", [128, 4 * D], BF16)
        wkr_sb = arena.sub("wkr_sb", [128, 4 * H * DR], BF16)
        cosT = arena.sub("cosT", [128, NT * 32], BF16)
        sinT = arena.sub("sinT", [128, NT * 32], BF16)
        xtile = [arena.sub(f"xtile{i}", [128, D], F32) for i in range(2)]
        u_scr2 = [arena.sub(f"u_scr{i}", [128, D], F32) for i in range(2)]
        xq = [arena.sub(f"xq{i}", [128, D], BF16) for i in range(2)]
        xqT = [arena.sub(f"xqT{i}", [128, D], BF16) for i in range(2)]
        latq = [arena.sub(f"latq{i}", [128, LAT], BF16) for i in range(2)]
        latqT = [arena.sub(f"latqT{i}", [128, LAT], BF16) for i in range(2)]
        kcraw = [arena.sub(f"kcraw{i}", [128, D], BF16) for i in range(2)]
        krf2 = [arena.sub(f"krf{i}", [128, H * DR], F32) for i in range(2)]
        krb = [arena.sub(f"krb{i}", [128, H * DR], BF16) for i in range(2)]
        krTs = [arena.sub("krTs0", [128, H * DR], BF16)] * 2
        kcs = [arena.sub(f"kcs{i}", [128, D], BF16) for i in range(2)]
        vst = [arena.sub(f"vst{i}", [128, D], BF16) for i in range(2)]
        sc_mx = [arena.sub(f"sc_mx{i}", [128, 1], F32) for i in range(2)]
        sc_s = [arena.sub(f"sc_s{i}", [128, 1], F32) for i in range(2)]
        invsx = [arena.sub(f"invsx{i}", [128, 1], F32) for i in range(2)]
        invsl = [arena.sub(f"invsl{i}", [128, 1], F32) for i in range(2)]
        dq_sc = [arena.sub(f"dq_sc{i}", [128, 4], F32) for i in range(2)]
        ss16_2 = [arena.sub(f"ss16_{i}", [128, 16], F32) for i in range(2)]
        cs16_2 = [arena.sub(f"cs16_{i}", [128, 16], F32) for i in range(2)]

        def emit_trig(kt_or_none, tile_j, cos_dst, sin_dst, pos_reads=()):
            v = kb.do
            if kt_or_none is not None:
                base = float(128 * kt_or_none)
                v("gpsimd", (lambda e, b=base: e.iota(
                    tv.ap, pattern=[[0, 1]], base=0, channel_multiplier=1,
                    allow_small_or_imprecise_dtypes=True)),
                  writes=[tv])
                cb = arena_cb[kt_or_none % 2]
                v("vector", (lambda e, b=base: e.memset(cb.ap, b)), writes=[cb])
                v("vector", (lambda e, cba=cb.ap: e.tensor_scalar(
                    tv.ap, tv.ap, cba, None, ALU.add)),
                  reads=[tv, cb], writes=[tv])
            else:
                base = float(1024 * tile_j)
                v("gpsimd", (lambda e: e.iota(
                    tv.ap, pattern=[[0, 1]], base=0, channel_multiplier=1,
                    allow_small_or_imprecise_dtypes=True)),
                  writes=[tv])
                cb = arena_cb[0]
                v("vector", (lambda e, b=base: e.memset(cb.ap, b)), writes=[cb])
                v("vector", (lambda e, cba=cb.ap: e.tensor_scalar(
                    tv.ap, tv.ap, cba, None, ALU.add)),
                  reads=[tv, cb], writes=[tv])
                v("vector", lambda e: e.tensor_scalar(
                    tv.ap, tv.ap, cofs_sb.ap, None, ALU.add),
                  reads=[tv, cofs_sb], writes=[tv])
            v("vector", lambda e: e.tensor_scalar(
                th.ap, IF32.ap, tv.ap, None, ALU.mult),
              reads=[IF32, tv], writes=[th])
            v("scalar", lambda e: e.activation(
                th2.ap, th.ap, AF.Identity, bias=C["magic"].ap,
                scale=C["i2pi"].ap),
              reads=[th, C["magic"], C["i2pi"]], writes=[th2])
            v("vector", lambda e: e.tensor_scalar(
                th2.ap, th2.ap, C["magic"].ap, None, ALU.subtract),
              reads=[th2, C["magic"]], writes=[th2])
            v("vector", lambda e: e.scalar_tensor_tensor(
                th2.ap, th2.ap, C["n2pi"].ap, th.ap, ALU.mult, ALU.add),
              reads=[th2, C["n2pi"], th], writes=[th2])
            v("vector", lambda e: e.tensor_scalar(
                th2.ap, th2.ap, C["pip"].ap, C["pin"].ap, ALU.min, ALU.max),
              reads=[th2, C["pip"], C["pin"]], writes=[th2])
            v("scalar", lambda e: e.activation(trig32.ap, th2.ap, AF.Sin),
              reads=[th2], writes=[trig32])
            v("vector", (lambda e, d=sin_dst.ap[:, 0:32]:
                         e.tensor_tensor(d, trig32.ap, alt.ap[:, 0:32],
                                         ALU.mult)),
              reads=[trig32, alt], writes=[sin_dst])
            v("scalar", lambda e: e.activation(th.ap, th2.ap, AF.Abs),
              reads=[th2], writes=[th])
            v("scalar", lambda e: e.activation(
                trig32.ap, th.ap, AF.Sin, bias=C["hpi"].ap, scale=C["neg1"].ap),
              reads=[th, C["hpi"], C["neg1"]], writes=[trig32])
            v("vector", (lambda e, d=cos_dst.ap[:, 0:32]:
                         e.tensor_copy(d, trig32.ap)),
              reads=[trig32], writes=[cos_dst])


        class TileView(Buf):
            pass

        cosT_t = [TileView(cosT.ap[:, kt * 32:(kt + 1) * 32]) for kt in range(NT)]
        sinT_t = [TileView(sinT.ap[:, kt * 32:(kt + 1) * 32]) for kt in range(NT)]
        for b in cosT_t:
            b.res = cosT.res
        for b in sinT_t:
            b.res = sinT.res

        if stage >= 1:
            kb.do("sync", lambda e: e.dma_start(
                out=wkvd_sb.ap.rearrange("p (a n) -> p a n", a=16),
                in_=tern["wkvd"].rearrange("(a p) n -> p a n", p=128)),
                writes=[wkvd_sb], dma=True)
            kb.do("sync", lambda e: e.dma_start(
                out=wku_sb.ap.rearrange("p (a n) -> p a n", a=4),
                in_=tern["wku"].rearrange("(a p) n -> p a n", p=128)),
                writes=[wku_sb], dma=True)
            kb.do("sync", lambda e: e.dma_start(
                out=wvu_sb.ap.rearrange("p (a n) -> p a n", a=4),
                in_=tern["wvu"].rearrange("(a p) n -> p a n", p=128)),
                writes=[wvu_sb], dma=True)
            kb.do("sync", lambda e: e.dma_start(
                out=wkr_sb.ap.rearrange("p (a n) -> p a n", a=4),
                in_=tern["wkr"].rearrange("(a p) n -> p a n", p=128)),
                writes=[wkr_sb], dma=True)
            for kt in range(NT):
                emit_trig(kt, None, cosT_t[kt], sinT_t[kt])

            def emit_A_h1(kt):
                par = kt % 2
                u_scr = u_scr2[par]
                krf = krf2[par]
                ss16 = ss16_2[par]
                cs16 = cs16_2[par]
                ropa = u_scr
                t1 = ropa.ap[:, 0:H * DR]
                t2 = ropa.ap[:, H * DR:2 * H * DR]
                xb, xqb, xqTb = xtile[par], xq[par], xqT[par]
                kb.do("sync", (lambda e, d=xb.ap,
                               s=x_full[kt * 128:(kt + 1) * 128, :]:
                               e.dma_start(out=d, in_=s)),
                      writes=[xb], dma=True, tag=f"A_x{kt}")
                quant(xb, xb.ap, xqb, invsx[par], 0, D,
                      sc_mx[par], sc_s[par], u_scr)
                if debug and kt == 0:
                    kb.do("gpsimd", (lambda e, s=xqb.ap, d=dbg["xq"]:
                                     e.dma_start(out=d, in_=s)),
                          reads=[xqb], dma=True)
                transposes(xqb, (lambda i, b=xqb: b.ap[:, i * 128:(i + 1) * 128]),
                           xqTb, 0, 16)
                lb = PS[2 + par]
                for ic in range(16):
                    kb.do("tensor", (lambda e, l=xqTb.ap[:, ic * 128:(ic + 1) * 128],
                                     r=wkvd_sb.ap[:, ic * 512:(ic + 1) * 512],
                                     st=(ic == 0), sp=(ic == 15), pb=lb.ap:
                                     e.matmul(pb, l, r, start=st, stop=sp)),
                          reads=[xqTb, wkvd_sb], writes=[lb], tag="A_lat")
                dqs = dq_sc[par]
                kb.do("vector", (lambda e, d=dqs.ap[:, 0:1],
                                 h=hws.ap[:, WI["wkvd"]:WI["wkvd"] + 1],
                                 ix=invsx[par].ap:
                                 e.tensor_scalar(d, h, ix, None, ALU.mult)),
                      reads=[hws, invsx[par]], writes=[dqs])
                mxb, sb = sc_mx[par], sc_s[par]
                kb.do("vector", (lambda e, pb=lb.ap: e.tensor_reduce(
                    mxb.ap, pb, AX.X, ALU.max, apply_absolute_value=True)),
                      reads=[lb], writes=[mxb])
                kb.do("vector", (lambda e, d=mxb.ap, q=dqs.ap[:, 0:1]:
                                 e.tensor_scalar(d, d, q, None, ALU.mult)),
                      reads=[mxb, dqs], writes=[mxb])
                kb.do("vector", lambda e, d=mxb.ap: e.tensor_scalar(
                    d, d, C["eps5"].ap, None, ALU.max),
                    reads=[mxb, C["eps5"]], writes=[mxb])
                kb.do("vector", (lambda e, d=invsl[par].ap, m=mxb.ap:
                                 e.tensor_scalar(d, m, C["i127"].ap, None, ALU.mult)),
                      reads=[mxb, C["i127"]], writes=[invsl[par]])
                kb.do("vector", (lambda e, d=sb.ap, m=mxb.ap:
                                 e.reciprocal(d, m)),
                      reads=[mxb], writes=[sb])
                kb.do("vector", (lambda e, d=sb.ap:
                                 e.tensor_scalar(d, d, C["c127"].ap, None, ALU.mult)),
                      reads=[sb, C["c127"]], writes=[sb])
                kb.do("vector", (lambda e, d=sb.ap, q=dqs.ap[:, 0:1]:
                                 e.tensor_scalar(d, d, q, None, ALU.mult)),
                      reads=[sb, dqs], writes=[sb])
                kb.do("scalar", (lambda e, pb=lb.ap, d=u_scr.ap[:, 0:512],
                                 s=sb.ap:
                                 e.activation(d, pb, AF.Identity,
                                              bias=C["magic"].ap, scale=s)),
                      reads=[lb, sb, C["magic"]], writes=[u_scr])
                kb.do("vector", (lambda e, d=latq[par].ap, s=u_scr.ap[:, 0:512]:
                                 e.tensor_scalar(d, s, C["magic"].ap, None,
                                                 ALU.subtract)),
                      reads=[u_scr, C["magic"]], writes=[latq[par]])
                if debug and kt == 0:
                    kb.do("gpsimd", (lambda e, s=latq[par].ap, d=dbg["latq"]:
                                     e.dma_start(out=d, in_=s)),
                          reads=[latq[par]], dma=True)
                    kb.do("gpsimd", (lambda e, s=invsx[par].ap,
                                     d=dbg["invs"][:, 0:1]:
                                     e.dma_start(out=d, in_=s,
                                                 allow_slow_non_contiguous=True)),
                          reads=[invsx[par]], dma=True)
                    kb.do("gpsimd", (lambda e, s=invsl[par].ap,
                                     d=dbg["invs"][:, 1:2]:
                                     e.dma_start(out=d, in_=s,
                                                 allow_slow_non_contiguous=True)),
                          reads=[invsl[par]], dma=True)
                transposes(latq[par],
                           (lambda i, b=latq[par]: b.ap[:, i * 128:(i + 1) * 128]),
                           latqT[par], 0, 4)

            def emit_A_h2(kt):
                par = kt % 2
                u_scr = u_scr2[par]
                krf = krf2[par]
                ss16 = ss16_2[par]
                cs16 = cs16_2[par]
                ropa = u_scr
                t1 = ropa.ap[:, 0:H * DR]
                t2 = ropa.ap[:, H * DR:2 * H * DR]
                lb = PS[2 + par]
                dqs = dq_sc[par]
                for wi_name, col in (("wku", 1), ("wvu", 2), ("wkr", 3)):
                    kb.do("vector", (lambda e, d=dqs.ap[:, col:col + 1],
                                     h=hws.ap[:, WI[wi_name]:WI[wi_name] + 1],
                                     il=invsl[par].ap:
                                     e.tensor_scalar(d, h, il, None, ALU.mult)),
                          reads=[hws, invsl[par]], writes=[dqs])
                for oc in range(4):
                    bank = PS[4 + (oc % 2)]
                    for lc in range(4):
                        kb.do("tensor", (lambda e,
                                         l=latqT[par].ap[:, lc * 128:(lc + 1) * 128],
                                         r=wku_sb.ap[:, lc * D + oc * 512: lc * D + (oc + 1) * 512],
                                         st=(lc == 0), sp=(lc == 3), pb=bank.ap:
                                         e.matmul(pb, l, r, start=st, stop=sp)),
                              reads=[latqT[par], wku_sb], writes=[bank], tag="A_kc")
                    kb.do("vector", (lambda e, pb=bank.ap,
                                     d=kcraw[par].ap[:, oc * 512:(oc + 1) * 512]:
                                     e.tensor_copy(d, pb)),
                          reads=[bank], writes=[kcraw[par]])
                if debug and kt == 0:
                    kb.do("gpsimd", (lambda e, s=kcraw[par].ap, d=dbg["kcraw"]:
                                     e.dma_start(out=d, in_=s)),
                          reads=[kcraw[par]], dma=True)
                kb.do("scalar", (lambda e, s=kcraw[par].ap, d=u_scr.ap:
                                 e.activation(d, s, AF.Square)),
                      reads=[kcraw[par]], writes=[u_scr])
                kb.do("vector", (lambda e, d=ss16.ap,
                                 s=u_scr.ap.rearrange("p (h d) -> p h d", h=16):
                                 e.tensor_reduce(d, s, AX.X, ALU.add)),
                      reads=[u_scr], writes=[ss16])
                kb.do("vector", (lambda e, q=dqs.ap[:, 1:2], d=dqs.ap[:, 0:1]:
                                 e.tensor_scalar(d, q, q, None, ALU.mult)),
                      reads=[dqs], writes=[dqs])
                kb.do("vector", (lambda e, q=dqs.ap[:, 0:1], d=ss16.ap:
                                 e.tensor_scalar(d, d, q, None, ALU.mult)),
                      reads=[ss16, dqs], writes=[ss16])
                kb.do("scalar", (lambda e, d=ss16.ap: e.activation(
                    d, d, AF.Sqrt, bias=C["eps6"].ap, scale=C["i128"].ap)),
                    reads=[ss16, C["eps6"], C["i128"]], writes=[ss16])
                kb.do("vector", (lambda e, d=cs16.ap, s=ss16.ap:
                                 e.reciprocal(d, s)),
                      reads=[ss16], writes=[cs16])
                kb.do("vector", (lambda e, q=dqs.ap[:, 1:2], d=cs16.ap:
                                 e.tensor_scalar(d, d, q, None, ALU.mult)),
                      reads=[cs16, dqs], writes=[cs16])
                for h in range(16):
                    kb.do("vector", (lambda e,
                                     s=kcraw[par].ap[:, h * 128:(h + 1) * 128],
                                     d=kcs[par].ap[:, h * 128:(h + 1) * 128],
                                     c=cs16.ap[:, h:h + 1]:
                                     e.tensor_scalar(d, s, c, None, ALU.mult)),
                          reads=[kcraw[par], cs16], writes=[kcs[par]])
                kb.do("gpsimd", (lambda e, s=kcs[par].ap,
                                 d=Kc_hbm[kt * 128:(kt + 1) * 128, :]:
                                 e.dma_start(out=d, in_=s)),
                      reads=[kcs[par]], dma=True, tag=f"A_kcst{kt}")
                if debug:
                    kb.do("gpsimd", (lambda e, s=kcs[par].ap,
                                     d=dbg["kc"][kt * 128:(kt + 1) * 128, :]:
                                     e.dma_start(out=d, in_=s)),
                          reads=[kcs[par]], dma=True)
                for oc in range(4):
                    bank = PS[4 + (oc % 2)]
                    for lc in range(4):
                        kb.do("tensor", (lambda e,
                                         l=latqT[par].ap[:, lc * 128:(lc + 1) * 128],
                                         r=wvu_sb.ap[:, lc * D + oc * 512: lc * D + (oc + 1) * 512],
                                         st=(lc == 0), sp=(lc == 3), pb=bank.ap:
                                         e.matmul(pb, l, r, start=st, stop=sp)),
                              reads=[latqT[par], wvu_sb], writes=[bank], tag="A_v")
                    kb.do("vector", (lambda e, pb=bank.ap,
                                     d=vst[par].ap[:, oc * 512:(oc + 1) * 512],
                                     q=dq_sc[par].ap[:, 2:3]:
                                     e.tensor_scalar(d, pb, q, None, ALU.mult)),
                          reads=[bank, dq_sc[par]], writes=[vst[par]])
                kb.do("gpsimd", (lambda e, s=vst[par].ap,
                                 d=V_hbm[kt * 128:(kt + 1) * 128, :]:
                                 e.dma_start(out=d, in_=s)),
                      reads=[vst[par]], dma=True, tag=f"A_vst{kt}")
                if debug:
                    kb.do("gpsimd", (lambda e, s=vst[par].ap,
                                     d=dbg["v"][kt * 128:(kt + 1) * 128, :]:
                                     e.dma_start(out=d, in_=s)),
                          reads=[vst[par]], dma=True)
                for oc in range(2):
                    bank = PS[4 + (oc % 2)]
                    for lc in range(4):
                        kb.do("tensor", (lambda e,
                                         l=latqT[par].ap[:, lc * 128:(lc + 1) * 128],
                                         r=wkr_sb.ap[:, lc * (H * DR) + oc * 512: lc * (H * DR) + (oc + 1) * 512],
                                         st=(lc == 0), sp=(lc == 3), pb=bank.ap:
                                         e.matmul(pb, l, r, start=st, stop=sp)),
                              reads=[latqT[par], wkr_sb], writes=[bank], tag="A_kr")
                    kb.do("vector", (lambda e, pb=bank.ap,
                                     d=krf.ap[:, oc * 512:(oc + 1) * 512],
                                     q=dq_sc[par].ap[:, 3:4]:
                                     e.tensor_scalar(d, pb, q, None, ALU.mult)),
                          reads=[bank, dq_sc[par]], writes=[krf])
                c3 = cosT_t[kt].ap.unsqueeze(1).unsqueeze(1).broadcast_to(
                    [128, 16, 2, 32])
                s3 = sinT_t[kt].ap.unsqueeze(1).unsqueeze(1).broadcast_to(
                    [128, 16, 2, 32])
                kr3 = krf.ap.rearrange("p (h r d) -> p h r d", h=16, r=2)
                krsw = krf.ap.rearrange("p (a b) -> p a b", b=2)[:, :, ::-1]
                kb.do("vector", (lambda e, tt=t1, aa=kr3, cc=c3: e.tensor_tensor(
                    tt.rearrange("p (h r d) -> p h r d", h=16, r=2), aa, cc,
                    ALU.mult)),
                      reads=[krf, cosT], writes=[ropa])
                kb.do("vector", (lambda e, tt=t2, aa=krsw, ss=s3: e.tensor_tensor(
                    tt.rearrange("p (h r d) -> p h r d", h=16, r=2), aa, ss,
                    ALU.mult)),
                      reads=[krf, sinT], writes=[ropa])
                kb.do("vector", (lambda e, d=krb[par].ap, a=t1, b=t2:
                                 e.tensor_tensor(d, a, b, ALU.add)),
                      reads=[ropa], writes=[krb[par]])
                if debug and kt == 2:
                    kb.do("gpsimd", (lambda e: e.dma_start(
                        out=dbg["trig"][:, 0:32], in_=cosT_t[2].ap)),
                          reads=[cosT], dma=True)
                    kb.do("gpsimd", (lambda e: e.dma_start(
                        out=dbg["trig"][:, 64:96], in_=sinT_t[2].ap)),
                          reads=[sinT], dma=True)
                    kb.do("gpsimd", (lambda e: e.dma_start(
                        out=dbg["trig"][:, 128:192], in_=alt.ap)),
                          reads=[alt], dma=True)
                    kb.do("gpsimd", (lambda e, s=krf.ap: e.dma_start(
                        out=dbg["krf"], in_=s)),
                          reads=[krf], dma=True)
                    kb.do("gpsimd", (lambda e, s=krb[par].ap: e.dma_start(
                        out=dbg["krb"], in_=s)),
                          reads=[krb[par]], dma=True)
                transposes(krb[par],
                           (lambda i, b=krb[par]: b.ap[:, i * 128:(i + 1) * 128]),
                           krTs[par], 0, 8)
                kb.do("gpsimd", (lambda e,
                                 s=krTs[par].ap.rearrange("p (a n) -> p a n", a=8),
                                 d=Kr_hbm.rearrange("h d t -> (h d) t")[
                                     :, kt * 128:(kt + 1) * 128].rearrange(
                                     "(a p) n -> p a n", p=128):
                                 e.dma_start(out=d, in_=s)),
                      reads=[krTs[par]], dma=True, tag=f"A_krst{kt}")
                if debug:
                    kb.do("gpsimd", (lambda e,
                                     s=krTs[par].ap.rearrange("p (a n) -> p a n", a=8),
                                     d=dbg["kr"].rearrange("h d t -> (h d) t")[
                                         :, kt * 128:(kt + 1) * 128].rearrange(
                                         "(a p) n -> p a n", p=128):
                                     e.dma_start(out=d, in_=s)),
                          reads=[krTs[par]], dma=True)


            for kt in range(NT + 1):
                if kt < NT:
                    emit_A_h1(kt)
                if kt >= 1:
                    emit_A_h2(kt - 1)
        kb.barrier()

        arena.reset()
        xqT_all = arena.sub("xqT_all", [128, NOWN * D], BF16)
        qcraw = arena.sub("qcraw", [128, NOWN * D], BF16)
        qrraw = arena.sub("qrraw", [128, NOWN * H * DR], BF16)
        wch = [arena.sub(f"wch{i}", [128, 16 * 512], BF16) for i in range(2)]
        xtileB = arena.sub("xtileB", [128, D], F32)
        u_scrB = arena.sub("u_scrB", [128, D], F32)
        xqB = arena.sub("xqB", [128, D], BF16)
        invsxB = arena.sub("invsxB", [128, NOWN], F32)
        mxB = arena.sub("mxB", [128, 1], F32)
        sB = arena.sub("sB", [128, 1], F32)
        ss16B = arena.sub("ss16B", [128, 16], F32)
        cs16B = arena.sub("cs16B", [128, 16], F32)
        dqB = arena.sub("dqB", [128, 2], F32)
        qrf = arena.sub("qrf", [128, H * DR], F32)
        ropaB = arena.sub("ropaB", [128, 2 * H * DR], F32)
        qrbB = arena.sub("qrbB", [128, H * DR], BF16)
        qstg = arena.sub("qstg", [128, D], BF16)
        qrstg = arena.sub("qrstg", [128, H * DR], BF16)
        cosB = arena.sub("cosB", [128, 32], F32)
        sinB = arena.sub("sinB", [128, 32], F32)
        tvB = arena.sub("tvB", [128, 1], F32)

        if stage >= 2:
            for j in range(NOWN):
                kb.do("sync", (lambda e, d=xtileB.ap,
                               s=x_own[j * 128:(j + 1) * 128, :]:
                               e.dma_start(out=d, in_=s)),
                      writes=[xtileB], dma=True, tag=f"B_x{j}")
                quant(xtileB, xtileB.ap, xqB, invsxB, j, D, mxB, sB, u_scrB)
                transposes(xqB, (lambda i, b=xqB: b.ap[:, i * 128:(i + 1) * 128]),
                           xqT_all, j * D, 16)
            for oc in range(4):
                wb = wch[oc % 2]
                kb.do("sync", (lambda e, d=wb.ap.rearrange("p (a n) -> p a n", a=16),
                               s=tern["wqc"][:, oc * 512:(oc + 1) * 512].rearrange(
                                   "(a p) n -> p a n", p=128):
                               e.dma_start(out=d, in_=s)),
                      writes=[wb], dma=True, tag=f"B_wqc{oc}")
                for j in range(NOWN):
                    bank = PS[2 + (j % 2)]
                    for ic in range(16):
                        kb.do("tensor", (lambda e,
                                         l=xqT_all.ap[:, j * D + ic * 128: j * D + (ic + 1) * 128],
                                         r=wb.ap[:, ic * 512:(ic + 1) * 512],
                                         st=(ic == 0), sp=(ic == 15), pb=bank.ap:
                                         e.matmul(pb, l, r, start=st, stop=sp)),
                              reads=[xqT_all, wb], writes=[bank], tag="B_qc")
                    kb.do("vector", (lambda e, pb=bank.ap,
                                     d=qcraw.ap[:, j * D + oc * 512: j * D + (oc + 1) * 512]:
                                     e.tensor_copy(d, pb)),
                          reads=[bank], writes=[qcraw])
            for oc in range(2):
                wb = wch[oc % 2]
                kb.do("sync", (lambda e, d=wb.ap.rearrange("p (a n) -> p a n", a=16),
                               s=tern["wqr"][:, oc * 512:(oc + 1) * 512].rearrange(
                                   "(a p) n -> p a n", p=128):
                               e.dma_start(out=d, in_=s)),
                      writes=[wb], dma=True, tag=f"B_wqr{oc}")
                for j in range(NOWN):
                    bank = PS[2 + (j % 2)]
                    for ic in range(16):
                        kb.do("tensor", (lambda e,
                                         l=xqT_all.ap[:, j * D + ic * 128: j * D + (ic + 1) * 128],
                                         r=wb.ap[:, ic * 512:(ic + 1) * 512],
                                         st=(ic == 0), sp=(ic == 15), pb=bank.ap:
                                         e.matmul(pb, l, r, start=st, stop=sp)),
                              reads=[xqT_all, wb], writes=[bank], tag="B_qr")
                    kb.do("vector", (lambda e, pb=bank.ap,
                                     d=qrraw.ap[:, j * (H * DR) + oc * 512: j * (H * DR) + (oc + 1) * 512]:
                                     e.tensor_copy(d, pb)),
                          reads=[bank], writes=[qrraw])
            for j in range(NOWN):
                kb.do("vector", (lambda e, d=dqB.ap[:, 0:1],
                                 h=hws.ap[:, WI["wqc"]:WI["wqc"] + 1],
                                 ix=invsxB.ap[:, j:j + 1]:
                                 e.tensor_scalar(d, h, ix, None, ALU.mult)),
                      reads=[hws, invsxB], writes=[dqB])
                kb.do("vector", (lambda e, d=dqB.ap[:, 1:2],
                                 h=hws.ap[:, WI["wqr"]:WI["wqr"] + 1],
                                 ix=invsxB.ap[:, j:j + 1]:
                                 e.tensor_scalar(d, h, ix, None, ALU.mult)),
                      reads=[hws, invsxB], writes=[dqB])
                qc_j = qcraw.ap[:, j * D:(j + 1) * D]
                kb.do("scalar", (lambda e, s=qc_j, d=u_scrB.ap:
                                 e.activation(d, s, AF.Square)),
                      reads=[qcraw], writes=[u_scrB])
                kb.do("vector", lambda e: e.tensor_reduce(
                    ss16B.ap, u_scrB.ap.rearrange("p (h d) -> p h d", h=16),
                    AX.X, ALU.add),
                    reads=[u_scrB], writes=[ss16B])
                kb.do("vector", (lambda e, q=dqB.ap[:, 0:1], d=mxB.ap:
                                 e.tensor_scalar(d, q, q, None, ALU.mult)),
                      reads=[dqB], writes=[mxB])
                kb.do("vector", (lambda e, q=mxB.ap:
                                 e.tensor_scalar(ss16B.ap, ss16B.ap, q, None,
                                                 ALU.mult)),
                      reads=[ss16B, mxB], writes=[ss16B])
                kb.do("scalar", lambda e: e.activation(
                    ss16B.ap, ss16B.ap, AF.Sqrt, bias=C["eps6"].ap,
                    scale=C["i128"].ap),
                    reads=[ss16B, C["eps6"], C["i128"]], writes=[ss16B])
                kb.do("vector", lambda e: e.reciprocal(cs16B.ap, ss16B.ap),
                      reads=[ss16B], writes=[cs16B])
                kb.do("vector", (lambda e, q=dqB.ap[:, 0:1]:
                                 e.tensor_scalar(cs16B.ap, cs16B.ap, q, None,
                                                 ALU.mult)),
                      reads=[cs16B, dqB], writes=[cs16B])
                for h in range(16):
                    kb.do("vector", (lambda e,
                                     s=qcraw.ap[:, j * D + h * 128: j * D + (h + 1) * 128],
                                     c=cs16B.ap[:, h:h + 1]:
                                     e.tensor_scalar(s, s, c, None, ALU.mult)),
                          reads=[qcraw, cs16B], writes=[qcraw])
                transposes(qcraw,
                           (lambda i, jj=j: qcraw.ap[:, jj * D + i * 128: jj * D + (i + 1) * 128]),
                           qstg, 0, 16, scale_col=0, scale_buf=g_col)
                kb.do("gpsimd", (lambda e, s=qstg.ap.rearrange("p (a n) -> p a n", a=16),
                                 d=Qc_hbm[:, j * 128:(j + 1) * 128].rearrange(
                                     "(a p) n -> p a n", p=128):
                                 e.dma_start(out=d, in_=s)),
                      reads=[qstg], dma=True, tag=f"B_qcst{j}")
                if debug:
                    kb.do("gpsimd", (lambda e, s=qstg.ap.rearrange("p (a n) -> p a n", a=16),
                                     d=dbg["qc"][:, j * 128:(j + 1) * 128].rearrange(
                                         "(a p) n -> p a n", p=128):
                                     e.dma_start(out=d, in_=s)),
                          reads=[qstg], dma=True)
                qr_j = qrraw.ap[:, j * H * DR:(j + 1) * H * DR]
                kb.do("vector", (lambda e, s=qr_j, d=qrf.ap, q=dqB.ap[:, 1:2]:
                                 e.tensor_scalar(d, s, q, None, ALU.mult)),
                      reads=[qrraw, dqB], writes=[qrf])
                emit_trig(None, j, cosB, sinB)
                c3 = cosB.ap.unsqueeze(1).unsqueeze(1).broadcast_to(
                    [128, 16, 2, 32])
                s3 = sinB.ap.unsqueeze(1).unsqueeze(1).broadcast_to(
                    [128, 16, 2, 32])
                qr3 = qrf.ap.rearrange("p (h r d) -> p h r d", h=16, r=2)
                qrsw = qrf.ap.rearrange("p (a b) -> p a b", b=2)[:, :, ::-1]
                t1 = ropaB.ap[:, 0:H * DR]
                t2 = ropaB.ap[:, H * DR:2 * H * DR]
                kb.do("vector", (lambda e, tt=t1, aa=qr3, cc=c3: e.tensor_tensor(
                    tt.rearrange("p (h r d) -> p h r d", h=16, r=2), aa, cc,
                    ALU.mult)),
                      reads=[qrf, cosB], writes=[ropaB])
                kb.do("vector", (lambda e, tt=t2, aa=qrsw, ss=s3: e.tensor_tensor(
                    tt.rearrange("p (h r d) -> p h r d", h=16, r=2), aa, ss,
                    ALU.mult)),
                      reads=[qrf, sinB], writes=[ropaB])
                kb.do("vector", (lambda e, d=qrbB.ap, a=t1, b=t2:
                                 e.tensor_tensor(d, a, b, ALU.add)),
                      reads=[ropaB], writes=[qrbB])
                transposes(qrbB,
                           (lambda i: qrbB.ap[:, i * 128:(i + 1) * 128]),
                           qrstg, 0, 8)
                kb.do("gpsimd", (lambda e,
                                 s=qrstg.ap.rearrange("p (a n) -> p a n", a=8),
                                 d=Qr_hbm.rearrange("h d t -> (h d) t")[
                                     :, j * 128:(j + 1) * 128].rearrange(
                                     "(a p) n -> p a n", p=128):
                                 e.dma_start(out=d, in_=s)),
                      reads=[qrstg], dma=True, tag=f"B_qrst{j}")
                if debug:
                    kb.do("gpsimd", (lambda e,
                                     s=qrstg.ap.rearrange("p (a n) -> p a n", a=8),
                                     d=dbg["qr"].rearrange("h d t -> (h d) t")[
                                         :, j * 128:(j + 1) * 128].rearrange(
                                         "(a p) n -> p a n", p=128):
                                     e.dma_start(out=d, in_=s)),
                          reads=[qrstg], dma=True)

        kb.barrier()

        arena.reset()
        u_fm = arena.sub("u_fm", [128, H * NOWN * 128], BF16)
        lrow = arena.sub("lrow", [1, 512], F32)
        invl_pre = arena.sub("invl_pre", [128, NOWN * 16], F32)
        qc_h = [arena.sub(f"qc_h{i}", [128, NOWN * 128], BF16) for i in range(2)]
        qr_h = [arena.sub(f"qr_h{i}", [65, NOWN * 128], BF16) for i in range(2)]
        kcb = [arena.sub(f"kcb{i}", [128, 512], BF16) for i in range(2)]
        krab = [arena.sub(f"krab{i}", [65, 512], BF16) for i in range(2)]
        vb = [arena.sub(f"vb{i}", [128, 512], BF16) for i in range(2)]
        ptst = [arena.sub(f"ptst{i}", [128, 512], BF16) for i in range(2)]
        sct_scr = arena.sub("sct_scr", [128, 256], F32)
        wtileC = [arena.sub(f"wtileC{i}", [128, 2048], F32) for i in range(2)]
        woutC = [arena.sub(f"woutC{i}", [128, 2048], BF16) for i in range(2)]
        sg1C = arena.sub("sg1C", [128, 2048], BF16)
        waccC = arena.sub("waccC", [128, 1], F32)
        wpartC = arena.sub("wpartC", [128, 1], F32)
        wscC = arena.sub("wscC", [1, 1], F32)
        mneg = arena.sub("mneg", [128, NOWN], F32)
        mnegb = arena.sub("mnegb", [128, 1], BF16)
        invl = arena.sub("invl", [128, NOWN * 16], F32)
        ltr = arena.sub("ltr", [128, 16], F32)

        EXT = [32 * (j // 4) + 32 for j in range(NOWN)]
        UP = [16 * p + 16 for p in range(4)]

        if stage >= 3:
            wgen = w_pipeline(W_LATE, wtileC, woutC, sg1C, waccC, wpartC,
                              wscC, arena, PS[7])
            wgen_live = [True]

            def pump_w(n):
                for _ in range(n):
                    try:
                        next(wgen)
                    except StopIteration:
                        wgen_live[0] = False
                        return

            for i in range(2):
                kb.do("vector", (lambda e, a=krab[i].ap[64:65, :]:
                                 e.memset(a, 1.0)),
                      writes=[krab[i]], tag="ones_r")
            for h in range(H):
                hp = h % 2
                qcb, qrb = qc_h[hp], qr_h[hp]
                kb.do("sync", (lambda e, d=qcb.ap,
                               s=Qc_hbm[h * 128:(h + 1) * 128, :]:
                               e.dma_start(out=d, in_=s)),
                      writes=[qcb], dma=True, tag=f"C_qc{h}")
                kb.do("sync", (lambda e, d=qrb.ap[0:64, :],
                               s=Qr_hbm[h, :, :]:
                               e.dma_start(out=d, in_=s)),
                      writes=[qrb], dma=True, tag=f"C_qr{h}")
                pump_w(14)
                memset_step(mneg, float(3.0e38))
                for kc in range(16):
                    kpar = kc % 2
                    kcc, krc = kcb[kpar], krab[kpar]
                    kb.do("sync", (lambda e, d=kcc.ap,
                                   s=Kc_hbm[kc * 512:(kc + 1) * 512,
                                            h * 128:(h + 1) * 128]:
                                   e.dma_start(out=d, in_=s, transpose=True)),
                          writes=[kcc], dma=True, tag=f"C1_kc{h}_{kc}")
                    kb.do("sync", (lambda e, d=krc.ap[0:64, :],
                                   s=Kr_hbm[h, :, kc * 512:(kc + 1) * 512]:
                                   e.dma_start(out=d, in_=s)),
                          writes=[krc], dma=True, tag=f"C1_kr{h}_{kc}")
                    for j in range(NOWN):
                        if kc * 4 >= EXT[j]:
                            continue
                        bank = PS[kc % 2]
                        kb.do("tensor", (lambda e,
                                         l=qcb.ap[:, j * 128:(j + 1) * 128],
                                         r=kcc.ap, pb=bank.ap:
                                         e.matmul(pb, l, r, start=True, stop=False)),
                              reads=[qcb, kcc], writes=[bank], tag="C1mmc")
                        kb.do("tensor", (lambda e,
                                         l=qrb.ap[0:64, j * 128:(j + 1) * 128],
                                         r=krc.ap[0:64, :], pb=bank.ap:
                                         e.matmul(pb, l, r, start=False, stop=True)),
                              reads=[qrb, krc], writes=[bank], tag="C1mmr")
                        kb.do("vector", (lambda e, pb=bank.ap, d=ltr.ap[:, 0:1]:
                                         e.tensor_reduce(d, pb, AX.X, ALU.max,
                                                         negate=True)),
                              reads=[bank], writes=[ltr])
                        kb.do("vector", (lambda e, d=mneg.ap[:, j:j + 1],
                                         s=ltr.ap[:, 0:1]:
                                         e.tensor_tensor(d, d, s, ALU.min)),
                              reads=[mneg, ltr], writes=[mneg])
                if debug and h == 3:
                    kb.do("gpsimd", (lambda e: e.dma_start(
                        out=dbg["mneg"], in_=mneg.ap)),
                          reads=[mneg], dma=True)
                pump_w(14)
                for j in range(NOWN):
                    kb.do("vector", (lambda e, s=mneg.ap[:, j:j + 1]:
                                     e.tensor_copy(mnegb.ap, s)),
                          reads=[mneg], writes=[mnegb])
                    bank = PSB[0]
                    kb.do("tensor", (lambda e, pb=bank.ap[:, 0:128]:
                                     e.transpose(pb[0:1, 0:128], mnegb.ap,
                                                 ident_b.ap)),
                          reads=[mnegb, ident_b], writes=[bank], tag="C_mt")
                    kb.do("vector", (lambda e, pb=bank.ap[0:1, 0:128],
                                     d=qrb.ap[64:65, j * 128:(j + 1) * 128]:
                                     e.tensor_copy(d, pb)),
                          reads=[bank], writes=[qrb])
                for half in range(2):
                    U = UP[2 * half + 1]
                    lbank = PS[6]
                    abank = PS[4]
                    qoff = half * 512
                    pending = None

                    def flush(ev):
                        kt_, pt_, vslice_ = ev
                        kb.do("tensor", (lambda e, r=pt_.ap, pb=lbank.ap[0:1, 0:512],
                                         st=(kt_ == 0), sp=(kt_ == U - 1):
                                         e.matmul(pb, ones_col_b.ap, r,
                                                  start=st, stop=sp)),
                              reads=[ones_col_b, pt_], writes=[lbank], tag="C2l")
                        kb.do("tensor", (lambda e, l=vslice_, r=pt_.ap,
                                         pb=abank.ap,
                                         st=(kt_ == 0), sp=(kt_ == U - 1):
                                         e.matmul(pb, l, r, start=st, stop=sp)),
                              reads=[vb[0], vb[1], pt_], writes=[abank],
                              tag="C2av")

                    for kc in range(U // 4):
                        kpar = kc % 2
                        kcc, krc, vcc = kcb[kpar], krab[kpar], vb[kpar]
                        kb.do("sync", (lambda e, d=kcc.ap,
                                       s=Kc_hbm[kc * 512:(kc + 1) * 512,
                                                h * 128:(h + 1) * 128]:
                                       e.dma_start(out=d, in_=s, transpose=True)),
                              writes=[kcc], dma=True, tag=f"C2_kc{h}_{half}_{kc}")
                        kb.do("sync", (lambda e, d=krc.ap[0:64, :],
                                       s=Kr_hbm[h, :, kc * 512:(kc + 1) * 512]:
                                       e.dma_start(out=d, in_=s)),
                              writes=[krc], dma=True, tag=f"C2_kr{h}_{half}_{kc}")
                        kb.do("sync", (lambda e,
                                       d=vcc.ap.rearrange("p (a n) -> p a n", a=4),
                                       s=V_hbm[kc * 512:(kc + 1) * 512,
                                               h * 128:(h + 1) * 128].rearrange(
                                           "(a p) n -> p a n", p=128):
                                       e.dma_start(out=d, in_=s)),
                              writes=[vcc], dma=True, tag=f"C2_v{h}_{half}_{kc}")
                        for kt_in in range(4):
                            kt = kc * 4 + kt_in
                            sbank = PS[2 + (kt % 2)]
                            pt = ptst[kt % 2]
                            kb.do("tensor", (lambda e,
                                             l=kcc.ap[:, kt_in * 128:(kt_in + 1) * 128],
                                             r=qcb.ap[:, qoff:qoff + 512],
                                             pb=sbank.ap:
                                             e.matmul(pb, l, r, start=True,
                                                      stop=False)),
                                  reads=[kcc, qcb], writes=[sbank], tag="C2mmc")
                            kb.do("tensor", (lambda e,
                                             l=krc.ap[0:65, kt_in * 128:(kt_in + 1) * 128],
                                             r=qrb.ap[0:65, qoff:qoff + 512],
                                             pb=sbank.ap:
                                             e.matmul(pb, l, r, start=False,
                                                      stop=True)),
                                  reads=[krc, qrb], writes=[sbank], tag="C2mmr")
                            if pending is not None:
                                flush(pending)
                            kb.do("scalar", (lambda e, pb=sbank.ap, d=pt.ap:
                                             e.activation(d, pb, AF.Exp,
                                                          scale=C["scl"].ap)),
                                  reads=[sbank, C["scl"]], writes=[pt],
                                  tag="C2exp")
                            for jj in range(4):
                                j = 4 * half + jj
                                if kt >= 8 * j:
                                    irel = kt - 8 * j
                                    kb.do("vector", (lambda e,
                                                     d=pt.ap[:, jj * 128:(jj + 1) * 128],
                                                     m=Mmask.ap[:, irel * 128:(irel + 1) * 128]:
                                                     e.tensor_tensor(d, d, m,
                                                                     ALU.mult)),
                                          reads=[pt, Mmask], writes=[pt],
                                          tag="C2msk")
                            pending = (kt, pt,
                                       vcc.ap[:, kt_in * 128:(kt_in + 1) * 128])
                    flush(pending)
                    kb.do("vector", (lambda e, pb=abank.ap,
                                     d=u_fm.ap[:, h * NOWN * 128 + half * 512: h * NOWN * 128 + half * 512 + 512]:
                                     e.tensor_copy(d, pb)),
                          reads=[abank], writes=[u_fm], tag="C2uev")
                    kb.do("vector", (lambda e, pb=lbank.ap[0:1, 0:512]:
                                     e.tensor_copy(lrow.ap, pb)),
                          reads=[lbank], writes=[lrow], tag="C2lev")
                    for qi in range(4):
                        j = 4 * half + qi
                        tb = PS[0]
                        kb.do("tensor", (lambda e, qq=qi, pb=tb.ap[:, 0:1]:
                                         e.transpose(
                                             pb, lrow.ap[0:1, qq * 128:(qq + 1) * 128],
                                             ones_row_f.ap[0:1, 0:1])),
                              reads=[lrow, ones_row_f], writes=[tb], tag="C2lt")
                        kb.do("vector", (lambda e, pb=tb.ap[:, 0:1],
                                         d=invl_pre.ap[:, j * 16 + h: j * 16 + h + 1]:
                                         e.tensor_copy(d, pb)),
                              reads=[tb], writes=[invl_pre], tag="C2ltev")
                pump_w(14)
            while wgen_live[0]:
                pump_w(50)
            for j in range(NOWN):
                kb.do("vector", (lambda e, s=invl_pre.ap[:, j * 16:(j + 1) * 16],
                                 d=ltr.ap[:, 0:16]:
                                 e.tensor_scalar(d, s, C["tiny"].ap, None,
                                                 ALU.max)),
                      reads=[invl_pre, C["tiny"]], writes=[ltr])
                kb.do("vector", (lambda e,
                                 d=invl.ap[:, j * 16:(j + 1) * 16]:
                                 e.reciprocal(d, ltr.ap[:, 0:16])),
                      reads=[ltr], writes=[invl])
            for j in range(NOWN):
                for h in range(H):
                    bank = PSB[(h % 2)]
                    kb.do("tensor", (lambda e,
                                     s=u_fm.ap[:, h * NOWN * 128 + j * 128: h * NOWN * 128 + (j + 1) * 128],
                                     pb=bank.ap[:, 0:128]:
                                     e.transpose(pb, s, ident_b.ap)),
                          reads=[u_fm, ident_b], writes=[bank], tag="Cf_ut")
                    kb.do("vector", (lambda e, pb=bank.ap[:, 0:128],
                                     d=attn_out.ap[:, j * D + h * 128: j * D + (h + 1) * 128],
                                     c=invl.ap[:, j * 16 + h: j * 16 + h + 1]:
                                     e.tensor_scalar(d, pb, c, None, ALU.mult)),
                          reads=[bank, invl], writes=[attn_out], tag="Cf_uev")
            if debug:
                for j in range(NOWN):
                    kb.do("gpsimd", (lambda e,
                                     s=attn_out.ap[:, j * D:(j + 1) * D],
                                     d=dbg["ao"][:, j * D:(j + 1) * D]:
                                     e.dma_start(out=d, in_=s)),
                          reads=[attn_out], dma=True)

        kb.barrier()

        arena.reset()
        subln_bc = arena.sub("subln_bc", [128, FF], BF16)
        subln_row = arena.sub("subln_row", [1, 512], F32)
        xo = [arena.sub(f"xo{i}", [128, D], F32) for i in range(2)]
        aoqT = [arena.sub(f"aoqT{i}", [128, D], BF16) for i in range(2)]
        h1 = [arena.sub(f"h1_{i}", [128, D], F32) for i in range(2)]
        act_b = [arena.sub(f"act{i}", [128, FF], BF16) for i in range(2)]
        h2 = xo[1]
        fqT = aoqT
        BqT = arena.sub("BqT", [128, FF], BF16)
        wdb = [arena.sub(f"wdb{i}", [128, 8 * 512], BF16) for i in range(4)]
        u_scrD = arena.sub("u_scrD", [128, D], F32)
        tq = arena.sub("tq", [128, D], BF16)
        rlu = arena.sub("rlu", [128, 512], BF16)
        rr2 = arena.sub("rr2", [128, 512], F32)
        scr512 = arena.sub("scr512", [128, 512], BF16)
        sscols = [arena.sub(f"sscols{i}", [128, 16], F32) for i in range(2)]
        mxD = arena.sub("mxD", [128, 1], F32)
        sD = arena.sub("sD", [128, 1], F32)
        sc1 = arena.sub("sc1", [128, 1], F32)
        sc2 = arena.sub("sc2", [128, 1], F32)
        dqD = [arena.sub(f"dqD{i}", [128, 4], F32) for i in range(2)]

        def ln_steps(hbuf, outbuf, eps_key="eps5"):
            kb.do("vector", (lambda e, s=hbuf.ap: e.tensor_reduce(
                mxD.ap, s, AX.X, ALU.add)),
                reads=[hbuf], writes=[mxD])
            kb.do("vector", lambda e: e.tensor_scalar(
                mxD.ap, mxD.ap, C["i2048"].ap, None, ALU.mult),
                reads=[mxD, C["i2048"]], writes=[mxD])
            kb.do("vector", (lambda e, s=hbuf.ap: e.tensor_scalar(
                s, s, mxD.ap, None, ALU.subtract)),
                reads=[hbuf, mxD], writes=[hbuf])
            kb.do("vector", (lambda e, s=hbuf.ap: e.scalar_tensor_tensor(
                u_scrD.ap, s, C["one"].ap, s, ALU.mult, ALU.mult,
                accum_out=sD.ap)),
                reads=[hbuf, C["one"]], writes=[u_scrD, sD])
            kb.do("scalar", (lambda e, ek=eps_key: e.activation(
                sD.ap, sD.ap, AF.Sqrt, bias=C[ek].ap, scale=C["i2048"].ap)),
                reads=[sD, C[eps_key], C["i2048"]], writes=[sD])
            kb.do("vector", lambda e: e.reciprocal(sc1.ap, sD.ap),
                  reads=[sD], writes=[sc1])
            kb.do("vector", (lambda e, s=hbuf.ap, d=outbuf.ap: e.tensor_scalar(
                d, s, sc1.ap, None, ALU.mult)),
                reads=[hbuf, sc1], writes=[outbuf])

        if stage >= 4:
            for i in range(16):
                bank = PS[6]
                kb.do("sync", (lambda e, s=subln_in[0:1, i * 512:(i + 1) * 512]:
                               e.dma_start(out=subln_row.ap, in_=s)),
                      writes=[subln_row], dma=True)
                kb.do("tensor", (lambda e,
                                 r=subln_row.ap[0:1, 0:512],
                                 pb=bank.ap:
                                 e.matmul(pb, ones_row_f.ap[0:1, :], r,
                                          start=True, stop=True)),
                      reads=[ones_row_f, subln_row], writes=[bank], tag="D_sb")
                kb.do("vector", (lambda e, pb=bank.ap,
                                 d=subln_bc.ap[:, i * 512:(i + 1) * 512]:
                                 e.tensor_copy(d, pb)),
                      reads=[bank], writes=[subln_bc])
            for grp in range(4):
                tiles = [2 * grp, 2 * grp + 1]
                for ti, t in enumerate(tiles):
                    kb.do("sync", (lambda e, d=xo[ti].ap,
                                   s=x_own[t * 128:(t + 1) * 128, :]:
                                   e.dma_start(out=d, in_=s)),
                          writes=[xo[ti]], dma=True, tag=f"D_x{t}")
                    ao = Buf(attn_out.ap[:, t * D:(t + 1) * D], "ao")
                    ao.res = attn_out.res
                    quant(ao, ao.ap, tq, dqD[ti], 0, D, mxD, sD, u_scrD)
                    kb.do("vector", (lambda e, d=dqD[ti].ap[:, 0:1],
                                     h=hws.ap[:, WI["wout"]:WI["wout"] + 1]:
                                     e.tensor_scalar(d, d, h, None, ALU.mult)),
                          reads=[dqD[ti], hws], writes=[dqD[ti]])
                    transposes(tq, (lambda i: tq.ap[:, i * 128:(i + 1) * 128]),
                               aoqT[ti], 0, 16)
                for oc in range(4):
                    for kh in range(2):
                        wb = wdb[(oc * 2 + kh) % 2]
                        kb.do("sync", (lambda e,
                                       d=wb.ap.rearrange("p (a n) -> p a n", a=8),
                                       s=tern["wout"][kh * 1024:(kh + 1) * 1024,
                                                      oc * 512:(oc + 1) * 512].rearrange(
                                           "(a p) n -> p a n", p=128):
                                       e.dma_start(out=d, in_=s)),
                              writes=[wb], dma=True, tag=f"D_wo{oc}_{kh}")
                        for ti in range(2):
                            bank = PS[2 + ti]
                            for ic in range(8):
                                kb.do("tensor", (lambda e,
                                                 l=aoqT[ti].ap[:, kh * 1024 + ic * 128: kh * 1024 + (ic + 1) * 128],
                                                 r=wb.ap[:, ic * 512:(ic + 1) * 512],
                                                 st=(kh == 0 and ic == 0),
                                                 sp=(kh == 1 and ic == 7),
                                                 pb=bank.ap:
                                                 e.matmul(pb, l, r, start=st,
                                                          stop=sp)),
                                      reads=[aoqT[ti], wb], writes=[bank],
                                      tag="D_womm")
                    for ti in range(2):
                        bank = PS[2 + ti]
                        kb.do("vector", (lambda e, pb=bank.ap,
                                         x=xo[ti].ap[:, oc * 512:(oc + 1) * 512],
                                         d=h1[ti].ap[:, oc * 512:(oc + 1) * 512],
                                         q=dqD[ti].ap[:, 0:1]:
                                         e.scalar_tensor_tensor(
                                             d, pb, q, x, ALU.mult, ALU.add)),
                              reads=[bank, xo[ti], dqD[ti]], writes=[h1[ti]],
                              tag="D_woev")
                for ti in range(2):
                    ln_steps(h1[ti], h1[ti])
                    quant(h1[ti], h1[ti].ap, tq, dqD[ti], 1, D, mxD, sD, u_scrD)
                    transposes(tq, (lambda i: tq.ap[:, i * 128:(i + 1) * 128]),
                               fqT[ti], 0, 16)
                    memset_step(sscols[ti], 0.0)
                for oc in range(16):
                    for kh in range(2):
                        wgb = wdb[kh % 2]
                        wub = wdb[2 + kh % 2]
                        kb.do("sync", (lambda e,
                                       d=wgb.ap.rearrange("p (a n) -> p a n", a=8),
                                       s=tern["wg"][kh * 1024:(kh + 1) * 1024,
                                                    oc * 512:(oc + 1) * 512].rearrange(
                                           "(a p) n -> p a n", p=128):
                                       e.dma_start(out=d, in_=s)),
                              writes=[wgb], dma=True, tag=f"D_wg{oc}_{kh}")
                        kb.do("sync", (lambda e,
                                       d=wub.ap.rearrange("p (a n) -> p a n", a=8),
                                       s=tern["wu"][kh * 1024:(kh + 1) * 1024,
                                                    oc * 512:(oc + 1) * 512].rearrange(
                                           "(a p) n -> p a n", p=128):
                                       e.dma_start(out=d, in_=s)),
                              writes=[wub], dma=True, tag=f"D_wu{oc}_{kh}")
                        for ti in range(2):
                            gbank = PS[2 + ti]
                            ubank = PS[4 + ti]
                            for ic in range(8):
                                kb.do("tensor", (lambda e,
                                                 l=fqT[ti].ap[:, kh * 1024 + ic * 128: kh * 1024 + (ic + 1) * 128],
                                                 r=wgb.ap[:, ic * 512:(ic + 1) * 512],
                                                 st=(kh == 0 and ic == 0),
                                                 sp=(kh == 1 and ic == 7),
                                                 pb=gbank.ap:
                                                 e.matmul(pb, l, r, start=st,
                                                          stop=sp)),
                                      reads=[fqT[ti], wgb], writes=[gbank],
                                      tag="D_gmm")
                                kb.do("tensor", (lambda e,
                                                 l=fqT[ti].ap[:, kh * 1024 + ic * 128: kh * 1024 + (ic + 1) * 128],
                                                 r=wub.ap[:, ic * 512:(ic + 1) * 512],
                                                 st=(kh == 0 and ic == 0),
                                                 sp=(kh == 1 and ic == 7),
                                                 pb=ubank.ap:
                                                 e.matmul(pb, l, r, start=st,
                                                          stop=sp)),
                                      reads=[fqT[ti], wub], writes=[ubank],
                                      tag="D_umm")
                    for ti in range(2):
                        gbank = PS[2 + ti]
                        ubank = PS[4 + ti]
                        kb.do("scalar", (lambda e, pb=gbank.ap, d=rlu.ap:
                                         e.activation(d, pb, AF.Relu)),
                              reads=[gbank], writes=[rlu], tag="D_relu")
                        kb.do("vector", lambda e: e.tensor_tensor(
                            rr2.ap, rlu.ap, rlu.ap, ALU.mult),
                            reads=[rlu], writes=[rr2], tag="D_rr")
                        kb.do("vector", (lambda e, pb=ubank.ap,
                                         d=act_b[ti].ap[:, oc * 512:(oc + 1) * 512]:
                                         e.tensor_tensor(d, rr2.ap, pb, ALU.mult)),
                              reads=[rr2, ubank], writes=[act_b[ti]], tag="D_act")
                        kb.do("vector", (lambda e,
                                         s=act_b[ti].ap[:, oc * 512:(oc + 1) * 512],
                                         acc=sscols[ti].ap[:, oc:oc + 1]:
                                         e.scalar_tensor_tensor(
                                             scr512.ap, s, C["one"].ap, s,
                                             ALU.mult, ALU.mult, accum_out=acc)),
                              reads=[act_b[ti], C["one"]],
                              writes=[scr512, sscols[ti]], tag="D_ss")
                for ti, t in enumerate(tiles):
                    kb.do("vector", (lambda e, d=sc1.ap,
                                     h=hws.ap[:, WI["wg"]:WI["wg"] + 1],
                                     ix=dqD[ti].ap[:, 1:2]:
                                     e.tensor_scalar(d, h, ix, None, ALU.mult)),
                          reads=[hws, dqD[ti]], writes=[sc1])
                    kb.do("vector", (lambda e, d=sc2.ap,
                                     h=hws.ap[:, WI["wu"]:WI["wu"] + 1],
                                     ix=dqD[ti].ap[:, 1:2]:
                                     e.tensor_scalar(d, h, ix, None, ALU.mult)),
                          reads=[hws, dqD[ti]], writes=[sc2])
                    kb.do("vector", lambda e: e.tensor_tensor(
                        sc1.ap, sc1.ap, sc1.ap, ALU.mult),
                        reads=[sc1], writes=[sc1])
                    kb.do("vector", (lambda e, d=dqD[ti].ap[:, 2:3]:
                                     e.tensor_tensor(d, sc1.ap, sc2.ap, ALU.mult)),
                          reads=[sc1, sc2], writes=[dqD[ti]])
                    kb.do("vector", (lambda e, s=sscols[ti].ap:
                                     e.tensor_reduce(mxD.ap, s, AX.X, ALU.add)),
                          reads=[sscols[ti]], writes=[mxD])
                    kb.do("vector", (lambda e, q=dqD[ti].ap[:, 2:3]:
                                     e.tensor_scalar(sc1.ap, q, q, None, ALU.mult)),
                          reads=[dqD[ti]], writes=[sc1])
                    kb.do("vector", lambda e: e.tensor_tensor(
                        mxD.ap, mxD.ap, sc1.ap, ALU.mult),
                        reads=[mxD, sc1], writes=[mxD])
                    kb.do("scalar", lambda e: e.activation(
                        mxD.ap, mxD.ap, AF.Sqrt, bias=C["eps6"].ap,
                        scale=C["i8192"].ap),
                        reads=[mxD, C["eps6"], C["i8192"]], writes=[mxD])
                    kb.do("vector", lambda e: e.reciprocal(sc2.ap, mxD.ap),
                          reads=[mxD], writes=[sc2])
                    kb.do("vector", (lambda e, q=dqD[ti].ap[:, 2:3]:
                                     e.tensor_tensor(sc2.ap, sc2.ap, q, ALU.mult)),
                          reads=[sc2, dqD[ti]], writes=[sc2])
                    for oc in range(16):
                        kb.do("vector", (lambda e,
                                         s=act_b[ti].ap[:, oc * 512:(oc + 1) * 512],
                                         g=subln_bc.ap[:, oc * 512:(oc + 1) * 512]:
                                         e.scalar_tensor_tensor(
                                             s, s, sc2.ap, g, ALU.mult, ALU.mult)),
                              reads=[act_b[ti], sc2, subln_bc],
                              writes=[act_b[ti]], tag="D_sub")
                    kb.do("vector", (lambda e, s=act_b[ti].ap:
                                     e.tensor_reduce(mxD.ap, s, AX.X, ALU.max,
                                                     apply_absolute_value=True)),
                          reads=[act_b[ti]], writes=[mxD])
                    kb.do("vector", lambda e: e.tensor_scalar(
                        mxD.ap, mxD.ap, C["eps5"].ap, None, ALU.max),
                        reads=[mxD, C["eps5"]], writes=[mxD])
                    kb.do("vector", (lambda e, d=dqD[ti].ap[:, 3:4]:
                                     e.tensor_scalar(d, mxD.ap, C["i127"].ap,
                                                     None, ALU.mult)),
                          reads=[mxD, C["i127"]], writes=[dqD[ti]])
                    kb.do("vector", (lambda e, d=dqD[ti].ap[:, 3:4],
                                     h=hws.ap[:, WI["wd"]:WI["wd"] + 1]:
                                     e.tensor_scalar(d, d, h, None, ALU.mult)),
                          reads=[dqD[ti], hws], writes=[dqD[ti]])
                    kb.do("vector", lambda e: e.reciprocal(sD.ap, mxD.ap),
                          reads=[mxD], writes=[sD])
                    kb.do("vector", lambda e: e.tensor_scalar(
                        sD.ap, sD.ap, C["c127"].ap, None, ALU.mult),
                        reads=[sD, C["c127"]], writes=[sD])
                    for oc in range(16):
                        kb.do("scalar", (lambda e,
                                         s=act_b[ti].ap[:, oc * 512:(oc + 1) * 512],
                                         d=u_scrD.ap[:, 0:512]:
                                         e.activation(d, s, AF.Identity,
                                                      bias=C["magic"].ap,
                                                      scale=sD.ap)),
                              reads=[act_b[ti], sD, C["magic"]],
                              writes=[u_scrD], tag="D_bq1")
                        kb.do("vector", (lambda e,
                                         d=act_b[ti].ap[:, oc * 512:(oc + 1) * 512],
                                         s=u_scrD.ap[:, 0:512]:
                                         e.tensor_scalar(d, s, C["magic"].ap,
                                                         None, ALU.subtract)),
                              reads=[u_scrD, C["magic"]], writes=[act_b[ti]],
                              tag="D_bq2")
                    transposes(act_b[ti],
                               (lambda i, tb=act_b[ti]: tb.ap[:, i * 128:(i + 1) * 128]),
                               BqT, 0, 64)
                    for oc2 in range(4):
                        bank = PS[2 + ti]
                        for kc in range(4):
                            wb = wdb[kc % 2]
                            kb.do("sync", (lambda e,
                                           d=wb.ap.rearrange("p (a n) -> p a n", a=8),
                                           s=tern["wd"][kc * 2048:(kc * 2048 + 1024),
                                                        oc2 * 512:(oc2 + 1) * 512].rearrange(
                                               "(a p) n -> p a n", p=128):
                                           e.dma_start(out=d, in_=s)),
                                  writes=[wb], dma=True, tag=f"D_wd")
                            wb2 = wdb[2 + kc % 2]
                            kb.do("sync", (lambda e,
                                           d=wb2.ap.rearrange("p (a n) -> p a n", a=8),
                                           s=tern["wd"][(kc * 2048 + 1024):(kc + 1) * 2048,
                                                        oc2 * 512:(oc2 + 1) * 512].rearrange(
                                               "(a p) n -> p a n", p=128):
                                           e.dma_start(out=d, in_=s)),
                                  writes=[wb2], dma=True, tag=f"D_wd2")
                            for hf, wbx in ((0, wb), (1, wb2)):
                                for ic in range(8):
                                    icg = kc * 16 + hf * 8 + ic
                                    kb.do("tensor", (lambda e,
                                                     l=BqT.ap[:, icg * 128:(icg + 1) * 128],
                                                     r=wbx.ap[:, ic * 512:(ic + 1) * 512],
                                                     st=(icg == 0), sp=(icg == 63),
                                                     pb=bank.ap:
                                                     e.matmul(pb, l, r, start=st,
                                                              stop=sp)),
                                          reads=[BqT, wbx], writes=[bank],
                                          tag="D_dmm")
                        kb.do("vector", (lambda e, pb=bank.ap,
                                         x=h1[ti].ap[:, oc2 * 512:(oc2 + 1) * 512],
                                         d=h2.ap[:, oc2 * 512:(oc2 + 1) * 512],
                                         q=dqD[ti].ap[:, 3:4]:
                                         e.scalar_tensor_tensor(
                                             d, pb, q, x, ALU.mult, ALU.add)),
                              reads=[bank, h1[ti], dqD[ti]], writes=[h2],
                              tag="D_dev")
                    ln_steps(h2, h2)
                    kb.do("gpsimd", (lambda e, s=h2.ap,
                                     d=y_out[t * 128:(t + 1) * 128, :]:
                                     e.dma_start(out=d, in_=s)),
                          reads=[h2], dma=True, tag=f"D_y{t}")

        last_q = {}
        for st in kb.steps:
            if st.dma:
                last_q[st.queue] = st
        kb.do("gpsimd", lambda e: e.nop(),
              extra=[s for s in last_q.values()], tag="final_wait")

        kb.emit(block)

    return nc


_nc_cache = {}


def _get_nc():
    key = (STAGE, DEBUG_DUMPS)
    if key not in _nc_cache:
        _nc_cache[key] = build_program(STAGE, DEBUG_DUMPS)
    return _nc_cache[key]


def kernel(L0_attn_x, L0_attn_wq_content, L0_attn_wq_rope, L0_attn_wkv_down,
           L0_attn_wk_up, L0_attn_wv_up, L0_attn_wkr_up, L0_attn_wout,
           L0_attn_qn_w, L0_attn_kn_w, L0_ffn_w_gate, L0_ffn_w_up,
           L0_ffn_w_down, L0_ffn_subln_weight):
    f32 = lambda a: np.ascontiguousarray(np.asarray(a, dtype=np.float32))
    x = f32(L0_attn_x).reshape(T, D)
    wT = {
        "wqc": f32(L0_attn_wq_content).T, "wqr": f32(L0_attn_wq_rope).T,
        "wkvd": f32(L0_attn_wkv_down).T, "wku": f32(L0_attn_wk_up).T,
        "wvu": f32(L0_attn_wv_up).T, "wkr": f32(L0_attn_wkr_up).T,
        "wout": f32(L0_attn_wout).T, "wg": f32(L0_ffn_w_gate).T,
        "wu": f32(L0_ffn_w_up).T, "wd": f32(L0_ffn_w_down).T,
    }
    wT = {k: np.ascontiguousarray(v) for k, v in wT.items()}
    qn = f32(L0_attn_qn_w).reshape(128, 1)
    kn = f32(L0_attn_kn_w).reshape(128, 1)
    subln = f32(L0_ffn_subln_weight).reshape(1, FF)
    xt = x.reshape(NT, 128, D)
    in_maps = []
    for c in range(NCORES):
        own = np.ascontiguousarray(
            xt[[8 * j + c for j in range(NOWN)]].reshape(NOWN * 128, D))
        m = {"x_full": x, "x_own": own,
             "cofs": np.full((128, 1), 128.0 * c, np.float32),
             "qn": qn, "kn": kn, "subln": subln}
        for nm in wT:
            m[nm + "T"] = wT[nm]
        in_maps.append(m)
    nc = _get_nc()
    res = run_bass_kernel_spmd(nc, in_maps, core_ids=list(range(NCORES)))
    y = np.empty((NT, 128, D), dtype=np.float32)
    for c in range(NCORES):
        yc = res.results[c]["y"].reshape(NOWN, 128, D)
        for j in range(NOWN):
            y[8 * j + c] = yc[j]
    return y.reshape(1, T, D)




# version 06
# speedup vs baseline: 1.0619x; 1.0619x over previous
import numpy as np
import concourse.bass as bass
import concourse.mybir as mybir
from concourse.bass_utils import run_bass_kernel_spmd

F32 = mybir.dt.float32
BF16 = mybir.dt.bfloat16
AF = mybir.ActivationFunctionType
ALU = mybir.AluOpType
AX = mybir.AxisListType

MAGIC = float(3 * 2 ** 22)
NCORES = 8
T, D, H, DH, DR, LAT, FF = 8192, 2048, 16, 128, 64, 512, 8192
NT = T // 128
NOWN = NT // NCORES
SCALE = float(np.float32(1.0 / np.sqrt(DH + DR)))
LN10K = float(np.log(10000.0))

STAGE = 4
DEBUG_DUMPS = False

ENGS = ("sync", "gpsimd", "scalar", "vector", "tensor")


class Step:
    __slots__ = ("eng", "fn", "dma", "waits", "mark", "cum", "tag", "eidx")

    def __init__(self, eng, fn, dma, tag):
        self.eng, self.fn, self.dma, self.tag = eng, fn, dma, tag
        self.waits = []
        self.mark = False
        self.cum = 0
        self.eidx = 0

    @property
    def queue(self):
        return ("q_" + self.eng) if self.dma else self.eng


class Res:
    __slots__ = ("w", "rs", "name")

    def __init__(self, name=""):
        self.w = None
        self.rs = []
        self.name = name


class Buf:
    __slots__ = ("ap", "res")

    def __init__(self, ap, name=""):
        self.ap = ap
        self.res = Res(name)

    def __getitem__(self, key):
        return self.ap[key]


class KB:
    def __init__(self, nc):
        self.nc = nc
        self.steps = []
        self.sems = {}

    def do(self, eng, fn, reads=(), writes=(), extra=(), dma=False, tag=""):
        st = Step(eng, fn, dma, tag)
        seen = {}

        def addw(w, raw):
            if w is None:
                return
            if id(w) in seen:
                if raw and not seen[id(w)][1]:
                    seen[id(w)][1] = True
                return
            ent = [w, raw]
            seen[id(w)] = ent
            st.waits.append(ent)

        for r in reads:
            addw(r.res.w if isinstance(r, Buf) else r.w, True)
        for r in writes:
            rr = r.res if isinstance(r, Buf) else r
            addw(rr.w, False)
            for s in rr.rs:
                addw(s, False)
        for w in extra:
            addw(w, False)
        self.steps.append(st)
        for r in reads:
            rr = r.res if isinstance(r, Buf) else r
            rr.rs.append(st)
        for r in writes:
            rr = r.res if isinstance(r, Buf) else r
            rr.w = st
            rr.rs = []
        return st

    def barrier(self):
        firsts = []
        last_by_q = {}
        for st in self.steps:
            last_by_q[st.queue] = st
        dma_lasts = [s for q, s in last_by_q.items() if q.startswith("q_")]
        for eng in ENGS:
            firsts.append(self.do(eng, lambda e: e.nop(), extra=dma_lasts,
                                   tag="bar1"))
        for eng in ENGS:
            self.do(eng, lambda e: e.nop(), extra=firsts, tag="bar2")

    FENCE_WINDOW = 10

    def emit(self, block):
        eidx = {}
        for st in self.steps:
            eidx[st.eng] = eidx.get(st.eng, 0) + 1
            st.eidx = eidx[st.eng]
        for st in self.steps:
            if st.dma:
                st.mark = True
            for w, raw in st.waits:
                if w.dma or w.eng != st.eng:
                    w.mark = True
                elif raw and st.eidx - w.eidx <= self.FENCE_WINDOW:
                    w.mark = True
        cum = {}
        for st in self.steps:
            q = st.queue
            if st.mark:
                cum[q] = cum.get(q, 0) + (16 if st.dma else 1)
            st.cum = cum.get(q, 0)
        for q in set(s.queue for s in self.steps):
            self.sems[q] = self.nc.alloc_semaphore(q)

        def mk_body(eng):
            def body(e):
                water = {}
                for st in self.steps:
                    if st.eng != eng:
                        continue
                    wmap = {}
                    for w, raw in st.waits:
                        if (not w.dma) and w.eng == eng and (
                                (not raw) or st.eidx - w.eidx > KB.FENCE_WINDOW):
                            continue
                        assert w.mark and w.cum > 0
                        q = w.queue
                        if wmap.get(q, 0) < w.cum:
                            wmap[q] = w.cum
                    for q, v in sorted(wmap.items()):
                        if water.get(q, 0) >= v:
                            continue
                        e.wait_ge(self.sems[q], v)
                        water[q] = v
                    inst = st.fn(e)
                    if st.tag:
                        inst.annotate(st.tag)
                    if st.mark:
                        inst.then_inc(self.sems[st.queue],
                                      16 if st.dma else 1)
            return body

        block.sync(mk_body("sync"))
        block.gpsimd(mk_body("gpsimd"))
        block.scalar(mk_body("scalar"))
        block.vector(mk_body("vector"))
        block.tensor(mk_body("tensor"))


class Arena:

    _n = 0

    def __init__(self, nc, nbytes):
        Arena._n += 1
        self.t = nc.alloc_sbuf_tensor(f"arena{Arena._n}", [128, nbytes // 4], F32)
        self.nbytes = nbytes
        self.cur = 0

    def reset(self):
        self.cur = 0

    def sub(self, name, shape, dtype):
        assert len(shape) == 2
        esz = 2 if dtype == BF16 else 4
        nb = shape[1] * esz
        off = (self.cur + 31) & ~31
        self.cur = off + nb
        assert self.cur <= self.nbytes, f"arena overflow at {name}: {self.cur}"
        base = self.t.ap()[:, off // 4:(off + nb + 3) // 4]
        ap = base.bitcast(dtype) if dtype == BF16 else base
        ap = ap[0:shape[0], 0:shape[1]]
        return Buf(ap, name)


def build_program(stage=STAGE, debug=DEBUG_DUMPS):
    nc = bass.Bass("TRN2", target_bir_lowering=False,
                   detect_race_conditions=False)

    dt = lambda n, s, d=F32, k="ExternalInput": nc.dram_tensor(n, s, d, kind=k).ap()
    x_full = dt("x_full", [T, D])
    x_own = dt("x_own", [NOWN * 128, D])
    cofs = dt("cofs", [128, 1])
    qn_in = dt("qn", [128, 1])
    kn_in = dt("kn", [128, 1])
    subln_in = dt("subln", [1, FF])
    w_in = {}
    WEIGHTS = [
        ("wqc", D, D), ("wqr", D, H * DR), ("wkvd", D, LAT),
        ("wku", LAT, D), ("wvu", LAT, D), ("wkr", LAT, H * DR),
        ("wout", D, D), ("wg", D, FF), ("wu", D, FF), ("wd", FF, D),
    ]
    for nm, K, N in WEIGHTS:
        w_in[nm] = dt(nm + "T", [K, N])
    tern = {nm: dt("t_" + nm, [K, N], BF16, "Internal")
            for nm, K, N in WEIGHTS}
    Kc_hbm = dt("kc_hbm", [T, D], BF16, "Internal")
    Kr_hbm = dt("kr_hbm", [H, DR, T], BF16, "Internal")
    V_hbm = dt("v_hbm", [T, D], BF16, "Internal")
    Qc_hbm = dt("qc_hbm", [D, NOWN * 128], BF16, "Internal")
    Qr_hbm = dt("qr_hbm", [H, DR, NOWN * 128], BF16, "Internal")
    y_out = dt("y", [NOWN * 128, D], F32, "ExternalOutput")
    dbg = {}
    if debug:
        dbg["tern"] = dt("dbg_tern", [D, LAT], F32, "ExternalOutput")
        dbg["hws"] = dt("dbg_hws", [128, 16], F32, "ExternalOutput")
        dbg["wacc"] = dt("dbg_wacc", [128, 16], F32, "ExternalOutput")
        dbg["trig"] = dt("dbg_trig", [128, 4 * 64], F32, "ExternalOutput")
        dbg["krf"] = dt("dbg_krf", [128, H * DR], F32, "ExternalOutput")
        dbg["krb"] = dt("dbg_krb", [128, H * DR], F32, "ExternalOutput")
        dbg["wpart"] = dt("dbg_wpart", [128, 16], F32, "ExternalOutput")
        if stage >= 1:
            dbg["kc"] = dt("dbg_kc", [T, D], F32, "ExternalOutput")
            dbg["kr"] = dt("dbg_kr", [H, DR, T], F32, "ExternalOutput")
            dbg["v"] = dt("dbg_v", [T, D], F32, "ExternalOutput")
            dbg["xq"] = dt("dbg_xq", [128, D], F32, "ExternalOutput")
            dbg["latq"] = dt("dbg_latq", [128, LAT], F32, "ExternalOutput")
            dbg["kcraw"] = dt("dbg_kcraw", [128, D], F32, "ExternalOutput")
            dbg["invs"] = dt("dbg_invs", [128, 4], F32, "ExternalOutput")
        if stage >= 2:
            dbg["qc"] = dt("dbg_qc", [D, NOWN * 128], F32, "ExternalOutput")
            dbg["qr"] = dt("dbg_qr", [H, DR, NOWN * 128], F32, "ExternalOutput")
        if stage >= 3:
            dbg["ao"] = dt("dbg_ao", [128, NOWN * D], F32, "ExternalOutput")
            dbg["mneg"] = dt("dbg_mneg", [128, NOWN], F32, "ExternalOutput")
            dbg["pt"] = dt("dbg_pt", [128, 256], F32, "ExternalOutput")
            dbg["sct"] = dt("dbg_sct", [128, 256], F32, "ExternalOutput")

    WI = {nm: i for i, (nm, _, _) in enumerate(WEIGHTS)}

    with (
        nc.psum_tensor("ps0", [128, 512], F32) as ps0_t,
        nc.psum_tensor("ps1", [128, 512], F32) as ps1_t,
        nc.psum_tensor("ps2", [128, 512], F32) as ps2_t,
        nc.psum_tensor("ps3", [128, 512], F32) as ps3_t,
        nc.psum_tensor("ps4", [128, 512], F32) as ps4_t,
        nc.psum_tensor("ps5", [128, 512], F32) as ps5_t,
        nc.psum_tensor("ps6", [128, 512], F32) as ps6_t,
        nc.psum_tensor("ps7", [128, 512], F32) as ps7_t,
        nc.Block() as block,
    ):
        PS = [Buf(t.ap(), f"ps{i}") for i, t in
              enumerate([ps0_t, ps1_t, ps2_t, ps3_t, ps4_t, ps5_t, ps6_t, ps7_t])]
        PSB = [Buf(t.ap().bitcast(BF16), f"psb{i}") for i, t in
               enumerate([ps0_t, ps1_t])]
        PSB[0].res = PS[0].res
        PSB[1].res = PS[1].res

        kb = KB(nc)

        persist = Arena(nc, 44 * 1024)
        C = {}
        const_vals = dict(
            magic=MAGIC, eps5=1e-5, eps6=1e-6, c127=127.0,
            i127=1.0 / 127.0, i128=1.0 / 128.0, i2048=1.0 / 2048.0,
            i8192=1.0 / 8192.0, half=0.5, neg1=-1.0, tiny=1e-12,
            zero=0.0, one=1.0, big=3.0e38, i2pi=float(1.0 / (2 * np.pi)),
            n2pi=float(-2 * np.pi), hpi=float(np.pi / 2),
            lnb=float(-LN10K / 32.0), scl=SCALE,
            pip=3.141592502593994, pin=-3.141592502593994,
        )
        for k in const_vals:
            C[k] = persist.sub("c_" + k, [128, 1], F32)
        ident_b = persist.sub("ident_b", [128, 128], BF16)
        ident_f = persist.sub("ident_f", [128, 128], F32)
        ones_col_b = persist.sub("ones_col_b", [128, 1], BF16)
        ones_col_f = persist.sub("ones_col_f", [128, 1], F32)
        ones_row_f = persist.sub("ones_row_f", [128, 128], F32)
        Mmask = persist.sub("Mmask", [128, 32 * 128], BF16)
        g_col = persist.sub("g_col", [128, 1], F32)
        cofs_sb = persist.sub("cofs_sb", [128, 1], F32)
        hws = persist.sub("hws", [128, 16], F32)
        hneg = persist.sub("hneg", [128, 16], F32)
        alt = persist.sub("alt", [128, 64], F32)
        IF32 = persist.sub("IF32", [128, 32], F32)
        attn_out = persist.sub("attn_out", [128, NOWN * D], BF16)
        tv = persist.sub("tv", [128, 1], F32)
        th = persist.sub("th", [128, 32], F32)
        th2 = persist.sub("th2", [128, 32], F32)
        trig32 = persist.sub("trig32", [128, 32], F32)
        arena_cb = [persist.sub(f"arena_cb{i}", [128, 1], F32) for i in range(2)]

        arena = Arena(nc, 162 * 1024)

        def memset_step(buf, val, eng="vector"):
            kb.do(eng, (lambda e, a=buf.ap, v=val: e.memset(a, v)),
                  writes=[buf])

        def quant(src_buf, src_ap, xq_buf, invs_buf, invs_col, n,
                  mx, s, u_scr, extra_reads=()):
            v = kb.do
            mxa = mx.ap
            v("vector", (lambda e, sa=src_ap: e.tensor_reduce(
                mxa, sa, AX.X, ALU.max, apply_absolute_value=True)),
              reads=[src_buf, *extra_reads], writes=[mx])
            v("vector", lambda e: e.tensor_scalar(
                mxa, mxa, C["eps5"].ap, None, ALU.max),
              reads=[mx, C["eps5"]], writes=[mx])
            v("vector", (lambda e, ia=invs_buf.ap[:, invs_col:invs_col + 1]:
                         e.tensor_scalar(ia, mxa, C["i127"].ap, None, ALU.mult)),
              reads=[mx, C["i127"]], writes=[invs_buf])
            v("vector", lambda e: e.reciprocal(s.ap, mxa),
              reads=[mx], writes=[s])
            v("vector", lambda e: e.tensor_scalar(
                s.ap, s.ap, C["c127"].ap, None, ALU.mult),
              reads=[s, C["c127"]], writes=[s])
            ua = u_scr.ap[:, 0:n]
            v("scalar", (lambda e, sa=src_ap: e.activation(
                ua, sa, AF.Identity, bias=C["magic"].ap, scale=s.ap)),
              reads=[src_buf, s, C["magic"], *extra_reads], writes=[u_scr])
            v("vector", (lambda e, xa=xq_buf.ap[:, 0:n]: e.tensor_scalar(
                xa, ua, C["magic"].ap, None, ALU.subtract)),
              reads=[u_scr, C["magic"]], writes=[xq_buf])

        def transposes(src_buf, src_ap_fn, dst_buf, dst_col0, nchunks,
                       scale_col=None, scale_buf=None):
            for i in range(nchunks):
                bank = PSB[(i // 8) % 2]
                slot = i % 8
                kb.do("tensor", (lambda e, sa=src_ap_fn(i), pa=bank.ap[:, slot * 128:(slot + 1) * 128]:
                                 e.transpose(pa, sa, ident_b.ap)),
                      reads=[src_buf, ident_b], writes=[bank], tag="tr")
                if scale_col is not None:
                    sc = scale_buf.ap[:, scale_col:scale_col + 1]
                    kb.do("vector", (lambda e, pa=bank.ap[:, slot * 128:(slot + 1) * 128],
                                     da=dst_buf.ap[:, dst_col0 + i * 128: dst_col0 + (i + 1) * 128],
                                     s=sc:
                                     e.tensor_scalar(da, pa, s, None, ALU.mult)),
                          reads=[bank, scale_buf], writes=[dst_buf], tag="trev")
                else:
                    kb.do("vector", (lambda e, pa=bank.ap[:, slot * 128:(slot + 1) * 128],
                                     da=dst_buf.ap[:, dst_col0 + i * 128: dst_col0 + (i + 1) * 128]:
                                     e.tensor_copy(da, pa)),
                          reads=[bank], writes=[dst_buf], tag="trev")

        arena.reset()
        mm_f32 = arena.sub("mm_f32", [128, 32 * 128], F32)

        v = kb.do
        for k, val in const_vals.items():
            v("vector", (lambda e, a=C[k].ap, x=val: e.memset(a, x)),
              writes=[C[k]], tag="const")
        v("gpsimd", lambda e: e.memset(ident_b.ap, 1.0), writes=[ident_b])
        v("gpsimd", lambda e: e.affine_select(
            ident_b.ap, ident_b.ap, pattern=[[-1, 128]],
            compare_op=ALU.is_ge, fill=0.0, base=0, channel_multiplier=1),
          reads=[ident_b], writes=[ident_b])
        v("gpsimd", lambda e: e.affine_select(
            ident_b.ap, ident_b.ap, pattern=[[1, 128]],
            compare_op=ALU.is_ge, fill=0.0, base=0, channel_multiplier=-1),
          reads=[ident_b], writes=[ident_b])
        v("gpsimd", lambda e: e.memset(ident_f.ap, 1.0), writes=[ident_f])
        v("gpsimd", lambda e: e.affine_select(
            ident_f.ap, ident_f.ap, pattern=[[-1, 128]],
            compare_op=ALU.is_ge, fill=0.0, base=0, channel_multiplier=1),
          reads=[ident_f], writes=[ident_f])
        v("gpsimd", lambda e: e.affine_select(
            ident_f.ap, ident_f.ap, pattern=[[1, 128]],
            compare_op=ALU.is_ge, fill=0.0, base=0, channel_multiplier=-1),
          reads=[ident_f], writes=[ident_f])
        v("vector", lambda e: e.memset(ones_col_b.ap, 1.0), writes=[ones_col_b])
        v("vector", lambda e: e.memset(ones_col_f.ap, 1.0), writes=[ones_col_f])
        v("vector", lambda e: e.memset(ones_row_f.ap, 1.0), writes=[ones_row_f])
        v("sync", lambda e: e.dma_start(out=cofs_sb.ap, in_=cofs),
          writes=[cofs_sb], dma=True)
        qn_sb = persist.sub("qn_sb", [128, 1], F32)
        kn_sb = persist.sub("kn_sb", [128, 1], F32)
        v("sync", lambda e: e.dma_start(out=qn_sb.ap, in_=qn_in),
          writes=[qn_sb], dma=True)
        v("sync", lambda e: e.dma_start(out=kn_sb.ap, in_=kn_in),
          writes=[kn_sb], dma=True)
        v("vector", lambda e: e.tensor_tensor(g_col.ap, qn_sb.ap, kn_sb.ap,
                                              ALU.mult),
          reads=[qn_sb, kn_sb], writes=[g_col])
        v("vector", lambda e: e.memset(hws.ap, 0.0), writes=[hws])
        v("vector", lambda e: e.memset(hneg.ap, 0.0), writes=[hneg])
        v("gpsimd", lambda e: e.iota(
            mm_f32.ap, pattern=[[128, 32], [-1, 128]], base=0,
            channel_multiplier=1, allow_small_or_imprecise_dtypes=True),
          writes=[mm_f32])
        v("vector", lambda e: e.tensor_scalar(
            mm_f32.ap, mm_f32.ap, cofs_sb.ap, None, ALU.subtract),
          reads=[mm_f32, cofs_sb], writes=[mm_f32])
        v("vector", lambda e: e.tensor_scalar(
            Mmask.ap, mm_f32.ap, C["zero"].ap, None, ALU.is_le),
          reads=[mm_f32, C["zero"]], writes=[Mmask])
        v("gpsimd", lambda e: e.iota(
            alt.ap, pattern=[[32, 2], [0, 32]], base=0, channel_multiplier=0,
            allow_small_or_imprecise_dtypes=True),
          writes=[alt])
        v("gpsimd", lambda e: e.iota(
            alt.ap, pattern=[[1, 64]], base=0, channel_multiplier=0,
            allow_small_or_imprecise_dtypes=True),
          writes=[alt])
        alt_h = persist.sub("alt_h", [128, 64], F32)
        v("vector", lambda e: e.tensor_scalar(
            alt_h.ap, alt.ap, C["half"].ap, C["magic"].ap, ALU.mult, ALU.add),
          reads=[alt, C["half"], C["magic"]], writes=[alt_h])
        v("vector", lambda e: e.tensor_scalar(
            alt_h.ap, alt_h.ap, C["magic"].ap, None, ALU.subtract),
          reads=[alt_h, C["magic"]], writes=[alt_h])
        v("vector", lambda e: e.tensor_scalar(
            alt_h.ap, alt.ap, C["half"].ap, None, ALU.mult),
          reads=[alt, C["half"]], writes=[alt_h])
        qtr = persist.sub("qtr", [128, 1], F32)
        v("vector", lambda e: e.memset(qtr.ap, 0.25), writes=[qtr])
        v("vector", lambda e: e.tensor_scalar(
            alt_h.ap, alt_h.ap, qtr.ap, C["magic"].ap, ALU.subtract, ALU.add),
          reads=[alt_h, qtr, C["magic"]], writes=[alt_h])
        v("vector", lambda e: e.tensor_scalar(
            alt_h.ap, alt_h.ap, C["magic"].ap, None, ALU.subtract),
          reads=[alt_h, C["magic"]], writes=[alt_h])
        two = persist.sub("two", [128, 1], F32)
        v("vector", lambda e: e.memset(two.ap, 2.0), writes=[two])
        v("vector", lambda e: e.tensor_scalar(
            alt_h.ap, alt_h.ap, two.ap, None, ALU.mult),
          reads=[alt_h, two], writes=[alt_h])
        v("vector", lambda e: e.tensor_tensor(
            alt.ap, alt.ap, alt_h.ap, ALU.subtract),
          reads=[alt, alt_h], writes=[alt])
        v("vector", lambda e: e.tensor_scalar(
            alt.ap, alt.ap, two.ap, C["one"].ap, ALU.mult, ALU.subtract),
          reads=[alt, two, C["one"]], writes=[alt])
        v("gpsimd", lambda e: e.iota(
            IF32.ap, pattern=[[1, 32]], base=0, channel_multiplier=0,
            allow_small_or_imprecise_dtypes=True),
          writes=[IF32])
        v("scalar", lambda e: e.activation(
            IF32.ap, IF32.ap, AF.Exp, scale=C["lnb"].ap),
          reads=[IF32, C["lnb"]], writes=[IF32])

        wtile = [arena.sub(f"wtile{i}", [128, 2048], F32) for i in range(2)]
        wout_b = [arena.sub(f"wtern{i}", [128, 2048], BF16) for i in range(2)]
        sg1 = arena.sub("sg1", [128, 2048], BF16)
        wacc = arena.sub("wacc", [128, 1], F32)
        wpart = arena.sub("wpart", [128, 1], F32)
        wsc = arena.sub("wsc", [1, 1], F32)

        def w_tiles(nm):
            K, N = [(k, n) for w, k, n in WEIGHTS if w == nm][0]
            ncol = min(N, 2048)
            out = []
            for kc in range(K // 128):
                for ncc in range(N // ncol):
                    out.append((kc, ncc, ncol))
            return out, K, N

        def w_pipeline(weights, wtile, wout_b, sg1, wacc, wpart, wsc, ar, wbank):
            for nm, K, N in weights:
                tiles, _, _ = w_tiles(nm)
                memset_step(wacc, 0.0)
                wi = WI[nm]
                for i, (kc, ncc, ncol) in enumerate(tiles):
                    b = wtile[i % 2]
                    srcw = w_in[nm][kc * 128:(kc + 1) * 128,
                                    ncc * ncol:(ncc + 1) * ncol]
                    kb.do("sync", (lambda e, d=b.ap[:, 0:ncol], s=srcw:
                                   e.dma_start(out=d, in_=s)),
                          writes=[b], dma=True, tag=f"w1ld_{nm}")
                    kb.do("vector", (lambda e, s=b.ap[:, 0:ncol]:
                                     e.tensor_reduce(wpart.ap, s, AX.X, ALU.add,
                                                     apply_absolute_value=True)),
                          reads=[b], writes=[wpart])
                    kb.do("vector", lambda e: e.tensor_tensor(
                        wacc.ap, wacc.ap, wpart.ap, ALU.add),
                        reads=[wacc, wpart], writes=[wacc])
                    yield
                kb.do("tensor", (lambda e, pb=wbank.ap: e.matmul(
                    pb[0:1, 0:1], ones_col_f.ap, wacc.ap,
                    start=True, stop=True)),
                    reads=[ones_col_f, wacc], writes=[wbank])
                inv_numel = float(np.float32(1.0 / (K * N)))
                cnum = ar.sub(f"cnum_{nm}", [1, 1], F32)
                kb.do("vector", (lambda e, a2=cnum.ap, x=inv_numel: e.memset(a2, x)),
                      writes=[cnum])
                kb.do("vector", (lambda e, cn=cnum.ap[0:1, :], pb=wbank.ap:
                                 e.tensor_scalar(
                    wsc.ap, pb[0:1, 0:1], cn, None, ALU.mult)),
                    reads=[wbank, cnum], writes=[wsc])
                kb.do("vector", lambda e: e.tensor_scalar(
                    wsc.ap, wsc.ap, C["eps5"].ap[0:1, :], None, ALU.max),
                    reads=[wsc, C["eps5"]], writes=[wsc])
                kb.do("vector", lambda e: e.tensor_scalar(
                    wsc.ap, wsc.ap, C["half"].ap[0:1, :], None, ALU.mult),
                    reads=[wsc, C["half"]], writes=[wsc])
                kb.do("tensor", (lambda e, pb=wbank.ap: e.matmul(
                    pb[:, 0:1], ones_row_f.ap[0:1, :], wsc.ap,
                    start=True, stop=True)),
                    reads=[ones_row_f, wsc], writes=[wbank])
                kb.do("vector", (lambda e, d=hws.ap[:, wi:wi + 1],
                                 pb=wbank.ap:
                                 e.tensor_copy(d, pb[:, 0:1])),
                      reads=[wbank], writes=[hws])
                kb.do("vector", (lambda e, d=hneg.ap[:, wi:wi + 1],
                                 s=hws.ap[:, wi:wi + 1]:
                                 e.tensor_scalar(d, s, C["neg1"].ap, None, ALU.mult)),
                      reads=[hws, C["neg1"]], writes=[hneg])
                yield
                for i, (kc, ncc, ncol) in enumerate(tiles):
                    b = wtile[i % 2]
                    ob = wout_b[i % 2]
                    srcw = w_in[nm][kc * 128:(kc + 1) * 128,
                                    ncc * ncol:(ncc + 1) * ncol]
                    dst = tern[nm][kc * 128:(kc + 1) * 128,
                                   ncc * ncol:(ncc + 1) * ncol]
                    kb.do("sync", (lambda e, d=b.ap[:, 0:ncol], s=srcw:
                                   e.dma_start(out=d, in_=s)),
                          writes=[b], dma=True, tag=f"w2ld_{nm}")
                    kb.do("scalar", (lambda e, s=b.ap[:, 0:ncol],
                                     d=sg1.ap[:, 0:ncol],
                                     hn=hneg.ap[:, wi:wi + 1]:
                                     e.activation(d, s, AF.Sign, bias=hn)),
                          reads=[b, hneg], writes=[sg1])
                    kb.do("scalar", (lambda e, s=b.ap[:, 0:ncol],
                                     d=ob.ap[:, 0:ncol],
                                     hp=hws.ap[:, wi:wi + 1]:
                                     e.activation(d, s, AF.Sign, bias=hp)),
                          reads=[b, hws], writes=[ob])
                    kb.do("vector", (lambda e, d=ob.ap[:, 0:ncol],
                                     s1=sg1.ap[:, 0:ncol]:
                                     e.tensor_tensor(d, d, s1, ALU.add)),
                          reads=[ob, sg1], writes=[ob])
                    kb.do("gpsimd", (lambda e, s=ob.ap[:, 0:ncol], d=dst:
                                     e.dma_start(out=d, in_=s)),
                          reads=[ob], dma=True, tag=f"w2st_{nm}")
                    yield

        W_EARLY = [w for w in WEIGHTS if w[0] in
                   ("wqc", "wqr", "wkvd", "wku", "wvu", "wkr")]
        W_LATE = [w for w in WEIGHTS if w[0] in ("wout", "wg", "wu", "wd")]
        for _ in w_pipeline(W_EARLY, wtile, wout_b, sg1, wacc, wpart, wsc,
                            arena, PS[6]):
            pass

        if debug:
            kb.do("gpsimd", (lambda e: e.dma_start(out=dbg["hws"], in_=hws.ap)),
                  reads=[hws], dma=True)
            for ch in range(16):
                bb = wout_b[0]
                kb.do("sync", (lambda e, d=bb.ap[:, 0:512],
                               s=tern["wkvd"][ch * 128:(ch + 1) * 128, :]:
                               e.dma_start(out=d, in_=s)),
                      writes=[bb], dma=True)
                kb.do("gpsimd", (lambda e, s=bb.ap[:, 0:512],
                                 d=dbg["tern"][ch * 128:(ch + 1) * 128, :]:
                                 e.dma_start(out=d, in_=s)),
                      reads=[bb], dma=True)

        kb.barrier()

        arena.reset()
        wkvd_sb = arena.sub("wkvd_sb", [128, 16 * LAT], BF16)
        wku_sb = arena.sub("wku_sb", [128, 4 * D], BF16)
        wvu_sb = arena.sub("wvu_sb", [128, 4 * D], BF16)
        wkr_sb = arena.sub("wkr_sb", [128, 4 * H * DR], BF16)
        cosT = arena.sub("cosT", [128, NT * 32], BF16)
        sinT = arena.sub("sinT", [128, NT * 32], BF16)
        xtile = [arena.sub(f"xtile{i}", [128, D], F32) for i in range(2)]
        u_scr2 = [arena.sub(f"u_scr{i}", [128, D], F32) for i in range(2)]
        xq = [arena.sub(f"xq{i}", [128, D], BF16) for i in range(2)]
        xqT = [arena.sub(f"xqT{i}", [128, D], BF16) for i in range(2)]
        latq = [arena.sub(f"latq{i}", [128, LAT], BF16) for i in range(2)]
        latqT = [arena.sub(f"latqT{i}", [128, LAT], BF16) for i in range(2)]
        kcraw = [arena.sub(f"kcraw{i}", [128, D], BF16) for i in range(2)]
        krf2 = [arena.sub(f"krf{i}", [128, H * DR], F32) for i in range(2)]
        krb = [arena.sub(f"krb{i}", [128, H * DR], BF16) for i in range(2)]
        krTs = [arena.sub("krTs0", [128, H * DR], BF16)] * 2
        kcs = [arena.sub(f"kcs{i}", [128, D], BF16) for i in range(2)]
        vst = [arena.sub(f"vst{i}", [128, D], BF16) for i in range(2)]
        sc_mx = [arena.sub(f"sc_mx{i}", [128, 1], F32) for i in range(2)]
        sc_s = [arena.sub(f"sc_s{i}", [128, 1], F32) for i in range(2)]
        invsx = [arena.sub(f"invsx{i}", [128, 1], F32) for i in range(2)]
        invsl = [arena.sub(f"invsl{i}", [128, 1], F32) for i in range(2)]
        dq_sc = [arena.sub(f"dq_sc{i}", [128, 4], F32) for i in range(2)]
        ss16_2 = [arena.sub(f"ss16_{i}", [128, 16], F32) for i in range(2)]
        cs16_2 = [arena.sub(f"cs16_{i}", [128, 16], F32) for i in range(2)]

        def emit_trig(kt_or_none, tile_j, cos_dst, sin_dst, pos_reads=()):
            v = kb.do
            if kt_or_none is not None:
                base = float(128 * kt_or_none)
                v("gpsimd", (lambda e, b=base: e.iota(
                    tv.ap, pattern=[[0, 1]], base=0, channel_multiplier=1,
                    allow_small_or_imprecise_dtypes=True)),
                  writes=[tv])
                cb = arena_cb[kt_or_none % 2]
                v("vector", (lambda e, b=base: e.memset(cb.ap, b)), writes=[cb])
                v("vector", (lambda e, cba=cb.ap: e.tensor_scalar(
                    tv.ap, tv.ap, cba, None, ALU.add)),
                  reads=[tv, cb], writes=[tv])
            else:
                base = float(1024 * tile_j)
                v("gpsimd", (lambda e: e.iota(
                    tv.ap, pattern=[[0, 1]], base=0, channel_multiplier=1,
                    allow_small_or_imprecise_dtypes=True)),
                  writes=[tv])
                cb = arena_cb[0]
                v("vector", (lambda e, b=base: e.memset(cb.ap, b)), writes=[cb])
                v("vector", (lambda e, cba=cb.ap: e.tensor_scalar(
                    tv.ap, tv.ap, cba, None, ALU.add)),
                  reads=[tv, cb], writes=[tv])
                v("vector", lambda e: e.tensor_scalar(
                    tv.ap, tv.ap, cofs_sb.ap, None, ALU.add),
                  reads=[tv, cofs_sb], writes=[tv])
            v("vector", lambda e: e.tensor_scalar(
                th.ap, IF32.ap, tv.ap, None, ALU.mult),
              reads=[IF32, tv], writes=[th])
            v("scalar", lambda e: e.activation(
                th2.ap, th.ap, AF.Identity, bias=C["magic"].ap,
                scale=C["i2pi"].ap),
              reads=[th, C["magic"], C["i2pi"]], writes=[th2])
            v("vector", lambda e: e.tensor_scalar(
                th2.ap, th2.ap, C["magic"].ap, None, ALU.subtract),
              reads=[th2, C["magic"]], writes=[th2])
            v("vector", lambda e: e.scalar_tensor_tensor(
                th2.ap, th2.ap, C["n2pi"].ap, th.ap, ALU.mult, ALU.add),
              reads=[th2, C["n2pi"], th], writes=[th2])
            v("vector", lambda e: e.tensor_scalar(
                th2.ap, th2.ap, C["pip"].ap, C["pin"].ap, ALU.min, ALU.max),
              reads=[th2, C["pip"], C["pin"]], writes=[th2])
            v("scalar", lambda e: e.activation(trig32.ap, th2.ap, AF.Sin),
              reads=[th2], writes=[trig32])
            v("vector", (lambda e, d=sin_dst.ap[:, 0:32]:
                         e.tensor_tensor(d, trig32.ap, alt.ap[:, 0:32],
                                         ALU.mult)),
              reads=[trig32, alt], writes=[sin_dst])
            v("scalar", lambda e: e.activation(th.ap, th2.ap, AF.Abs),
              reads=[th2], writes=[th])
            v("scalar", lambda e: e.activation(
                trig32.ap, th.ap, AF.Sin, bias=C["hpi"].ap, scale=C["neg1"].ap),
              reads=[th, C["hpi"], C["neg1"]], writes=[trig32])
            v("vector", (lambda e, d=cos_dst.ap[:, 0:32]:
                         e.tensor_copy(d, trig32.ap)),
              reads=[trig32], writes=[cos_dst])


        class TileView(Buf):
            pass

        cosT_t = [TileView(cosT.ap[:, kt * 32:(kt + 1) * 32]) for kt in range(NT)]
        sinT_t = [TileView(sinT.ap[:, kt * 32:(kt + 1) * 32]) for kt in range(NT)]
        for b in cosT_t:
            b.res = cosT.res
        for b in sinT_t:
            b.res = sinT.res

        if stage >= 1:
            kb.do("sync", lambda e: e.dma_start(
                out=wkvd_sb.ap.rearrange("p (a n) -> p a n", a=16),
                in_=tern["wkvd"].rearrange("(a p) n -> p a n", p=128)),
                writes=[wkvd_sb], dma=True)
            kb.do("sync", lambda e: e.dma_start(
                out=wku_sb.ap.rearrange("p (a n) -> p a n", a=4),
                in_=tern["wku"].rearrange("(a p) n -> p a n", p=128)),
                writes=[wku_sb], dma=True)
            kb.do("sync", lambda e: e.dma_start(
                out=wvu_sb.ap.rearrange("p (a n) -> p a n", a=4),
                in_=tern["wvu"].rearrange("(a p) n -> p a n", p=128)),
                writes=[wvu_sb], dma=True)
            kb.do("sync", lambda e: e.dma_start(
                out=wkr_sb.ap.rearrange("p (a n) -> p a n", a=4),
                in_=tern["wkr"].rearrange("(a p) n -> p a n", p=128)),
                writes=[wkr_sb], dma=True)
            for kt in range(NT):
                emit_trig(kt, None, cosT_t[kt], sinT_t[kt])

            def emit_A_h1(kt):
                par = kt % 2
                u_scr = u_scr2[par]
                krf = krf2[par]
                ss16 = ss16_2[par]
                cs16 = cs16_2[par]
                ropa = u_scr
                t1 = ropa.ap[:, 0:H * DR]
                t2 = ropa.ap[:, H * DR:2 * H * DR]
                xb, xqb, xqTb = xtile[par], xq[par], xqT[par]
                kb.do("sync", (lambda e, d=xb.ap,
                               s=x_full[kt * 128:(kt + 1) * 128, :]:
                               e.dma_start(out=d, in_=s)),
                      writes=[xb], dma=True, tag=f"A_x{kt}")
                quant(xb, xb.ap, xqb, invsx[par], 0, D,
                      sc_mx[par], sc_s[par], u_scr)
                if debug and kt == 0:
                    kb.do("gpsimd", (lambda e, s=xqb.ap, d=dbg["xq"]:
                                     e.dma_start(out=d, in_=s)),
                          reads=[xqb], dma=True)
                transposes(xqb, (lambda i, b=xqb: b.ap[:, i * 128:(i + 1) * 128]),
                           xqTb, 0, 16)
                lb = PS[2 + par]
                for ic in range(16):
                    kb.do("tensor", (lambda e, l=xqTb.ap[:, ic * 128:(ic + 1) * 128],
                                     r=wkvd_sb.ap[:, ic * 512:(ic + 1) * 512],
                                     st=(ic == 0), sp=(ic == 15), pb=lb.ap:
                                     e.matmul(pb, l, r, start=st, stop=sp)),
                          reads=[xqTb, wkvd_sb], writes=[lb], tag="A_lat")
                dqs = dq_sc[par]
                kb.do("vector", (lambda e, d=dqs.ap[:, 0:1],
                                 h=hws.ap[:, WI["wkvd"]:WI["wkvd"] + 1],
                                 ix=invsx[par].ap:
                                 e.tensor_scalar(d, h, ix, None, ALU.mult)),
                      reads=[hws, invsx[par]], writes=[dqs])
                mxb, sb = sc_mx[par], sc_s[par]
                kb.do("vector", (lambda e, pb=lb.ap: e.tensor_reduce(
                    mxb.ap, pb, AX.X, ALU.max, apply_absolute_value=True)),
                      reads=[lb], writes=[mxb])
                kb.do("vector", (lambda e, d=mxb.ap, q=dqs.ap[:, 0:1]:
                                 e.tensor_scalar(d, d, q, None, ALU.mult)),
                      reads=[mxb, dqs], writes=[mxb])
                kb.do("vector", lambda e, d=mxb.ap: e.tensor_scalar(
                    d, d, C["eps5"].ap, None, ALU.max),
                    reads=[mxb, C["eps5"]], writes=[mxb])
                kb.do("vector", (lambda e, d=invsl[par].ap, m=mxb.ap:
                                 e.tensor_scalar(d, m, C["i127"].ap, None, ALU.mult)),
                      reads=[mxb, C["i127"]], writes=[invsl[par]])
                kb.do("vector", (lambda e, d=sb.ap, m=mxb.ap:
                                 e.reciprocal(d, m)),
                      reads=[mxb], writes=[sb])
                kb.do("vector", (lambda e, d=sb.ap:
                                 e.tensor_scalar(d, d, C["c127"].ap, None, ALU.mult)),
                      reads=[sb, C["c127"]], writes=[sb])
                kb.do("vector", (lambda e, d=sb.ap, q=dqs.ap[:, 0:1]:
                                 e.tensor_scalar(d, d, q, None, ALU.mult)),
                      reads=[sb, dqs], writes=[sb])
                kb.do("scalar", (lambda e, pb=lb.ap, d=u_scr.ap[:, 0:512],
                                 s=sb.ap:
                                 e.activation(d, pb, AF.Identity,
                                              bias=C["magic"].ap, scale=s)),
                      reads=[lb, sb, C["magic"]], writes=[u_scr])
                kb.do("vector", (lambda e, d=latq[par].ap, s=u_scr.ap[:, 0:512]:
                                 e.tensor_scalar(d, s, C["magic"].ap, None,
                                                 ALU.subtract)),
                      reads=[u_scr, C["magic"]], writes=[latq[par]])
                if debug and kt == 0:
                    kb.do("gpsimd", (lambda e, s=latq[par].ap, d=dbg["latq"]:
                                     e.dma_start(out=d, in_=s)),
                          reads=[latq[par]], dma=True)
                    kb.do("gpsimd", (lambda e, s=invsx[par].ap,
                                     d=dbg["invs"][:, 0:1]:
                                     e.dma_start(out=d, in_=s,
                                                 allow_slow_non_contiguous=True)),
                          reads=[invsx[par]], dma=True)
                    kb.do("gpsimd", (lambda e, s=invsl[par].ap,
                                     d=dbg["invs"][:, 1:2]:
                                     e.dma_start(out=d, in_=s,
                                                 allow_slow_non_contiguous=True)),
                          reads=[invsl[par]], dma=True)
                transposes(latq[par],
                           (lambda i, b=latq[par]: b.ap[:, i * 128:(i + 1) * 128]),
                           latqT[par], 0, 4)

            def emit_A_h2(kt):
                par = kt % 2
                u_scr = u_scr2[par]
                krf = krf2[par]
                ss16 = ss16_2[par]
                cs16 = cs16_2[par]
                ropa = u_scr
                t1 = ropa.ap[:, 0:H * DR]
                t2 = ropa.ap[:, H * DR:2 * H * DR]
                lb = PS[2 + par]
                dqs = dq_sc[par]
                for wi_name, col in (("wku", 1), ("wvu", 2), ("wkr", 3)):
                    kb.do("vector", (lambda e, d=dqs.ap[:, col:col + 1],
                                     h=hws.ap[:, WI[wi_name]:WI[wi_name] + 1],
                                     il=invsl[par].ap:
                                     e.tensor_scalar(d, h, il, None, ALU.mult)),
                          reads=[hws, invsl[par]], writes=[dqs])
                for oc in range(4):
                    bank = PS[4 + (oc % 2)]
                    for lc in range(4):
                        kb.do("tensor", (lambda e,
                                         l=latqT[par].ap[:, lc * 128:(lc + 1) * 128],
                                         r=wku_sb.ap[:, lc * D + oc * 512: lc * D + (oc + 1) * 512],
                                         st=(lc == 0), sp=(lc == 3), pb=bank.ap:
                                         e.matmul(pb, l, r, start=st, stop=sp)),
                              reads=[latqT[par], wku_sb], writes=[bank], tag="A_kc")
                    kb.do("vector", (lambda e, pb=bank.ap,
                                     d=kcraw[par].ap[:, oc * 512:(oc + 1) * 512]:
                                     e.tensor_copy(d, pb)),
                          reads=[bank], writes=[kcraw[par]])
                if debug and kt == 0:
                    kb.do("gpsimd", (lambda e, s=kcraw[par].ap, d=dbg["kcraw"]:
                                     e.dma_start(out=d, in_=s)),
                          reads=[kcraw[par]], dma=True)
                kb.do("scalar", (lambda e, s=kcraw[par].ap, d=u_scr.ap:
                                 e.activation(d, s, AF.Square)),
                      reads=[kcraw[par]], writes=[u_scr])
                kb.do("vector", (lambda e, d=ss16.ap,
                                 s=u_scr.ap.rearrange("p (h d) -> p h d", h=16):
                                 e.tensor_reduce(d, s, AX.X, ALU.add)),
                      reads=[u_scr], writes=[ss16])
                kb.do("vector", (lambda e, q=dqs.ap[:, 1:2], d=dqs.ap[:, 0:1]:
                                 e.tensor_scalar(d, q, q, None, ALU.mult)),
                      reads=[dqs], writes=[dqs])
                kb.do("vector", (lambda e, q=dqs.ap[:, 0:1], d=ss16.ap:
                                 e.tensor_scalar(d, d, q, None, ALU.mult)),
                      reads=[ss16, dqs], writes=[ss16])
                kb.do("scalar", (lambda e, d=ss16.ap: e.activation(
                    d, d, AF.Sqrt, bias=C["eps6"].ap, scale=C["i128"].ap)),
                    reads=[ss16, C["eps6"], C["i128"]], writes=[ss16])
                kb.do("vector", (lambda e, d=cs16.ap, s=ss16.ap:
                                 e.reciprocal(d, s)),
                      reads=[ss16], writes=[cs16])
                kb.do("vector", (lambda e, q=dqs.ap[:, 1:2], d=cs16.ap:
                                 e.tensor_scalar(d, d, q, None, ALU.mult)),
                      reads=[cs16, dqs], writes=[cs16])
                for h in range(16):
                    kb.do("vector", (lambda e,
                                     s=kcraw[par].ap[:, h * 128:(h + 1) * 128],
                                     d=kcs[par].ap[:, h * 128:(h + 1) * 128],
                                     c=cs16.ap[:, h:h + 1]:
                                     e.tensor_scalar(d, s, c, None, ALU.mult)),
                          reads=[kcraw[par], cs16], writes=[kcs[par]])
                kb.do("gpsimd", (lambda e, s=kcs[par].ap,
                                 d=Kc_hbm[kt * 128:(kt + 1) * 128, :]:
                                 e.dma_start(out=d, in_=s)),
                      reads=[kcs[par]], dma=True, tag=f"A_kcst{kt}")
                if debug:
                    kb.do("gpsimd", (lambda e, s=kcs[par].ap,
                                     d=dbg["kc"][kt * 128:(kt + 1) * 128, :]:
                                     e.dma_start(out=d, in_=s)),
                          reads=[kcs[par]], dma=True)
                for oc in range(4):
                    bank = PS[4 + (oc % 2)]
                    for lc in range(4):
                        kb.do("tensor", (lambda e,
                                         l=latqT[par].ap[:, lc * 128:(lc + 1) * 128],
                                         r=wvu_sb.ap[:, lc * D + oc * 512: lc * D + (oc + 1) * 512],
                                         st=(lc == 0), sp=(lc == 3), pb=bank.ap:
                                         e.matmul(pb, l, r, start=st, stop=sp)),
                              reads=[latqT[par], wvu_sb], writes=[bank], tag="A_v")
                    kb.do("vector", (lambda e, pb=bank.ap,
                                     d=vst[par].ap[:, oc * 512:(oc + 1) * 512],
                                     q=dq_sc[par].ap[:, 2:3]:
                                     e.tensor_scalar(d, pb, q, None, ALU.mult)),
                          reads=[bank, dq_sc[par]], writes=[vst[par]])
                kb.do("gpsimd", (lambda e, s=vst[par].ap,
                                 d=V_hbm[kt * 128:(kt + 1) * 128, :]:
                                 e.dma_start(out=d, in_=s)),
                      reads=[vst[par]], dma=True, tag=f"A_vst{kt}")
                if debug:
                    kb.do("gpsimd", (lambda e, s=vst[par].ap,
                                     d=dbg["v"][kt * 128:(kt + 1) * 128, :]:
                                     e.dma_start(out=d, in_=s)),
                          reads=[vst[par]], dma=True)
                for oc in range(2):
                    bank = PS[4 + (oc % 2)]
                    for lc in range(4):
                        kb.do("tensor", (lambda e,
                                         l=latqT[par].ap[:, lc * 128:(lc + 1) * 128],
                                         r=wkr_sb.ap[:, lc * (H * DR) + oc * 512: lc * (H * DR) + (oc + 1) * 512],
                                         st=(lc == 0), sp=(lc == 3), pb=bank.ap:
                                         e.matmul(pb, l, r, start=st, stop=sp)),
                              reads=[latqT[par], wkr_sb], writes=[bank], tag="A_kr")
                    kb.do("vector", (lambda e, pb=bank.ap,
                                     d=krf.ap[:, oc * 512:(oc + 1) * 512],
                                     q=dq_sc[par].ap[:, 3:4]:
                                     e.tensor_scalar(d, pb, q, None, ALU.mult)),
                          reads=[bank, dq_sc[par]], writes=[krf])
                c3 = cosT_t[kt].ap.unsqueeze(1).unsqueeze(1).broadcast_to(
                    [128, 16, 2, 32])
                s3 = sinT_t[kt].ap.unsqueeze(1).unsqueeze(1).broadcast_to(
                    [128, 16, 2, 32])
                kr3 = krf.ap.rearrange("p (h r d) -> p h r d", h=16, r=2)
                krsw = krf.ap.rearrange("p (a b) -> p a b", b=2)[:, :, ::-1]
                kb.do("vector", (lambda e, tt=t1, aa=kr3, cc=c3: e.tensor_tensor(
                    tt.rearrange("p (h r d) -> p h r d", h=16, r=2), aa, cc,
                    ALU.mult)),
                      reads=[krf, cosT], writes=[ropa])
                kb.do("vector", (lambda e, tt=t2, aa=krsw, ss=s3: e.tensor_tensor(
                    tt.rearrange("p (h r d) -> p h r d", h=16, r=2), aa, ss,
                    ALU.mult)),
                      reads=[krf, sinT], writes=[ropa])
                kb.do("vector", (lambda e, d=krb[par].ap, a=t1, b=t2:
                                 e.tensor_tensor(d, a, b, ALU.add)),
                      reads=[ropa], writes=[krb[par]])
                if debug and kt == 2:
                    kb.do("gpsimd", (lambda e: e.dma_start(
                        out=dbg["trig"][:, 0:32], in_=cosT_t[2].ap)),
                          reads=[cosT], dma=True)
                    kb.do("gpsimd", (lambda e: e.dma_start(
                        out=dbg["trig"][:, 64:96], in_=sinT_t[2].ap)),
                          reads=[sinT], dma=True)
                    kb.do("gpsimd", (lambda e: e.dma_start(
                        out=dbg["trig"][:, 128:192], in_=alt.ap)),
                          reads=[alt], dma=True)
                    kb.do("gpsimd", (lambda e, s=krf.ap: e.dma_start(
                        out=dbg["krf"], in_=s)),
                          reads=[krf], dma=True)
                    kb.do("gpsimd", (lambda e, s=krb[par].ap: e.dma_start(
                        out=dbg["krb"], in_=s)),
                          reads=[krb[par]], dma=True)
                transposes(krb[par],
                           (lambda i, b=krb[par]: b.ap[:, i * 128:(i + 1) * 128]),
                           krTs[par], 0, 8)
                kb.do("gpsimd", (lambda e,
                                 s=krTs[par].ap.rearrange("p (a n) -> p a n", a=8),
                                 d=Kr_hbm.rearrange("h d t -> (h d) t")[
                                     :, kt * 128:(kt + 1) * 128].rearrange(
                                     "(a p) n -> p a n", p=128):
                                 e.dma_start(out=d, in_=s)),
                      reads=[krTs[par]], dma=True, tag=f"A_krst{kt}")
                if debug:
                    kb.do("gpsimd", (lambda e,
                                     s=krTs[par].ap.rearrange("p (a n) -> p a n", a=8),
                                     d=dbg["kr"].rearrange("h d t -> (h d) t")[
                                         :, kt * 128:(kt + 1) * 128].rearrange(
                                         "(a p) n -> p a n", p=128):
                                     e.dma_start(out=d, in_=s)),
                          reads=[krTs[par]], dma=True)


            for kt in range(NT + 1):
                if kt < NT:
                    emit_A_h1(kt)
                if kt >= 1:
                    emit_A_h2(kt - 1)
        kb.barrier()

        arena.reset()
        xqT_all = arena.sub("xqT_all", [128, NOWN * D], BF16)
        qcraw = arena.sub("qcraw", [128, NOWN * D], BF16)
        qrraw = arena.sub("qrraw", [128, NOWN * H * DR], BF16)
        wch = [arena.sub(f"wch{i}", [128, 16 * 512], BF16) for i in range(2)]
        xtileB = arena.sub("xtileB", [128, D], F32)
        u_scrB = arena.sub("u_scrB", [128, D], F32)
        xqB = arena.sub("xqB", [128, D], BF16)
        invsxB = arena.sub("invsxB", [128, NOWN], F32)
        mxB = arena.sub("mxB", [128, 1], F32)
        sB = arena.sub("sB", [128, 1], F32)
        ss16B = arena.sub("ss16B", [128, 16], F32)
        cs16B = arena.sub("cs16B", [128, 16], F32)
        dqB = arena.sub("dqB", [128, 2], F32)
        qrf = arena.sub("qrf", [128, H * DR], F32)
        ropaB = arena.sub("ropaB", [128, 2 * H * DR], F32)
        qrbB = arena.sub("qrbB", [128, H * DR], BF16)
        qstg = arena.sub("qstg", [128, D], BF16)
        qrstg = arena.sub("qrstg", [128, H * DR], BF16)
        cosB = arena.sub("cosB", [128, 32], F32)
        sinB = arena.sub("sinB", [128, 32], F32)
        tvB = arena.sub("tvB", [128, 1], F32)

        if stage >= 2:
            for j in range(NOWN):
                kb.do("sync", (lambda e, d=xtileB.ap,
                               s=x_own[j * 128:(j + 1) * 128, :]:
                               e.dma_start(out=d, in_=s)),
                      writes=[xtileB], dma=True, tag=f"B_x{j}")
                quant(xtileB, xtileB.ap, xqB, invsxB, j, D, mxB, sB, u_scrB)
                transposes(xqB, (lambda i, b=xqB: b.ap[:, i * 128:(i + 1) * 128]),
                           xqT_all, j * D, 16)
            for oc in range(4):
                wb = wch[oc % 2]
                kb.do("sync", (lambda e, d=wb.ap.rearrange("p (a n) -> p a n", a=16),
                               s=tern["wqc"][:, oc * 512:(oc + 1) * 512].rearrange(
                                   "(a p) n -> p a n", p=128):
                               e.dma_start(out=d, in_=s)),
                      writes=[wb], dma=True, tag=f"B_wqc{oc}")
                for j in range(NOWN):
                    bank = PS[2 + (j % 2)]
                    for ic in range(16):
                        kb.do("tensor", (lambda e,
                                         l=xqT_all.ap[:, j * D + ic * 128: j * D + (ic + 1) * 128],
                                         r=wb.ap[:, ic * 512:(ic + 1) * 512],
                                         st=(ic == 0), sp=(ic == 15), pb=bank.ap:
                                         e.matmul(pb, l, r, start=st, stop=sp)),
                              reads=[xqT_all, wb], writes=[bank], tag="B_qc")
                    kb.do("vector", (lambda e, pb=bank.ap,
                                     d=qcraw.ap[:, j * D + oc * 512: j * D + (oc + 1) * 512]:
                                     e.tensor_copy(d, pb)),
                          reads=[bank], writes=[qcraw])
            for oc in range(2):
                wb = wch[oc % 2]
                kb.do("sync", (lambda e, d=wb.ap.rearrange("p (a n) -> p a n", a=16),
                               s=tern["wqr"][:, oc * 512:(oc + 1) * 512].rearrange(
                                   "(a p) n -> p a n", p=128):
                               e.dma_start(out=d, in_=s)),
                      writes=[wb], dma=True, tag=f"B_wqr{oc}")
                for j in range(NOWN):
                    bank = PS[2 + (j % 2)]
                    for ic in range(16):
                        kb.do("tensor", (lambda e,
                                         l=xqT_all.ap[:, j * D + ic * 128: j * D + (ic + 1) * 128],
                                         r=wb.ap[:, ic * 512:(ic + 1) * 512],
                                         st=(ic == 0), sp=(ic == 15), pb=bank.ap:
                                         e.matmul(pb, l, r, start=st, stop=sp)),
                              reads=[xqT_all, wb], writes=[bank], tag="B_qr")
                    kb.do("vector", (lambda e, pb=bank.ap,
                                     d=qrraw.ap[:, j * (H * DR) + oc * 512: j * (H * DR) + (oc + 1) * 512]:
                                     e.tensor_copy(d, pb)),
                          reads=[bank], writes=[qrraw])
            for j in range(NOWN):
                kb.do("vector", (lambda e, d=dqB.ap[:, 0:1],
                                 h=hws.ap[:, WI["wqc"]:WI["wqc"] + 1],
                                 ix=invsxB.ap[:, j:j + 1]:
                                 e.tensor_scalar(d, h, ix, None, ALU.mult)),
                      reads=[hws, invsxB], writes=[dqB])
                kb.do("vector", (lambda e, d=dqB.ap[:, 1:2],
                                 h=hws.ap[:, WI["wqr"]:WI["wqr"] + 1],
                                 ix=invsxB.ap[:, j:j + 1]:
                                 e.tensor_scalar(d, h, ix, None, ALU.mult)),
                      reads=[hws, invsxB], writes=[dqB])
                qc_j = qcraw.ap[:, j * D:(j + 1) * D]
                kb.do("scalar", (lambda e, s=qc_j, d=u_scrB.ap:
                                 e.activation(d, s, AF.Square)),
                      reads=[qcraw], writes=[u_scrB])
                kb.do("vector", lambda e: e.tensor_reduce(
                    ss16B.ap, u_scrB.ap.rearrange("p (h d) -> p h d", h=16),
                    AX.X, ALU.add),
                    reads=[u_scrB], writes=[ss16B])
                kb.do("vector", (lambda e, q=dqB.ap[:, 0:1], d=mxB.ap:
                                 e.tensor_scalar(d, q, q, None, ALU.mult)),
                      reads=[dqB], writes=[mxB])
                kb.do("vector", (lambda e, q=mxB.ap:
                                 e.tensor_scalar(ss16B.ap, ss16B.ap, q, None,
                                                 ALU.mult)),
                      reads=[ss16B, mxB], writes=[ss16B])
                kb.do("scalar", lambda e: e.activation(
                    ss16B.ap, ss16B.ap, AF.Sqrt, bias=C["eps6"].ap,
                    scale=C["i128"].ap),
                    reads=[ss16B, C["eps6"], C["i128"]], writes=[ss16B])
                kb.do("vector", lambda e: e.reciprocal(cs16B.ap, ss16B.ap),
                      reads=[ss16B], writes=[cs16B])
                kb.do("vector", (lambda e, q=dqB.ap[:, 0:1]:
                                 e.tensor_scalar(cs16B.ap, cs16B.ap, q, None,
                                                 ALU.mult)),
                      reads=[cs16B, dqB], writes=[cs16B])
                for h in range(16):
                    kb.do("vector", (lambda e,
                                     s=qcraw.ap[:, j * D + h * 128: j * D + (h + 1) * 128],
                                     c=cs16B.ap[:, h:h + 1]:
                                     e.tensor_scalar(s, s, c, None, ALU.mult)),
                          reads=[qcraw, cs16B], writes=[qcraw])
                transposes(qcraw,
                           (lambda i, jj=j: qcraw.ap[:, jj * D + i * 128: jj * D + (i + 1) * 128]),
                           qstg, 0, 16, scale_col=0, scale_buf=g_col)
                kb.do("gpsimd", (lambda e, s=qstg.ap.rearrange("p (a n) -> p a n", a=16),
                                 d=Qc_hbm[:, j * 128:(j + 1) * 128].rearrange(
                                     "(a p) n -> p a n", p=128):
                                 e.dma_start(out=d, in_=s)),
                      reads=[qstg], dma=True, tag=f"B_qcst{j}")
                if debug:
                    kb.do("gpsimd", (lambda e, s=qstg.ap.rearrange("p (a n) -> p a n", a=16),
                                     d=dbg["qc"][:, j * 128:(j + 1) * 128].rearrange(
                                         "(a p) n -> p a n", p=128):
                                     e.dma_start(out=d, in_=s)),
                          reads=[qstg], dma=True)
                qr_j = qrraw.ap[:, j * H * DR:(j + 1) * H * DR]
                kb.do("vector", (lambda e, s=qr_j, d=qrf.ap, q=dqB.ap[:, 1:2]:
                                 e.tensor_scalar(d, s, q, None, ALU.mult)),
                      reads=[qrraw, dqB], writes=[qrf])
                emit_trig(None, j, cosB, sinB)
                c3 = cosB.ap.unsqueeze(1).unsqueeze(1).broadcast_to(
                    [128, 16, 2, 32])
                s3 = sinB.ap.unsqueeze(1).unsqueeze(1).broadcast_to(
                    [128, 16, 2, 32])
                qr3 = qrf.ap.rearrange("p (h r d) -> p h r d", h=16, r=2)
                qrsw = qrf.ap.rearrange("p (a b) -> p a b", b=2)[:, :, ::-1]
                t1 = ropaB.ap[:, 0:H * DR]
                t2 = ropaB.ap[:, H * DR:2 * H * DR]
                kb.do("vector", (lambda e, tt=t1, aa=qr3, cc=c3: e.tensor_tensor(
                    tt.rearrange("p (h r d) -> p h r d", h=16, r=2), aa, cc,
                    ALU.mult)),
                      reads=[qrf, cosB], writes=[ropaB])
                kb.do("vector", (lambda e, tt=t2, aa=qrsw, ss=s3: e.tensor_tensor(
                    tt.rearrange("p (h r d) -> p h r d", h=16, r=2), aa, ss,
                    ALU.mult)),
                      reads=[qrf, sinB], writes=[ropaB])
                kb.do("vector", (lambda e, d=qrbB.ap, a=t1, b=t2:
                                 e.tensor_tensor(d, a, b, ALU.add)),
                      reads=[ropaB], writes=[qrbB])
                transposes(qrbB,
                           (lambda i: qrbB.ap[:, i * 128:(i + 1) * 128]),
                           qrstg, 0, 8)
                kb.do("gpsimd", (lambda e,
                                 s=qrstg.ap.rearrange("p (a n) -> p a n", a=8),
                                 d=Qr_hbm.rearrange("h d t -> (h d) t")[
                                     :, j * 128:(j + 1) * 128].rearrange(
                                     "(a p) n -> p a n", p=128):
                                 e.dma_start(out=d, in_=s)),
                      reads=[qrstg], dma=True, tag=f"B_qrst{j}")
                if debug:
                    kb.do("gpsimd", (lambda e,
                                     s=qrstg.ap.rearrange("p (a n) -> p a n", a=8),
                                     d=dbg["qr"].rearrange("h d t -> (h d) t")[
                                         :, j * 128:(j + 1) * 128].rearrange(
                                         "(a p) n -> p a n", p=128):
                                     e.dma_start(out=d, in_=s)),
                          reads=[qrstg], dma=True)

        kb.barrier()

        arena.reset()
        u_fm = arena.sub("u_fm", [128, H * NOWN * 128], BF16)
        lrow = arena.sub("lrow", [1, 512], F32)
        invl_pre = arena.sub("invl_pre", [128, NOWN * 16], F32)
        qc_h = [arena.sub(f"qc_h{i}", [128, NOWN * 128], BF16) for i in range(2)]
        qr_h = [arena.sub(f"qr_h{i}", [65, NOWN * 128], BF16) for i in range(2)]
        kcb = [arena.sub(f"kcb{i}", [128, 512], BF16) for i in range(2)]
        krab = [arena.sub(f"krab{i}", [65, 512], BF16) for i in range(2)]
        vb = [arena.sub(f"vb{i}", [128, 512], BF16) for i in range(2)]
        ptst = [arena.sub(f"ptst{i}", [128, 512], BF16) for i in range(2)]
        sct_scr = arena.sub("sct_scr", [128, 256], F32)
        wtileC = [arena.sub(f"wtileC{i}", [128, 2048], F32) for i in range(2)]
        woutC = [arena.sub(f"woutC{i}", [128, 2048], BF16) for i in range(2)]
        sg1C = arena.sub("sg1C", [128, 2048], BF16)
        waccC = arena.sub("waccC", [128, 1], F32)
        wpartC = arena.sub("wpartC", [128, 1], F32)
        wscC = arena.sub("wscC", [1, 1], F32)
        mneg = arena.sub("mneg", [128, NOWN], F32)
        mnegb = arena.sub("mnegb", [128, 1], BF16)
        invl = arena.sub("invl", [128, NOWN * 16], F32)
        ltr = arena.sub("ltr", [128, 16], F32)

        EXT = [32 * (j // 4) + 32 for j in range(NOWN)]
        UP = [16 * p + 16 for p in range(4)]

        if stage >= 3:
            wgen = w_pipeline(W_LATE, wtileC, woutC, sg1C, waccC, wpartC,
                              wscC, arena, PS[7])
            wgen_live = [True]

            def pump_w(n):
                for _ in range(n):
                    try:
                        next(wgen)
                    except StopIteration:
                        wgen_live[0] = False
                        return

            for i in range(2):
                kb.do("vector", (lambda e, a=krab[i].ap[64:65, :]:
                                 e.memset(a, 1.0)),
                      writes=[krab[i]], tag="ones_r")
            for h in range(H):
                hp = h % 2
                qcb, qrb = qc_h[hp], qr_h[hp]
                kb.do("sync", (lambda e, d=qcb.ap,
                               s=Qc_hbm[h * 128:(h + 1) * 128, :]:
                               e.dma_start(out=d, in_=s)),
                      writes=[qcb], dma=True, tag=f"C_qc{h}")
                kb.do("sync", (lambda e, d=qrb.ap[0:64, :],
                               s=Qr_hbm[h, :, :]:
                               e.dma_start(out=d, in_=s)),
                      writes=[qrb], dma=True, tag=f"C_qr{h}")
                memset_step(mneg, float(3.0e38))
                for kc in range(16):
                    pump_w(1)
                    kpar = kc % 2
                    kcc, krc = kcb[kpar], krab[kpar]
                    kb.do("sync", (lambda e, d=kcc.ap,
                                   s=Kc_hbm[kc * 512:(kc + 1) * 512,
                                            h * 128:(h + 1) * 128]:
                                   e.dma_start(out=d, in_=s, transpose=True)),
                          writes=[kcc], dma=True, tag=f"C1_kc{h}_{kc}")
                    kb.do("sync", (lambda e, d=krc.ap[0:64, :],
                                   s=Kr_hbm[h, :, kc * 512:(kc + 1) * 512]:
                                   e.dma_start(out=d, in_=s)),
                          writes=[krc], dma=True, tag=f"C1_kr{h}_{kc}")
                    for j in range(NOWN):
                        if kc * 4 >= EXT[j]:
                            continue
                        bank = PS[kc % 2]
                        kb.do("tensor", (lambda e,
                                         l=qcb.ap[:, j * 128:(j + 1) * 128],
                                         r=kcc.ap, pb=bank.ap:
                                         e.matmul(pb, l, r, start=True, stop=False)),
                              reads=[qcb, kcc], writes=[bank], tag="C1mmc")
                        kb.do("tensor", (lambda e,
                                         l=qrb.ap[0:64, j * 128:(j + 1) * 128],
                                         r=krc.ap[0:64, :], pb=bank.ap:
                                         e.matmul(pb, l, r, start=False, stop=True)),
                              reads=[qrb, krc], writes=[bank], tag="C1mmr")
                        kb.do("vector", (lambda e, pb=bank.ap, d=ltr.ap[:, 0:1]:
                                         e.tensor_reduce(d, pb, AX.X, ALU.max,
                                                         negate=True)),
                              reads=[bank], writes=[ltr])
                        kb.do("vector", (lambda e, d=mneg.ap[:, j:j + 1],
                                         s=ltr.ap[:, 0:1]:
                                         e.tensor_tensor(d, d, s, ALU.min)),
                              reads=[mneg, ltr], writes=[mneg])
                if debug and h == 3:
                    kb.do("gpsimd", (lambda e: e.dma_start(
                        out=dbg["mneg"], in_=mneg.ap)),
                          reads=[mneg], dma=True)
                for j in range(NOWN):
                    kb.do("vector", (lambda e, s=mneg.ap[:, j:j + 1]:
                                     e.tensor_copy(mnegb.ap, s)),
                          reads=[mneg], writes=[mnegb])
                    bank = PSB[0]
                    kb.do("tensor", (lambda e, pb=bank.ap[:, 0:128]:
                                     e.transpose(pb[0:1, 0:128], mnegb.ap,
                                                 ident_b.ap)),
                          reads=[mnegb, ident_b], writes=[bank], tag="C_mt")
                    kb.do("vector", (lambda e, pb=bank.ap[0:1, 0:128],
                                     d=qrb.ap[64:65, j * 128:(j + 1) * 128]:
                                     e.tensor_copy(d, pb)),
                          reads=[bank], writes=[qrb])
                for half in range(2):
                    U = UP[2 * half + 1]
                    lbank = PS[6]
                    abank = PS[4]
                    qoff = half * 512
                    pending = None

                    def flush(ev):
                        kt_, pt_, vslice_ = ev
                        kb.do("tensor", (lambda e, r=pt_.ap, pb=lbank.ap[0:1, 0:512],
                                         st=(kt_ == 0), sp=(kt_ == U - 1):
                                         e.matmul(pb, ones_col_b.ap, r,
                                                  start=st, stop=sp)),
                              reads=[ones_col_b, pt_], writes=[lbank], tag="C2l")
                        kb.do("tensor", (lambda e, l=vslice_, r=pt_.ap,
                                         pb=abank.ap,
                                         st=(kt_ == 0), sp=(kt_ == U - 1):
                                         e.matmul(pb, l, r, start=st, stop=sp)),
                              reads=[vb[0], vb[1], pt_], writes=[abank],
                              tag="C2av")

                    for kc in range(U // 4):
                        pump_w(1)
                        kpar = kc % 2
                        kcc, krc, vcc = kcb[kpar], krab[kpar], vb[kpar]
                        kb.do("sync", (lambda e, d=kcc.ap,
                                       s=Kc_hbm[kc * 512:(kc + 1) * 512,
                                                h * 128:(h + 1) * 128]:
                                       e.dma_start(out=d, in_=s, transpose=True)),
                              writes=[kcc], dma=True, tag=f"C2_kc{h}_{half}_{kc}")
                        kb.do("sync", (lambda e, d=krc.ap[0:64, :],
                                       s=Kr_hbm[h, :, kc * 512:(kc + 1) * 512]:
                                       e.dma_start(out=d, in_=s)),
                              writes=[krc], dma=True, tag=f"C2_kr{h}_{half}_{kc}")
                        kb.do("sync", (lambda e,
                                       d=vcc.ap.rearrange("p (a n) -> p a n", a=4),
                                       s=V_hbm[kc * 512:(kc + 1) * 512,
                                               h * 128:(h + 1) * 128].rearrange(
                                           "(a p) n -> p a n", p=128):
                                       e.dma_start(out=d, in_=s)),
                              writes=[vcc], dma=True, tag=f"C2_v{h}_{half}_{kc}")
                        for kt_in in range(4):
                            kt = kc * 4 + kt_in
                            sbank = PS[2 + (kt % 2)]
                            pt = ptst[kt % 2]
                            kb.do("tensor", (lambda e,
                                             l=kcc.ap[:, kt_in * 128:(kt_in + 1) * 128],
                                             r=qcb.ap[:, qoff:qoff + 512],
                                             pb=sbank.ap:
                                             e.matmul(pb, l, r, start=True,
                                                      stop=False)),
                                  reads=[kcc, qcb], writes=[sbank], tag="C2mmc")
                            kb.do("tensor", (lambda e,
                                             l=krc.ap[0:65, kt_in * 128:(kt_in + 1) * 128],
                                             r=qrb.ap[0:65, qoff:qoff + 512],
                                             pb=sbank.ap:
                                             e.matmul(pb, l, r, start=False,
                                                      stop=True)),
                                  reads=[krc, qrb], writes=[sbank], tag="C2mmr")
                            if pending is not None:
                                flush(pending)
                            kb.do("scalar", (lambda e, pb=sbank.ap, d=pt.ap:
                                             e.activation(d, pb, AF.Exp,
                                                          scale=C["scl"].ap)),
                                  reads=[sbank, C["scl"]], writes=[pt],
                                  tag="C2exp")
                            for jj in range(4):
                                j = 4 * half + jj
                                if kt >= 8 * j:
                                    irel = kt - 8 * j
                                    kb.do("vector", (lambda e,
                                                     d=pt.ap[:, jj * 128:(jj + 1) * 128],
                                                     m=Mmask.ap[:, irel * 128:(irel + 1) * 128]:
                                                     e.tensor_tensor(d, d, m,
                                                                     ALU.mult)),
                                          reads=[pt, Mmask], writes=[pt],
                                          tag="C2msk")
                            pending = (kt, pt,
                                       vcc.ap[:, kt_in * 128:(kt_in + 1) * 128])
                    flush(pending)
                    kb.do("vector", (lambda e, pb=abank.ap,
                                     d=u_fm.ap[:, h * NOWN * 128 + half * 512: h * NOWN * 128 + half * 512 + 512]:
                                     e.tensor_copy(d, pb)),
                          reads=[abank], writes=[u_fm], tag="C2uev")
                    kb.do("vector", (lambda e, pb=lbank.ap[0:1, 0:512]:
                                     e.tensor_copy(lrow.ap, pb)),
                          reads=[lbank], writes=[lrow], tag="C2lev")
                    for qi in range(4):
                        j = 4 * half + qi
                        tb = PS[0]
                        kb.do("tensor", (lambda e, qq=qi, pb=tb.ap[:, 0:1]:
                                         e.transpose(
                                             pb, lrow.ap[0:1, qq * 128:(qq + 1) * 128],
                                             ones_row_f.ap[0:1, 0:1])),
                              reads=[lrow, ones_row_f], writes=[tb], tag="C2lt")
                        kb.do("vector", (lambda e, pb=tb.ap[:, 0:1],
                                         d=invl_pre.ap[:, j * 16 + h: j * 16 + h + 1]:
                                         e.tensor_copy(d, pb)),
                              reads=[tb], writes=[invl_pre], tag="C2ltev")
                pump_w(14)
            while wgen_live[0]:
                pump_w(50)
            for j in range(NOWN):
                kb.do("vector", (lambda e, s=invl_pre.ap[:, j * 16:(j + 1) * 16],
                                 d=ltr.ap[:, 0:16]:
                                 e.tensor_scalar(d, s, C["tiny"].ap, None,
                                                 ALU.max)),
                      reads=[invl_pre, C["tiny"]], writes=[ltr])
                kb.do("vector", (lambda e,
                                 d=invl.ap[:, j * 16:(j + 1) * 16]:
                                 e.reciprocal(d, ltr.ap[:, 0:16])),
                      reads=[ltr], writes=[invl])
            for j in range(NOWN):
                for h in range(H):
                    bank = PSB[(h % 2)]
                    kb.do("tensor", (lambda e,
                                     s=u_fm.ap[:, h * NOWN * 128 + j * 128: h * NOWN * 128 + (j + 1) * 128],
                                     pb=bank.ap[:, 0:128]:
                                     e.transpose(pb, s, ident_b.ap)),
                          reads=[u_fm, ident_b], writes=[bank], tag="Cf_ut")
                    kb.do("vector", (lambda e, pb=bank.ap[:, 0:128],
                                     d=attn_out.ap[:, j * D + h * 128: j * D + (h + 1) * 128],
                                     c=invl.ap[:, j * 16 + h: j * 16 + h + 1]:
                                     e.tensor_scalar(d, pb, c, None, ALU.mult)),
                          reads=[bank, invl], writes=[attn_out], tag="Cf_uev")
            if debug:
                for j in range(NOWN):
                    kb.do("gpsimd", (lambda e,
                                     s=attn_out.ap[:, j * D:(j + 1) * D],
                                     d=dbg["ao"][:, j * D:(j + 1) * D]:
                                     e.dma_start(out=d, in_=s)),
                          reads=[attn_out], dma=True)

        kb.barrier()

        arena.reset()
        subln_bc = arena.sub("subln_bc", [128, FF], BF16)
        subln_row = arena.sub("subln_row", [1, 512], F32)
        xo = [arena.sub(f"xo{i}", [128, D], F32) for i in range(2)]
        aoqT = [arena.sub(f"aoqT{i}", [128, D], BF16) for i in range(2)]
        h1 = [arena.sub(f"h1_{i}", [128, D], F32) for i in range(2)]
        act_b = [arena.sub(f"act{i}", [128, FF], BF16) for i in range(2)]
        h2 = xo[1]
        fqT = aoqT
        BqT = arena.sub("BqT", [128, FF], BF16)
        wdb = [arena.sub(f"wdb{i}", [128, 8 * 512], BF16) for i in range(4)]
        u_scrD = arena.sub("u_scrD", [128, D], F32)
        tq = arena.sub("tq", [128, D], BF16)
        rlu = arena.sub("rlu", [128, 512], BF16)
        rr2 = arena.sub("rr2", [128, 512], F32)
        scr512 = arena.sub("scr512", [128, 512], BF16)
        sscols = [arena.sub(f"sscols{i}", [128, 16], F32) for i in range(2)]
        mxD = arena.sub("mxD", [128, 1], F32)
        sD = arena.sub("sD", [128, 1], F32)
        sc1 = arena.sub("sc1", [128, 1], F32)
        sc2 = arena.sub("sc2", [128, 1], F32)
        dqD = [arena.sub(f"dqD{i}", [128, 4], F32) for i in range(2)]

        def ln_steps(hbuf, outbuf, eps_key="eps5"):
            kb.do("vector", (lambda e, s=hbuf.ap: e.tensor_reduce(
                mxD.ap, s, AX.X, ALU.add)),
                reads=[hbuf], writes=[mxD])
            kb.do("vector", lambda e: e.tensor_scalar(
                mxD.ap, mxD.ap, C["i2048"].ap, None, ALU.mult),
                reads=[mxD, C["i2048"]], writes=[mxD])
            kb.do("vector", (lambda e, s=hbuf.ap: e.tensor_scalar(
                s, s, mxD.ap, None, ALU.subtract)),
                reads=[hbuf, mxD], writes=[hbuf])
            kb.do("vector", (lambda e, s=hbuf.ap: e.scalar_tensor_tensor(
                u_scrD.ap, s, C["one"].ap, s, ALU.mult, ALU.mult,
                accum_out=sD.ap)),
                reads=[hbuf, C["one"]], writes=[u_scrD, sD])
            kb.do("scalar", (lambda e, ek=eps_key: e.activation(
                sD.ap, sD.ap, AF.Sqrt, bias=C[ek].ap, scale=C["i2048"].ap)),
                reads=[sD, C[eps_key], C["i2048"]], writes=[sD])
            kb.do("vector", lambda e: e.reciprocal(sc1.ap, sD.ap),
                  reads=[sD], writes=[sc1])
            kb.do("vector", (lambda e, s=hbuf.ap, d=outbuf.ap: e.tensor_scalar(
                d, s, sc1.ap, None, ALU.mult)),
                reads=[hbuf, sc1], writes=[outbuf])

        if stage >= 4:
            for i in range(16):
                bank = PS[6]
                kb.do("sync", (lambda e, s=subln_in[0:1, i * 512:(i + 1) * 512]:
                               e.dma_start(out=subln_row.ap, in_=s)),
                      writes=[subln_row], dma=True)
                kb.do("tensor", (lambda e,
                                 r=subln_row.ap[0:1, 0:512],
                                 pb=bank.ap:
                                 e.matmul(pb, ones_row_f.ap[0:1, :], r,
                                          start=True, stop=True)),
                      reads=[ones_row_f, subln_row], writes=[bank], tag="D_sb")
                kb.do("vector", (lambda e, pb=bank.ap,
                                 d=subln_bc.ap[:, i * 512:(i + 1) * 512]:
                                 e.tensor_copy(d, pb)),
                      reads=[bank], writes=[subln_bc])
            for grp in range(4):
                tiles = [2 * grp, 2 * grp + 1]
                for ti, t in enumerate(tiles):
                    kb.do("sync", (lambda e, d=xo[ti].ap,
                                   s=x_own[t * 128:(t + 1) * 128, :]:
                                   e.dma_start(out=d, in_=s)),
                          writes=[xo[ti]], dma=True, tag=f"D_x{t}")
                    ao = Buf(attn_out.ap[:, t * D:(t + 1) * D], "ao")
                    ao.res = attn_out.res
                    quant(ao, ao.ap, tq, dqD[ti], 0, D, mxD, sD, u_scrD)
                    kb.do("vector", (lambda e, d=dqD[ti].ap[:, 0:1],
                                     h=hws.ap[:, WI["wout"]:WI["wout"] + 1]:
                                     e.tensor_scalar(d, d, h, None, ALU.mult)),
                          reads=[dqD[ti], hws], writes=[dqD[ti]])
                    transposes(tq, (lambda i: tq.ap[:, i * 128:(i + 1) * 128]),
                               aoqT[ti], 0, 16)
                for oc in range(4):
                    for kh in range(2):
                        wb = wdb[(oc * 2 + kh) % 2]
                        kb.do("sync", (lambda e,
                                       d=wb.ap.rearrange("p (a n) -> p a n", a=8),
                                       s=tern["wout"][kh * 1024:(kh + 1) * 1024,
                                                      oc * 512:(oc + 1) * 512].rearrange(
                                           "(a p) n -> p a n", p=128):
                                       e.dma_start(out=d, in_=s)),
                              writes=[wb], dma=True, tag=f"D_wo{oc}_{kh}")
                        for ti in range(2):
                            bank = PS[2 + ti]
                            for ic in range(8):
                                kb.do("tensor", (lambda e,
                                                 l=aoqT[ti].ap[:, kh * 1024 + ic * 128: kh * 1024 + (ic + 1) * 128],
                                                 r=wb.ap[:, ic * 512:(ic + 1) * 512],
                                                 st=(kh == 0 and ic == 0),
                                                 sp=(kh == 1 and ic == 7),
                                                 pb=bank.ap:
                                                 e.matmul(pb, l, r, start=st,
                                                          stop=sp)),
                                      reads=[aoqT[ti], wb], writes=[bank],
                                      tag="D_womm")
                    for ti in range(2):
                        bank = PS[2 + ti]
                        kb.do("vector", (lambda e, pb=bank.ap,
                                         x=xo[ti].ap[:, oc * 512:(oc + 1) * 512],
                                         d=h1[ti].ap[:, oc * 512:(oc + 1) * 512],
                                         q=dqD[ti].ap[:, 0:1]:
                                         e.scalar_tensor_tensor(
                                             d, pb, q, x, ALU.mult, ALU.add)),
                              reads=[bank, xo[ti], dqD[ti]], writes=[h1[ti]],
                              tag="D_woev")
                for ti in range(2):
                    ln_steps(h1[ti], h1[ti])
                    quant(h1[ti], h1[ti].ap, tq, dqD[ti], 1, D, mxD, sD, u_scrD)
                    transposes(tq, (lambda i: tq.ap[:, i * 128:(i + 1) * 128]),
                               fqT[ti], 0, 16)
                    memset_step(sscols[ti], 0.0)
                for oc in range(16):
                    for kh in range(2):
                        wgb = wdb[kh % 2]
                        wub = wdb[2 + kh % 2]
                        kb.do("sync", (lambda e,
                                       d=wgb.ap.rearrange("p (a n) -> p a n", a=8),
                                       s=tern["wg"][kh * 1024:(kh + 1) * 1024,
                                                    oc * 512:(oc + 1) * 512].rearrange(
                                           "(a p) n -> p a n", p=128):
                                       e.dma_start(out=d, in_=s)),
                              writes=[wgb], dma=True, tag=f"D_wg{oc}_{kh}")
                        kb.do("sync", (lambda e,
                                       d=wub.ap.rearrange("p (a n) -> p a n", a=8),
                                       s=tern["wu"][kh * 1024:(kh + 1) * 1024,
                                                    oc * 512:(oc + 1) * 512].rearrange(
                                           "(a p) n -> p a n", p=128):
                                       e.dma_start(out=d, in_=s)),
                              writes=[wub], dma=True, tag=f"D_wu{oc}_{kh}")
                        for ti in range(2):
                            gbank = PS[2 + ti]
                            ubank = PS[4 + ti]
                            for ic in range(8):
                                kb.do("tensor", (lambda e,
                                                 l=fqT[ti].ap[:, kh * 1024 + ic * 128: kh * 1024 + (ic + 1) * 128],
                                                 r=wgb.ap[:, ic * 512:(ic + 1) * 512],
                                                 st=(kh == 0 and ic == 0),
                                                 sp=(kh == 1 and ic == 7),
                                                 pb=gbank.ap:
                                                 e.matmul(pb, l, r, start=st,
                                                          stop=sp)),
                                      reads=[fqT[ti], wgb], writes=[gbank],
                                      tag="D_gmm")
                                kb.do("tensor", (lambda e,
                                                 l=fqT[ti].ap[:, kh * 1024 + ic * 128: kh * 1024 + (ic + 1) * 128],
                                                 r=wub.ap[:, ic * 512:(ic + 1) * 512],
                                                 st=(kh == 0 and ic == 0),
                                                 sp=(kh == 1 and ic == 7),
                                                 pb=ubank.ap:
                                                 e.matmul(pb, l, r, start=st,
                                                          stop=sp)),
                                      reads=[fqT[ti], wub], writes=[ubank],
                                      tag="D_umm")
                    for ti in range(2):
                        gbank = PS[2 + ti]
                        ubank = PS[4 + ti]
                        kb.do("scalar", (lambda e, pb=gbank.ap, d=rlu.ap:
                                         e.activation(d, pb, AF.Relu)),
                              reads=[gbank], writes=[rlu], tag="D_relu")
                        kb.do("vector", lambda e: e.tensor_tensor(
                            rr2.ap, rlu.ap, rlu.ap, ALU.mult),
                            reads=[rlu], writes=[rr2], tag="D_rr")
                        kb.do("vector", (lambda e, pb=ubank.ap,
                                         d=act_b[ti].ap[:, oc * 512:(oc + 1) * 512]:
                                         e.tensor_tensor(d, rr2.ap, pb, ALU.mult)),
                              reads=[rr2, ubank], writes=[act_b[ti]], tag="D_act")
                        kb.do("vector", (lambda e,
                                         s=act_b[ti].ap[:, oc * 512:(oc + 1) * 512],
                                         acc=sscols[ti].ap[:, oc:oc + 1]:
                                         e.scalar_tensor_tensor(
                                             scr512.ap, s, C["one"].ap, s,
                                             ALU.mult, ALU.mult, accum_out=acc)),
                              reads=[act_b[ti], C["one"]],
                              writes=[scr512, sscols[ti]], tag="D_ss")
                for ti, t in enumerate(tiles):
                    kb.do("vector", (lambda e, d=sc1.ap,
                                     h=hws.ap[:, WI["wg"]:WI["wg"] + 1],
                                     ix=dqD[ti].ap[:, 1:2]:
                                     e.tensor_scalar(d, h, ix, None, ALU.mult)),
                          reads=[hws, dqD[ti]], writes=[sc1])
                    kb.do("vector", (lambda e, d=sc2.ap,
                                     h=hws.ap[:, WI["wu"]:WI["wu"] + 1],
                                     ix=dqD[ti].ap[:, 1:2]:
                                     e.tensor_scalar(d, h, ix, None, ALU.mult)),
                          reads=[hws, dqD[ti]], writes=[sc2])
                    kb.do("vector", lambda e: e.tensor_tensor(
                        sc1.ap, sc1.ap, sc1.ap, ALU.mult),
                        reads=[sc1], writes=[sc1])
                    kb.do("vector", (lambda e, d=dqD[ti].ap[:, 2:3]:
                                     e.tensor_tensor(d, sc1.ap, sc2.ap, ALU.mult)),
                          reads=[sc1, sc2], writes=[dqD[ti]])
                    kb.do("vector", (lambda e, s=sscols[ti].ap:
                                     e.tensor_reduce(mxD.ap, s, AX.X, ALU.add)),
                          reads=[sscols[ti]], writes=[mxD])
                    kb.do("vector", (lambda e, q=dqD[ti].ap[:, 2:3]:
                                     e.tensor_scalar(sc1.ap, q, q, None, ALU.mult)),
                          reads=[dqD[ti]], writes=[sc1])
                    kb.do("vector", lambda e: e.tensor_tensor(
                        mxD.ap, mxD.ap, sc1.ap, ALU.mult),
                        reads=[mxD, sc1], writes=[mxD])
                    kb.do("scalar", lambda e: e.activation(
                        mxD.ap, mxD.ap, AF.Sqrt, bias=C["eps6"].ap,
                        scale=C["i8192"].ap),
                        reads=[mxD, C["eps6"], C["i8192"]], writes=[mxD])
                    kb.do("vector", lambda e: e.reciprocal(sc2.ap, mxD.ap),
                          reads=[mxD], writes=[sc2])
                    kb.do("vector", (lambda e, q=dqD[ti].ap[:, 2:3]:
                                     e.tensor_tensor(sc2.ap, sc2.ap, q, ALU.mult)),
                          reads=[sc2, dqD[ti]], writes=[sc2])
                    for oc in range(16):
                        kb.do("vector", (lambda e,
                                         s=act_b[ti].ap[:, oc * 512:(oc + 1) * 512],
                                         g=subln_bc.ap[:, oc * 512:(oc + 1) * 512]:
                                         e.scalar_tensor_tensor(
                                             s, s, sc2.ap, g, ALU.mult, ALU.mult)),
                              reads=[act_b[ti], sc2, subln_bc],
                              writes=[act_b[ti]], tag="D_sub")
                    kb.do("vector", (lambda e, s=act_b[ti].ap:
                                     e.tensor_reduce(mxD.ap, s, AX.X, ALU.max,
                                                     apply_absolute_value=True)),
                          reads=[act_b[ti]], writes=[mxD])
                    kb.do("vector", lambda e: e.tensor_scalar(
                        mxD.ap, mxD.ap, C["eps5"].ap, None, ALU.max),
                        reads=[mxD, C["eps5"]], writes=[mxD])
                    kb.do("vector", (lambda e, d=dqD[ti].ap[:, 3:4]:
                                     e.tensor_scalar(d, mxD.ap, C["i127"].ap,
                                                     None, ALU.mult)),
                          reads=[mxD, C["i127"]], writes=[dqD[ti]])
                    kb.do("vector", (lambda e, d=dqD[ti].ap[:, 3:4],
                                     h=hws.ap[:, WI["wd"]:WI["wd"] + 1]:
                                     e.tensor_scalar(d, d, h, None, ALU.mult)),
                          reads=[dqD[ti], hws], writes=[dqD[ti]])
                    kb.do("vector", lambda e: e.reciprocal(sD.ap, mxD.ap),
                          reads=[mxD], writes=[sD])
                    kb.do("vector", lambda e: e.tensor_scalar(
                        sD.ap, sD.ap, C["c127"].ap, None, ALU.mult),
                        reads=[sD, C["c127"]], writes=[sD])
                    for oc in range(16):
                        kb.do("scalar", (lambda e,
                                         s=act_b[ti].ap[:, oc * 512:(oc + 1) * 512],
                                         d=u_scrD.ap[:, 0:512]:
                                         e.activation(d, s, AF.Identity,
                                                      bias=C["magic"].ap,
                                                      scale=sD.ap)),
                              reads=[act_b[ti], sD, C["magic"]],
                              writes=[u_scrD], tag="D_bq1")
                        kb.do("vector", (lambda e,
                                         d=act_b[ti].ap[:, oc * 512:(oc + 1) * 512],
                                         s=u_scrD.ap[:, 0:512]:
                                         e.tensor_scalar(d, s, C["magic"].ap,
                                                         None, ALU.subtract)),
                              reads=[u_scrD, C["magic"]], writes=[act_b[ti]],
                              tag="D_bq2")
                    transposes(act_b[ti],
                               (lambda i, tb=act_b[ti]: tb.ap[:, i * 128:(i + 1) * 128]),
                               BqT, 0, 64)
                    for oc2 in range(4):
                        bank = PS[2 + ti]
                        for kc in range(4):
                            wb = wdb[kc % 2]
                            kb.do("sync", (lambda e,
                                           d=wb.ap.rearrange("p (a n) -> p a n", a=8),
                                           s=tern["wd"][kc * 2048:(kc * 2048 + 1024),
                                                        oc2 * 512:(oc2 + 1) * 512].rearrange(
                                               "(a p) n -> p a n", p=128):
                                           e.dma_start(out=d, in_=s)),
                                  writes=[wb], dma=True, tag=f"D_wd")
                            wb2 = wdb[2 + kc % 2]
                            kb.do("sync", (lambda e,
                                           d=wb2.ap.rearrange("p (a n) -> p a n", a=8),
                                           s=tern["wd"][(kc * 2048 + 1024):(kc + 1) * 2048,
                                                        oc2 * 512:(oc2 + 1) * 512].rearrange(
                                               "(a p) n -> p a n", p=128):
                                           e.dma_start(out=d, in_=s)),
                                  writes=[wb2], dma=True, tag=f"D_wd2")
                            for hf, wbx in ((0, wb), (1, wb2)):
                                for ic in range(8):
                                    icg = kc * 16 + hf * 8 + ic
                                    kb.do("tensor", (lambda e,
                                                     l=BqT.ap[:, icg * 128:(icg + 1) * 128],
                                                     r=wbx.ap[:, ic * 512:(ic + 1) * 512],
                                                     st=(icg == 0), sp=(icg == 63),
                                                     pb=bank.ap:
                                                     e.matmul(pb, l, r, start=st,
                                                              stop=sp)),
                                          reads=[BqT, wbx], writes=[bank],
                                          tag="D_dmm")
                        kb.do("vector", (lambda e, pb=bank.ap,
                                         x=h1[ti].ap[:, oc2 * 512:(oc2 + 1) * 512],
                                         d=h2.ap[:, oc2 * 512:(oc2 + 1) * 512],
                                         q=dqD[ti].ap[:, 3:4]:
                                         e.scalar_tensor_tensor(
                                             d, pb, q, x, ALU.mult, ALU.add)),
                              reads=[bank, h1[ti], dqD[ti]], writes=[h2],
                              tag="D_dev")
                    ln_steps(h2, h2)
                    kb.do("gpsimd", (lambda e, s=h2.ap,
                                     d=y_out[t * 128:(t + 1) * 128, :]:
                                     e.dma_start(out=d, in_=s)),
                          reads=[h2], dma=True, tag=f"D_y{t}")

        last_q = {}
        for st in kb.steps:
            if st.dma:
                last_q[st.queue] = st
        kb.do("gpsimd", lambda e: e.nop(),
              extra=[s for s in last_q.values()], tag="final_wait")

        kb.emit(block)

    return nc


_nc_cache = {}


def _get_nc():
    key = (STAGE, DEBUG_DUMPS)
    if key not in _nc_cache:
        _nc_cache[key] = build_program(STAGE, DEBUG_DUMPS)
    return _nc_cache[key]


def kernel(L0_attn_x, L0_attn_wq_content, L0_attn_wq_rope, L0_attn_wkv_down,
           L0_attn_wk_up, L0_attn_wv_up, L0_attn_wkr_up, L0_attn_wout,
           L0_attn_qn_w, L0_attn_kn_w, L0_ffn_w_gate, L0_ffn_w_up,
           L0_ffn_w_down, L0_ffn_subln_weight):
    f32 = lambda a: np.ascontiguousarray(np.asarray(a, dtype=np.float32))
    x = f32(L0_attn_x).reshape(T, D)
    wT = {
        "wqc": f32(L0_attn_wq_content).T, "wqr": f32(L0_attn_wq_rope).T,
        "wkvd": f32(L0_attn_wkv_down).T, "wku": f32(L0_attn_wk_up).T,
        "wvu": f32(L0_attn_wv_up).T, "wkr": f32(L0_attn_wkr_up).T,
        "wout": f32(L0_attn_wout).T, "wg": f32(L0_ffn_w_gate).T,
        "wu": f32(L0_ffn_w_up).T, "wd": f32(L0_ffn_w_down).T,
    }
    wT = {k: np.ascontiguousarray(v) for k, v in wT.items()}
    qn = f32(L0_attn_qn_w).reshape(128, 1)
    kn = f32(L0_attn_kn_w).reshape(128, 1)
    subln = f32(L0_ffn_subln_weight).reshape(1, FF)
    xt = x.reshape(NT, 128, D)
    in_maps = []
    for c in range(NCORES):
        own = np.ascontiguousarray(
            xt[[8 * j + c for j in range(NOWN)]].reshape(NOWN * 128, D))
        m = {"x_full": x, "x_own": own,
             "cofs": np.full((128, 1), 128.0 * c, np.float32),
             "qn": qn, "kn": kn, "subln": subln}
        for nm in wT:
            m[nm + "T"] = wT[nm]
        in_maps.append(m)
    nc = _get_nc()
    res = run_bass_kernel_spmd(nc, in_maps, core_ids=list(range(NCORES)))
    y = np.empty((NT, 128, D), dtype=np.float32)
    for c in range(NCORES):
        yc = res.results[c]["y"].reshape(NOWN, 128, D)
        for j in range(NOWN):
            y[8 * j + c] = yc[j]
    return y.reshape(1, T, D)




# version 07
# speedup vs baseline: 1.1104x; 1.0457x over previous
import numpy as np
import concourse.bass as bass
import concourse.mybir as mybir
from concourse.bass_utils import run_bass_kernel_spmd

F32 = mybir.dt.float32
BF16 = mybir.dt.bfloat16
AF = mybir.ActivationFunctionType
ALU = mybir.AluOpType
AX = mybir.AxisListType

MAGIC = float(3 * 2 ** 22)
NCORES = 8
T, D, H, DH, DR, LAT, FF = 8192, 2048, 16, 128, 64, 512, 8192
NT = T // 128
NOWN = NT // NCORES
SCALE = float(np.float32(1.0 / np.sqrt(DH + DR)))
LN10K = float(np.log(10000.0))

STAGE = 4
DEBUG_DUMPS = False

ENGS = ("sync", "gpsimd", "scalar", "vector", "tensor")


class Step:
    __slots__ = ("eng", "fn", "dma", "waits", "mark", "cum", "tag", "eidx")

    def __init__(self, eng, fn, dma, tag):
        self.eng, self.fn, self.dma, self.tag = eng, fn, dma, tag
        self.waits = []
        self.mark = False
        self.cum = 0
        self.eidx = 0

    @property
    def queue(self):
        return ("q_" + self.eng) if self.dma else self.eng


class Res:
    __slots__ = ("w", "rs", "name")

    def __init__(self, name=""):
        self.w = None
        self.rs = []
        self.name = name


class Buf:
    __slots__ = ("ap", "res")

    def __init__(self, ap, name=""):
        self.ap = ap
        self.res = Res(name)

    def __getitem__(self, key):
        return self.ap[key]


class KB:
    def __init__(self, nc):
        self.nc = nc
        self.steps = []
        self.sems = {}
        self.last_by_eng = {}
        self.bar_gate = {}

    def do(self, eng, fn, reads=(), writes=(), extra=(), dma=False, tag=""):
        st = Step(eng, fn, dma, tag)
        seen = {}

        def addw(w, raw):
            if w is None:
                return
            if id(w) in seen:
                if raw and not seen[id(w)][1]:
                    seen[id(w)][1] = True
                return
            ent = [w, raw]
            seen[id(w)] = ent
            st.waits.append(ent)

        for r in reads:
            addw(r.res.w if isinstance(r, Buf) else r.w, True)
        for r in writes:
            rr = r.res if isinstance(r, Buf) else r
            addw(rr.w, False)
            for s in rr.rs:
                addw(s, False)
        for w in extra:
            addw(w, False)
        g = self.bar_gate.get(eng)
        if g is not None:
            addw(g, False)
        self.steps.append(st)
        self.last_by_eng[eng] = st
        for r in reads:
            rr = r.res if isinstance(r, Buf) else r
            rr.rs.append(st)
        for r in writes:
            rr = r.res if isinstance(r, Buf) else r
            rr.w = st
            rr.rs = []
        return st

    def barrier(self):
        last_by_q = {}
        for st in self.steps:
            last_by_q[st.queue] = st
        dma_lasts = [s for q, s in last_by_q.items() if q.startswith("q_")]
        firsts = []
        for eng in ENGS:
            prev = self.last_by_eng.get(eng)
            ex = dma_lasts + ([prev] if prev is not None else [])
            firsts.append(self.do(eng, lambda e: e.nop(), extra=ex,
                                   tag="bar1"))
        for eng in ENGS:
            st = self.do(eng, lambda e: e.nop(), extra=firsts, tag="bar2")
            self.bar_gate[eng] = st


    COST = {"tensor": 230, "vector": 420, "scalar": 560, "gpsimd": 250,
            "sync": 60}
    DMA_ISSUE = 90
    DMA_LAT = 1800

    def reorder(self):
        import heapq
        steps = self.steps
        n = len(steps)
        idx = {id(s): i for i, s in enumerate(steps)}
        indeg = [0] * n
        succ = [[] for _ in range(n)]
        for i, st in enumerate(steps):
            for w, _raw in st.waits:
                j = idx[id(w)]
                succ[j].append(i)
                indeg[i] += 1
        eng_time = {e: 0.0 for e in ENGS}
        dep_time = [0.0] * n
        ready = {e: [] for e in ENGS}
        for i in range(n):
            if indeg[i] == 0:
                heapq.heappush(ready[steps[i].eng], (0.0, i))
        out = []
        while len(out) < n:
            best = None
            for e in ENGS:
                if not ready[e]:
                    continue
                dt, i = ready[e][0]
                start = max(eng_time[e], dt)
                key = (start, i)
                if best is None or key < best[0]:
                    best = (key, e)
            (start, i), e = best
            heapq.heappop(ready[e])
            st = steps[i]
            if st.dma:
                eng_time[e] = start + self.DMA_ISSUE
                fin = start + self.DMA_LAT
            else:
                fin = start + self.COST[e]
                eng_time[e] = fin
            out.append(st)
            for j in succ[i]:
                indeg[j] -= 1
                dep_time[j] = max(dep_time[j], fin)
                if indeg[j] == 0:
                    heapq.heappush(ready[steps[j].eng], (dep_time[j], j))
        self.steps = out

    FENCE_WINDOW = 10

    def emit(self, block):
        self.reorder()
        eidx = {}
        for st in self.steps:
            eidx[st.eng] = eidx.get(st.eng, 0) + 1
            st.eidx = eidx[st.eng]
        for st in self.steps:
            if st.dma:
                st.mark = True
            for w, raw in st.waits:
                if w.dma or w.eng != st.eng:
                    w.mark = True
                elif raw and st.eidx - w.eidx <= self.FENCE_WINDOW:
                    w.mark = True
        cum = {}
        for st in self.steps:
            q = st.queue
            if st.mark:
                cum[q] = cum.get(q, 0) + (16 if st.dma else 1)
            st.cum = cum.get(q, 0)
        for q in set(s.queue for s in self.steps):
            self.sems[q] = self.nc.alloc_semaphore(q)

        def mk_body(eng):
            def body(e):
                water = {}
                for st in self.steps:
                    if st.eng != eng:
                        continue
                    wmap = {}
                    for w, raw in st.waits:
                        if (not w.dma) and w.eng == eng and (
                                (not raw) or st.eidx - w.eidx > KB.FENCE_WINDOW):
                            continue
                        assert w.mark and w.cum > 0
                        q = w.queue
                        if wmap.get(q, 0) < w.cum:
                            wmap[q] = w.cum
                    for q, v in sorted(wmap.items()):
                        if water.get(q, 0) >= v:
                            continue
                        e.wait_ge(self.sems[q], v)
                        water[q] = v
                    inst = st.fn(e)
                    if st.tag:
                        inst.annotate(st.tag)
                    if st.mark:
                        inst.then_inc(self.sems[st.queue],
                                      16 if st.dma else 1)
            return body

        block.sync(mk_body("sync"))
        block.gpsimd(mk_body("gpsimd"))
        block.scalar(mk_body("scalar"))
        block.vector(mk_body("vector"))
        block.tensor(mk_body("tensor"))


class Arena:

    _n = 0

    def __init__(self, nc, nbytes):
        Arena._n += 1
        self.t = nc.alloc_sbuf_tensor(f"arena{Arena._n}", [128, nbytes // 4], F32)
        self.nbytes = nbytes
        self.cur = 0

    def reset(self):
        self.cur = 0

    def sub(self, name, shape, dtype):
        assert len(shape) == 2
        esz = 2 if dtype == BF16 else 4
        nb = shape[1] * esz
        off = (self.cur + 31) & ~31
        self.cur = off + nb
        assert self.cur <= self.nbytes, f"arena overflow at {name}: {self.cur}"
        base = self.t.ap()[:, off // 4:(off + nb + 3) // 4]
        ap = base.bitcast(dtype) if dtype == BF16 else base
        ap = ap[0:shape[0], 0:shape[1]]
        return Buf(ap, name)


def build_program(stage=STAGE, debug=DEBUG_DUMPS):
    nc = bass.Bass("TRN2", target_bir_lowering=False,
                   detect_race_conditions=False)

    dt = lambda n, s, d=F32, k="ExternalInput": nc.dram_tensor(n, s, d, kind=k).ap()
    x_full = dt("x_full", [T, D])
    x_own = dt("x_own", [NOWN * 128, D])
    cofs = dt("cofs", [128, 1])
    qn_in = dt("qn", [128, 1])
    kn_in = dt("kn", [128, 1])
    subln_in = dt("subln", [1, FF])
    w_in = {}
    WEIGHTS = [
        ("wqc", D, D), ("wqr", D, H * DR), ("wkvd", D, LAT),
        ("wku", LAT, D), ("wvu", LAT, D), ("wkr", LAT, H * DR),
        ("wout", D, D), ("wg", D, FF), ("wu", D, FF), ("wd", FF, D),
    ]
    for nm, K, N in WEIGHTS:
        w_in[nm] = dt(nm + "T", [K, N])
    tern = {nm: dt("t_" + nm, [K, N], BF16, "Internal")
            for nm, K, N in WEIGHTS}
    Kc_hbm = dt("kc_hbm", [T, D], BF16, "Internal")
    Kr_hbm = dt("kr_hbm", [H, DR, T], BF16, "Internal")
    V_hbm = dt("v_hbm", [T, D], BF16, "Internal")
    Qc_hbm = dt("qc_hbm", [D, NOWN * 128], BF16, "Internal")
    Qr_hbm = dt("qr_hbm", [H, DR, NOWN * 128], BF16, "Internal")
    y_out = dt("y", [NOWN * 128, D], F32, "ExternalOutput")
    dbg = {}
    if debug:
        dbg["tern"] = dt("dbg_tern", [D, LAT], F32, "ExternalOutput")
        dbg["hws"] = dt("dbg_hws", [128, 16], F32, "ExternalOutput")
        dbg["wacc"] = dt("dbg_wacc", [128, 16], F32, "ExternalOutput")
        dbg["trig"] = dt("dbg_trig", [128, 4 * 64], F32, "ExternalOutput")
        dbg["krf"] = dt("dbg_krf", [128, H * DR], F32, "ExternalOutput")
        dbg["krb"] = dt("dbg_krb", [128, H * DR], F32, "ExternalOutput")
        dbg["wpart"] = dt("dbg_wpart", [128, 16], F32, "ExternalOutput")
        if stage >= 1:
            dbg["kc"] = dt("dbg_kc", [T, D], F32, "ExternalOutput")
            dbg["kr"] = dt("dbg_kr", [H, DR, T], F32, "ExternalOutput")
            dbg["v"] = dt("dbg_v", [T, D], F32, "ExternalOutput")
            dbg["xq"] = dt("dbg_xq", [128, D], F32, "ExternalOutput")
            dbg["latq"] = dt("dbg_latq", [128, LAT], F32, "ExternalOutput")
            dbg["kcraw"] = dt("dbg_kcraw", [128, D], F32, "ExternalOutput")
            dbg["invs"] = dt("dbg_invs", [128, 4], F32, "ExternalOutput")
        if stage >= 2:
            dbg["qc"] = dt("dbg_qc", [D, NOWN * 128], F32, "ExternalOutput")
            dbg["qr"] = dt("dbg_qr", [H, DR, NOWN * 128], F32, "ExternalOutput")
        if stage >= 3:
            dbg["ao"] = dt("dbg_ao", [128, NOWN * D], F32, "ExternalOutput")
            dbg["mneg"] = dt("dbg_mneg", [128, NOWN], F32, "ExternalOutput")
            dbg["pt"] = dt("dbg_pt", [128, 256], F32, "ExternalOutput")
            dbg["sct"] = dt("dbg_sct", [128, 256], F32, "ExternalOutput")

    WI = {nm: i for i, (nm, _, _) in enumerate(WEIGHTS)}

    with (
        nc.psum_tensor("ps0", [128, 512], F32) as ps0_t,
        nc.psum_tensor("ps1", [128, 512], F32) as ps1_t,
        nc.psum_tensor("ps2", [128, 512], F32) as ps2_t,
        nc.psum_tensor("ps3", [128, 512], F32) as ps3_t,
        nc.psum_tensor("ps4", [128, 512], F32) as ps4_t,
        nc.psum_tensor("ps5", [128, 512], F32) as ps5_t,
        nc.psum_tensor("ps6", [128, 512], F32) as ps6_t,
        nc.psum_tensor("ps7", [128, 512], F32) as ps7_t,
        nc.Block() as block,
    ):
        PS = [Buf(t.ap(), f"ps{i}") for i, t in
              enumerate([ps0_t, ps1_t, ps2_t, ps3_t, ps4_t, ps5_t, ps6_t, ps7_t])]
        PSB = [Buf(t.ap().bitcast(BF16), f"psb{i}") for i, t in
               enumerate([ps0_t, ps1_t])]
        PSB[0].res = PS[0].res
        PSB[1].res = PS[1].res

        kb = KB(nc)

        persist = Arena(nc, 44 * 1024)
        C = {}
        const_vals = dict(
            magic=MAGIC, eps5=1e-5, eps6=1e-6, c127=127.0,
            i127=1.0 / 127.0, i128=1.0 / 128.0, i2048=1.0 / 2048.0,
            i8192=1.0 / 8192.0, half=0.5, neg1=-1.0, tiny=1e-12,
            zero=0.0, one=1.0, big=3.0e38, i2pi=float(1.0 / (2 * np.pi)),
            n2pi=float(-2 * np.pi), hpi=float(np.pi / 2),
            lnb=float(-LN10K / 32.0), scl=SCALE,
            pip=3.141592502593994, pin=-3.141592502593994,
        )
        for k in const_vals:
            C[k] = persist.sub("c_" + k, [128, 1], F32)
        ident_b = persist.sub("ident_b", [128, 128], BF16)
        ident_f = persist.sub("ident_f", [128, 128], F32)
        ones_col_b = persist.sub("ones_col_b", [128, 1], BF16)
        ones_col_f = persist.sub("ones_col_f", [128, 1], F32)
        ones_row_f = persist.sub("ones_row_f", [128, 128], F32)
        Mmask = persist.sub("Mmask", [128, 32 * 128], BF16)
        g_col = persist.sub("g_col", [128, 1], F32)
        cofs_sb = persist.sub("cofs_sb", [128, 1], F32)
        hws = persist.sub("hws", [128, 16], F32)
        hneg = persist.sub("hneg", [128, 16], F32)
        alt = persist.sub("alt", [128, 64], F32)
        IF32 = persist.sub("IF32", [128, 32], F32)
        attn_out = persist.sub("attn_out", [128, NOWN * D], BF16)
        tv = persist.sub("tv", [128, 1], F32)
        th = persist.sub("th", [128, 32], F32)
        th2 = persist.sub("th2", [128, 32], F32)
        trig32 = persist.sub("trig32", [128, 32], F32)
        arena_cb = [persist.sub(f"arena_cb{i}", [128, 1], F32) for i in range(2)]

        arena = Arena(nc, 162 * 1024)

        def memset_step(buf, val, eng="vector"):
            kb.do(eng, (lambda e, a=buf.ap, v=val: e.memset(a, v)),
                  writes=[buf])

        def quant(src_buf, src_ap, xq_buf, invs_buf, invs_col, n,
                  mx, s, u_scr, extra_reads=()):
            v = kb.do
            mxa = mx.ap
            v("vector", (lambda e, sa=src_ap: e.tensor_reduce(
                mxa, sa, AX.X, ALU.max, apply_absolute_value=True)),
              reads=[src_buf, *extra_reads], writes=[mx])
            v("vector", lambda e: e.tensor_scalar(
                mxa, mxa, C["eps5"].ap, None, ALU.max),
              reads=[mx, C["eps5"]], writes=[mx])
            v("vector", (lambda e, ia=invs_buf.ap[:, invs_col:invs_col + 1]:
                         e.tensor_scalar(ia, mxa, C["i127"].ap, None, ALU.mult)),
              reads=[mx, C["i127"]], writes=[invs_buf])
            v("vector", lambda e: e.reciprocal(s.ap, mxa),
              reads=[mx], writes=[s])
            v("vector", lambda e: e.tensor_scalar(
                s.ap, s.ap, C["c127"].ap, None, ALU.mult),
              reads=[s, C["c127"]], writes=[s])
            ua = u_scr.ap[:, 0:n]
            v("scalar", (lambda e, sa=src_ap: e.activation(
                ua, sa, AF.Identity, bias=C["magic"].ap, scale=s.ap)),
              reads=[src_buf, s, C["magic"], *extra_reads], writes=[u_scr])
            v("vector", (lambda e, xa=xq_buf.ap[:, 0:n]: e.tensor_scalar(
                xa, ua, C["magic"].ap, None, ALU.subtract)),
              reads=[u_scr, C["magic"]], writes=[xq_buf])

        def transposes(src_buf, src_ap_fn, dst_buf, dst_col0, nchunks,
                       scale_col=None, scale_buf=None):
            for i in range(nchunks):
                bank = PSB[(i // 8) % 2]
                slot = i % 8
                kb.do("tensor", (lambda e, sa=src_ap_fn(i), pa=bank.ap[:, slot * 128:(slot + 1) * 128]:
                                 e.transpose(pa, sa, ident_b.ap)),
                      reads=[src_buf, ident_b], writes=[bank], tag="tr")
                if scale_col is not None:
                    sc = scale_buf.ap[:, scale_col:scale_col + 1]
                    kb.do("vector", (lambda e, pa=bank.ap[:, slot * 128:(slot + 1) * 128],
                                     da=dst_buf.ap[:, dst_col0 + i * 128: dst_col0 + (i + 1) * 128],
                                     s=sc:
                                     e.tensor_scalar(da, pa, s, None, ALU.mult)),
                          reads=[bank, scale_buf], writes=[dst_buf], tag="trev")
                else:
                    kb.do("vector", (lambda e, pa=bank.ap[:, slot * 128:(slot + 1) * 128],
                                     da=dst_buf.ap[:, dst_col0 + i * 128: dst_col0 + (i + 1) * 128]:
                                     e.tensor_copy(da, pa)),
                          reads=[bank], writes=[dst_buf], tag="trev")

        arena.reset()
        mm_f32 = arena.sub("mm_f32", [128, 32 * 128], F32)

        v = kb.do
        for k, val in const_vals.items():
            v("vector", (lambda e, a=C[k].ap, x=val: e.memset(a, x)),
              writes=[C[k]], tag="const")
        v("gpsimd", lambda e: e.memset(ident_b.ap, 1.0), writes=[ident_b])
        v("gpsimd", lambda e: e.affine_select(
            ident_b.ap, ident_b.ap, pattern=[[-1, 128]],
            compare_op=ALU.is_ge, fill=0.0, base=0, channel_multiplier=1),
          reads=[ident_b], writes=[ident_b])
        v("gpsimd", lambda e: e.affine_select(
            ident_b.ap, ident_b.ap, pattern=[[1, 128]],
            compare_op=ALU.is_ge, fill=0.0, base=0, channel_multiplier=-1),
          reads=[ident_b], writes=[ident_b])
        v("gpsimd", lambda e: e.memset(ident_f.ap, 1.0), writes=[ident_f])
        v("gpsimd", lambda e: e.affine_select(
            ident_f.ap, ident_f.ap, pattern=[[-1, 128]],
            compare_op=ALU.is_ge, fill=0.0, base=0, channel_multiplier=1),
          reads=[ident_f], writes=[ident_f])
        v("gpsimd", lambda e: e.affine_select(
            ident_f.ap, ident_f.ap, pattern=[[1, 128]],
            compare_op=ALU.is_ge, fill=0.0, base=0, channel_multiplier=-1),
          reads=[ident_f], writes=[ident_f])
        v("vector", lambda e: e.memset(ones_col_b.ap, 1.0), writes=[ones_col_b])
        v("vector", lambda e: e.memset(ones_col_f.ap, 1.0), writes=[ones_col_f])
        v("vector", lambda e: e.memset(ones_row_f.ap, 1.0), writes=[ones_row_f])
        v("sync", lambda e: e.dma_start(out=cofs_sb.ap, in_=cofs),
          writes=[cofs_sb], dma=True)
        qn_sb = persist.sub("qn_sb", [128, 1], F32)
        kn_sb = persist.sub("kn_sb", [128, 1], F32)
        v("sync", lambda e: e.dma_start(out=qn_sb.ap, in_=qn_in),
          writes=[qn_sb], dma=True)
        v("sync", lambda e: e.dma_start(out=kn_sb.ap, in_=kn_in),
          writes=[kn_sb], dma=True)
        v("vector", lambda e: e.tensor_tensor(g_col.ap, qn_sb.ap, kn_sb.ap,
                                              ALU.mult),
          reads=[qn_sb, kn_sb], writes=[g_col])
        v("vector", lambda e: e.memset(hws.ap, 0.0), writes=[hws])
        v("vector", lambda e: e.memset(hneg.ap, 0.0), writes=[hneg])
        v("gpsimd", lambda e: e.iota(
            mm_f32.ap, pattern=[[128, 32], [-1, 128]], base=0,
            channel_multiplier=1, allow_small_or_imprecise_dtypes=True),
          writes=[mm_f32])
        v("vector", lambda e: e.tensor_scalar(
            mm_f32.ap, mm_f32.ap, cofs_sb.ap, None, ALU.subtract),
          reads=[mm_f32, cofs_sb], writes=[mm_f32])
        v("vector", lambda e: e.tensor_scalar(
            Mmask.ap, mm_f32.ap, C["zero"].ap, None, ALU.is_le),
          reads=[mm_f32, C["zero"]], writes=[Mmask])
        v("gpsimd", lambda e: e.iota(
            alt.ap, pattern=[[32, 2], [0, 32]], base=0, channel_multiplier=0,
            allow_small_or_imprecise_dtypes=True),
          writes=[alt])
        v("gpsimd", lambda e: e.iota(
            alt.ap, pattern=[[1, 64]], base=0, channel_multiplier=0,
            allow_small_or_imprecise_dtypes=True),
          writes=[alt])
        alt_h = persist.sub("alt_h", [128, 64], F32)
        v("vector", lambda e: e.tensor_scalar(
            alt_h.ap, alt.ap, C["half"].ap, C["magic"].ap, ALU.mult, ALU.add),
          reads=[alt, C["half"], C["magic"]], writes=[alt_h])
        v("vector", lambda e: e.tensor_scalar(
            alt_h.ap, alt_h.ap, C["magic"].ap, None, ALU.subtract),
          reads=[alt_h, C["magic"]], writes=[alt_h])
        v("vector", lambda e: e.tensor_scalar(
            alt_h.ap, alt.ap, C["half"].ap, None, ALU.mult),
          reads=[alt, C["half"]], writes=[alt_h])
        qtr = persist.sub("qtr", [128, 1], F32)
        v("vector", lambda e: e.memset(qtr.ap, 0.25), writes=[qtr])
        v("vector", lambda e: e.tensor_scalar(
            alt_h.ap, alt_h.ap, qtr.ap, C["magic"].ap, ALU.subtract, ALU.add),
          reads=[alt_h, qtr, C["magic"]], writes=[alt_h])
        v("vector", lambda e: e.tensor_scalar(
            alt_h.ap, alt_h.ap, C["magic"].ap, None, ALU.subtract),
          reads=[alt_h, C["magic"]], writes=[alt_h])
        two = persist.sub("two", [128, 1], F32)
        v("vector", lambda e: e.memset(two.ap, 2.0), writes=[two])
        v("vector", lambda e: e.tensor_scalar(
            alt_h.ap, alt_h.ap, two.ap, None, ALU.mult),
          reads=[alt_h, two], writes=[alt_h])
        v("vector", lambda e: e.tensor_tensor(
            alt.ap, alt.ap, alt_h.ap, ALU.subtract),
          reads=[alt, alt_h], writes=[alt])
        v("vector", lambda e: e.tensor_scalar(
            alt.ap, alt.ap, two.ap, C["one"].ap, ALU.mult, ALU.subtract),
          reads=[alt, two, C["one"]], writes=[alt])
        v("gpsimd", lambda e: e.iota(
            IF32.ap, pattern=[[1, 32]], base=0, channel_multiplier=0,
            allow_small_or_imprecise_dtypes=True),
          writes=[IF32])
        v("scalar", lambda e: e.activation(
            IF32.ap, IF32.ap, AF.Exp, scale=C["lnb"].ap),
          reads=[IF32, C["lnb"]], writes=[IF32])

        wtile = [arena.sub(f"wtile{i}", [128, 2048], F32) for i in range(2)]
        wout_b = [arena.sub(f"wtern{i}", [128, 2048], BF16) for i in range(2)]
        sg1 = arena.sub("sg1", [128, 2048], BF16)
        wacc = arena.sub("wacc", [128, 1], F32)
        wpart = arena.sub("wpart", [128, 1], F32)
        wsc = arena.sub("wsc", [1, 1], F32)

        def w_tiles(nm):
            K, N = [(k, n) for w, k, n in WEIGHTS if w == nm][0]
            ncol = min(N, 2048)
            out = []
            for kc in range(K // 128):
                for ncc in range(N // ncol):
                    out.append((kc, ncc, ncol))
            return out, K, N

        def w_pipeline(weights, wtile, wout_b, sg1, wacc, wpart, wsc, ar, wbank):
            for nm, K, N in weights:
                tiles, _, _ = w_tiles(nm)
                memset_step(wacc, 0.0)
                wi = WI[nm]
                for i, (kc, ncc, ncol) in enumerate(tiles):
                    b = wtile[i % 2]
                    srcw = w_in[nm][kc * 128:(kc + 1) * 128,
                                    ncc * ncol:(ncc + 1) * ncol]
                    kb.do("sync", (lambda e, d=b.ap[:, 0:ncol], s=srcw:
                                   e.dma_start(out=d, in_=s)),
                          writes=[b], dma=True, tag=f"w1ld_{nm}")
                    kb.do("vector", (lambda e, s=b.ap[:, 0:ncol]:
                                     e.tensor_reduce(wpart.ap, s, AX.X, ALU.add,
                                                     apply_absolute_value=True)),
                          reads=[b], writes=[wpart])
                    kb.do("vector", lambda e: e.tensor_tensor(
                        wacc.ap, wacc.ap, wpart.ap, ALU.add),
                        reads=[wacc, wpart], writes=[wacc])
                    yield
                kb.do("tensor", (lambda e, pb=wbank.ap: e.matmul(
                    pb[0:1, 0:1], ones_col_f.ap, wacc.ap,
                    start=True, stop=True)),
                    reads=[ones_col_f, wacc], writes=[wbank])
                inv_numel = float(np.float32(1.0 / (K * N)))
                cnum = ar.sub(f"cnum_{nm}", [1, 1], F32)
                kb.do("vector", (lambda e, a2=cnum.ap, x=inv_numel: e.memset(a2, x)),
                      writes=[cnum])
                kb.do("vector", (lambda e, cn=cnum.ap[0:1, :], pb=wbank.ap:
                                 e.tensor_scalar(
                    wsc.ap, pb[0:1, 0:1], cn, None, ALU.mult)),
                    reads=[wbank, cnum], writes=[wsc])
                kb.do("vector", lambda e: e.tensor_scalar(
                    wsc.ap, wsc.ap, C["eps5"].ap[0:1, :], None, ALU.max),
                    reads=[wsc, C["eps5"]], writes=[wsc])
                kb.do("vector", lambda e: e.tensor_scalar(
                    wsc.ap, wsc.ap, C["half"].ap[0:1, :], None, ALU.mult),
                    reads=[wsc, C["half"]], writes=[wsc])
                kb.do("tensor", (lambda e, pb=wbank.ap: e.matmul(
                    pb[:, 0:1], ones_row_f.ap[0:1, :], wsc.ap,
                    start=True, stop=True)),
                    reads=[ones_row_f, wsc], writes=[wbank])
                kb.do("vector", (lambda e, d=hws.ap[:, wi:wi + 1],
                                 pb=wbank.ap:
                                 e.tensor_copy(d, pb[:, 0:1])),
                      reads=[wbank], writes=[hws])
                kb.do("vector", (lambda e, d=hneg.ap[:, wi:wi + 1],
                                 s=hws.ap[:, wi:wi + 1]:
                                 e.tensor_scalar(d, s, C["neg1"].ap, None, ALU.mult)),
                      reads=[hws, C["neg1"]], writes=[hneg])
                yield
                for i, (kc, ncc, ncol) in enumerate(tiles):
                    b = wtile[i % 2]
                    ob = wout_b[i % 2]
                    srcw = w_in[nm][kc * 128:(kc + 1) * 128,
                                    ncc * ncol:(ncc + 1) * ncol]
                    dst = tern[nm][kc * 128:(kc + 1) * 128,
                                   ncc * ncol:(ncc + 1) * ncol]
                    kb.do("sync", (lambda e, d=b.ap[:, 0:ncol], s=srcw:
                                   e.dma_start(out=d, in_=s)),
                          writes=[b], dma=True, tag=f"w2ld_{nm}")
                    kb.do("scalar", (lambda e, s=b.ap[:, 0:ncol],
                                     d=sg1.ap[:, 0:ncol],
                                     hn=hneg.ap[:, wi:wi + 1]:
                                     e.activation(d, s, AF.Sign, bias=hn)),
                          reads=[b, hneg], writes=[sg1])
                    kb.do("scalar", (lambda e, s=b.ap[:, 0:ncol],
                                     d=ob.ap[:, 0:ncol],
                                     hp=hws.ap[:, wi:wi + 1]:
                                     e.activation(d, s, AF.Sign, bias=hp)),
                          reads=[b, hws], writes=[ob])
                    kb.do("vector", (lambda e, d=ob.ap[:, 0:ncol],
                                     s1=sg1.ap[:, 0:ncol]:
                                     e.tensor_tensor(d, d, s1, ALU.add)),
                          reads=[ob, sg1], writes=[ob])
                    kb.do("gpsimd", (lambda e, s=ob.ap[:, 0:ncol], d=dst:
                                     e.dma_start(out=d, in_=s)),
                          reads=[ob], dma=True, tag=f"w2st_{nm}")
                    yield

        W_EARLY = [w for w in WEIGHTS if w[0] in
                   ("wqc", "wqr", "wkvd", "wku", "wvu", "wkr")]
        W_LATE = [w for w in WEIGHTS if w[0] in ("wout", "wg", "wu", "wd")]
        for _ in w_pipeline(W_EARLY, wtile, wout_b, sg1, wacc, wpart, wsc,
                            arena, PS[6]):
            pass

        if debug:
            kb.do("gpsimd", (lambda e: e.dma_start(out=dbg["hws"], in_=hws.ap)),
                  reads=[hws], dma=True)
            for ch in range(16):
                bb = wout_b[0]
                kb.do("sync", (lambda e, d=bb.ap[:, 0:512],
                               s=tern["wkvd"][ch * 128:(ch + 1) * 128, :]:
                               e.dma_start(out=d, in_=s)),
                      writes=[bb], dma=True)
                kb.do("gpsimd", (lambda e, s=bb.ap[:, 0:512],
                                 d=dbg["tern"][ch * 128:(ch + 1) * 128, :]:
                                 e.dma_start(out=d, in_=s)),
                      reads=[bb], dma=True)

        kb.barrier()

        arena.reset()
        wkvd_sb = arena.sub("wkvd_sb", [128, 16 * LAT], BF16)
        wku_sb = arena.sub("wku_sb", [128, 4 * D], BF16)
        wvu_sb = arena.sub("wvu_sb", [128, 4 * D], BF16)
        wkr_sb = arena.sub("wkr_sb", [128, 4 * H * DR], BF16)
        cosT = arena.sub("cosT", [128, NT * 32], BF16)
        sinT = arena.sub("sinT", [128, NT * 32], BF16)
        xtile = [arena.sub(f"xtile{i}", [128, D], F32) for i in range(2)]
        u_scr2 = [arena.sub(f"u_scr{i}", [128, D], F32) for i in range(2)]
        xq = [arena.sub(f"xq{i}", [128, D], BF16) for i in range(2)]
        xqT = [arena.sub(f"xqT{i}", [128, D], BF16) for i in range(2)]
        latq = [arena.sub(f"latq{i}", [128, LAT], BF16) for i in range(2)]
        latqT = [arena.sub(f"latqT{i}", [128, LAT], BF16) for i in range(2)]
        kcraw = [arena.sub(f"kcraw{i}", [128, D], BF16) for i in range(2)]
        krf2 = [arena.sub(f"krf{i}", [128, H * DR], F32) for i in range(2)]
        krb = [arena.sub(f"krb{i}", [128, H * DR], BF16) for i in range(2)]
        krTs = [arena.sub("krTs0", [128, H * DR], BF16)] * 2
        kcs = [arena.sub(f"kcs{i}", [128, D], BF16) for i in range(2)]
        vst = [arena.sub(f"vst{i}", [128, D], BF16) for i in range(2)]
        sc_mx = [arena.sub(f"sc_mx{i}", [128, 1], F32) for i in range(2)]
        sc_s = [arena.sub(f"sc_s{i}", [128, 1], F32) for i in range(2)]
        invsx = [arena.sub(f"invsx{i}", [128, 1], F32) for i in range(2)]
        invsl = [arena.sub(f"invsl{i}", [128, 1], F32) for i in range(2)]
        dq_sc = [arena.sub(f"dq_sc{i}", [128, 4], F32) for i in range(2)]
        ss16_2 = [arena.sub(f"ss16_{i}", [128, 16], F32) for i in range(2)]
        cs16_2 = [arena.sub(f"cs16_{i}", [128, 16], F32) for i in range(2)]

        def emit_trig(kt_or_none, tile_j, cos_dst, sin_dst, pos_reads=()):
            v = kb.do
            if kt_or_none is not None:
                base = float(128 * kt_or_none)
                v("gpsimd", (lambda e, b=base: e.iota(
                    tv.ap, pattern=[[0, 1]], base=0, channel_multiplier=1,
                    allow_small_or_imprecise_dtypes=True)),
                  writes=[tv])
                cb = arena_cb[kt_or_none % 2]
                v("vector", (lambda e, b=base: e.memset(cb.ap, b)), writes=[cb])
                v("vector", (lambda e, cba=cb.ap: e.tensor_scalar(
                    tv.ap, tv.ap, cba, None, ALU.add)),
                  reads=[tv, cb], writes=[tv])
            else:
                base = float(1024 * tile_j)
                v("gpsimd", (lambda e: e.iota(
                    tv.ap, pattern=[[0, 1]], base=0, channel_multiplier=1,
                    allow_small_or_imprecise_dtypes=True)),
                  writes=[tv])
                cb = arena_cb[0]
                v("vector", (lambda e, b=base: e.memset(cb.ap, b)), writes=[cb])
                v("vector", (lambda e, cba=cb.ap: e.tensor_scalar(
                    tv.ap, tv.ap, cba, None, ALU.add)),
                  reads=[tv, cb], writes=[tv])
                v("vector", lambda e: e.tensor_scalar(
                    tv.ap, tv.ap, cofs_sb.ap, None, ALU.add),
                  reads=[tv, cofs_sb], writes=[tv])
            v("vector", lambda e: e.tensor_scalar(
                th.ap, IF32.ap, tv.ap, None, ALU.mult),
              reads=[IF32, tv], writes=[th])
            v("scalar", lambda e: e.activation(
                th2.ap, th.ap, AF.Identity, bias=C["magic"].ap,
                scale=C["i2pi"].ap),
              reads=[th, C["magic"], C["i2pi"]], writes=[th2])
            v("vector", lambda e: e.tensor_scalar(
                th2.ap, th2.ap, C["magic"].ap, None, ALU.subtract),
              reads=[th2, C["magic"]], writes=[th2])
            v("vector", lambda e: e.scalar_tensor_tensor(
                th2.ap, th2.ap, C["n2pi"].ap, th.ap, ALU.mult, ALU.add),
              reads=[th2, C["n2pi"], th], writes=[th2])
            v("vector", lambda e: e.tensor_scalar(
                th2.ap, th2.ap, C["pip"].ap, C["pin"].ap, ALU.min, ALU.max),
              reads=[th2, C["pip"], C["pin"]], writes=[th2])
            v("scalar", lambda e: e.activation(trig32.ap, th2.ap, AF.Sin),
              reads=[th2], writes=[trig32])
            v("vector", (lambda e, d=sin_dst.ap[:, 0:32]:
                         e.tensor_tensor(d, trig32.ap, alt.ap[:, 0:32],
                                         ALU.mult)),
              reads=[trig32, alt], writes=[sin_dst])
            v("scalar", lambda e: e.activation(th.ap, th2.ap, AF.Abs),
              reads=[th2], writes=[th])
            v("scalar", lambda e: e.activation(
                trig32.ap, th.ap, AF.Sin, bias=C["hpi"].ap, scale=C["neg1"].ap),
              reads=[th, C["hpi"], C["neg1"]], writes=[trig32])
            v("vector", (lambda e, d=cos_dst.ap[:, 0:32]:
                         e.tensor_copy(d, trig32.ap)),
              reads=[trig32], writes=[cos_dst])


        class TileView(Buf):
            pass

        cosT_t = [TileView(cosT.ap[:, kt * 32:(kt + 1) * 32]) for kt in range(NT)]
        sinT_t = [TileView(sinT.ap[:, kt * 32:(kt + 1) * 32]) for kt in range(NT)]
        for b in cosT_t:
            b.res = cosT.res
        for b in sinT_t:
            b.res = sinT.res

        if stage >= 1:
            kb.do("sync", lambda e: e.dma_start(
                out=wkvd_sb.ap.rearrange("p (a n) -> p a n", a=16),
                in_=tern["wkvd"].rearrange("(a p) n -> p a n", p=128)),
                writes=[wkvd_sb], dma=True)
            kb.do("sync", lambda e: e.dma_start(
                out=wku_sb.ap.rearrange("p (a n) -> p a n", a=4),
                in_=tern["wku"].rearrange("(a p) n -> p a n", p=128)),
                writes=[wku_sb], dma=True)
            kb.do("sync", lambda e: e.dma_start(
                out=wvu_sb.ap.rearrange("p (a n) -> p a n", a=4),
                in_=tern["wvu"].rearrange("(a p) n -> p a n", p=128)),
                writes=[wvu_sb], dma=True)
            kb.do("sync", lambda e: e.dma_start(
                out=wkr_sb.ap.rearrange("p (a n) -> p a n", a=4),
                in_=tern["wkr"].rearrange("(a p) n -> p a n", p=128)),
                writes=[wkr_sb], dma=True)
            for kt in range(NT):
                emit_trig(kt, None, cosT_t[kt], sinT_t[kt])

            def emit_A_h1(kt):
                par = kt % 2
                u_scr = u_scr2[par]
                krf = krf2[par]
                ss16 = ss16_2[par]
                cs16 = cs16_2[par]
                ropa = u_scr
                t1 = ropa.ap[:, 0:H * DR]
                t2 = ropa.ap[:, H * DR:2 * H * DR]
                xb, xqb, xqTb = xtile[par], xq[par], xqT[par]
                kb.do("sync", (lambda e, d=xb.ap,
                               s=x_full[kt * 128:(kt + 1) * 128, :]:
                               e.dma_start(out=d, in_=s)),
                      writes=[xb], dma=True, tag=f"A_x{kt}")
                quant(xb, xb.ap, xqb, invsx[par], 0, D,
                      sc_mx[par], sc_s[par], u_scr)
                if debug and kt == 0:
                    kb.do("gpsimd", (lambda e, s=xqb.ap, d=dbg["xq"]:
                                     e.dma_start(out=d, in_=s)),
                          reads=[xqb], dma=True)
                transposes(xqb, (lambda i, b=xqb: b.ap[:, i * 128:(i + 1) * 128]),
                           xqTb, 0, 16)
                lb = PS[2 + par]
                for ic in range(16):
                    kb.do("tensor", (lambda e, l=xqTb.ap[:, ic * 128:(ic + 1) * 128],
                                     r=wkvd_sb.ap[:, ic * 512:(ic + 1) * 512],
                                     st=(ic == 0), sp=(ic == 15), pb=lb.ap:
                                     e.matmul(pb, l, r, start=st, stop=sp)),
                          reads=[xqTb, wkvd_sb], writes=[lb], tag="A_lat")
                dqs = dq_sc[par]
                kb.do("vector", (lambda e, d=dqs.ap[:, 0:1],
                                 h=hws.ap[:, WI["wkvd"]:WI["wkvd"] + 1],
                                 ix=invsx[par].ap:
                                 e.tensor_scalar(d, h, ix, None, ALU.mult)),
                      reads=[hws, invsx[par]], writes=[dqs])
                mxb, sb = sc_mx[par], sc_s[par]
                kb.do("vector", (lambda e, pb=lb.ap: e.tensor_reduce(
                    mxb.ap, pb, AX.X, ALU.max, apply_absolute_value=True)),
                      reads=[lb], writes=[mxb])
                kb.do("vector", (lambda e, d=mxb.ap, q=dqs.ap[:, 0:1]:
                                 e.tensor_scalar(d, d, q, None, ALU.mult)),
                      reads=[mxb, dqs], writes=[mxb])
                kb.do("vector", lambda e, d=mxb.ap: e.tensor_scalar(
                    d, d, C["eps5"].ap, None, ALU.max),
                    reads=[mxb, C["eps5"]], writes=[mxb])
                kb.do("vector", (lambda e, d=invsl[par].ap, m=mxb.ap:
                                 e.tensor_scalar(d, m, C["i127"].ap, None, ALU.mult)),
                      reads=[mxb, C["i127"]], writes=[invsl[par]])
                kb.do("vector", (lambda e, d=sb.ap, m=mxb.ap:
                                 e.reciprocal(d, m)),
                      reads=[mxb], writes=[sb])
                kb.do("vector", (lambda e, d=sb.ap:
                                 e.tensor_scalar(d, d, C["c127"].ap, None, ALU.mult)),
                      reads=[sb, C["c127"]], writes=[sb])
                kb.do("vector", (lambda e, d=sb.ap, q=dqs.ap[:, 0:1]:
                                 e.tensor_scalar(d, d, q, None, ALU.mult)),
                      reads=[sb, dqs], writes=[sb])
                kb.do("scalar", (lambda e, pb=lb.ap, d=u_scr.ap[:, 0:512],
                                 s=sb.ap:
                                 e.activation(d, pb, AF.Identity,
                                              bias=C["magic"].ap, scale=s)),
                      reads=[lb, sb, C["magic"]], writes=[u_scr])
                kb.do("vector", (lambda e, d=latq[par].ap, s=u_scr.ap[:, 0:512]:
                                 e.tensor_scalar(d, s, C["magic"].ap, None,
                                                 ALU.subtract)),
                      reads=[u_scr, C["magic"]], writes=[latq[par]])
                if debug and kt == 0:
                    kb.do("gpsimd", (lambda e, s=latq[par].ap, d=dbg["latq"]:
                                     e.dma_start(out=d, in_=s)),
                          reads=[latq[par]], dma=True)
                    kb.do("gpsimd", (lambda e, s=invsx[par].ap,
                                     d=dbg["invs"][:, 0:1]:
                                     e.dma_start(out=d, in_=s,
                                                 allow_slow_non_contiguous=True)),
                          reads=[invsx[par]], dma=True)
                    kb.do("gpsimd", (lambda e, s=invsl[par].ap,
                                     d=dbg["invs"][:, 1:2]:
                                     e.dma_start(out=d, in_=s,
                                                 allow_slow_non_contiguous=True)),
                          reads=[invsl[par]], dma=True)
                transposes(latq[par],
                           (lambda i, b=latq[par]: b.ap[:, i * 128:(i + 1) * 128]),
                           latqT[par], 0, 4)

            def emit_A_h2(kt):
                par = kt % 2
                u_scr = u_scr2[par]
                krf = krf2[par]
                ss16 = ss16_2[par]
                cs16 = cs16_2[par]
                ropa = u_scr
                t1 = ropa.ap[:, 0:H * DR]
                t2 = ropa.ap[:, H * DR:2 * H * DR]
                lb = PS[2 + par]
                dqs = dq_sc[par]
                for wi_name, col in (("wku", 1), ("wvu", 2), ("wkr", 3)):
                    kb.do("vector", (lambda e, d=dqs.ap[:, col:col + 1],
                                     h=hws.ap[:, WI[wi_name]:WI[wi_name] + 1],
                                     il=invsl[par].ap:
                                     e.tensor_scalar(d, h, il, None, ALU.mult)),
                          reads=[hws, invsl[par]], writes=[dqs])
                for oc in range(4):
                    bank = PS[4 + (oc % 2)]
                    for lc in range(4):
                        kb.do("tensor", (lambda e,
                                         l=latqT[par].ap[:, lc * 128:(lc + 1) * 128],
                                         r=wku_sb.ap[:, lc * D + oc * 512: lc * D + (oc + 1) * 512],
                                         st=(lc == 0), sp=(lc == 3), pb=bank.ap:
                                         e.matmul(pb, l, r, start=st, stop=sp)),
                              reads=[latqT[par], wku_sb], writes=[bank], tag="A_kc")
                    kb.do("vector", (lambda e, pb=bank.ap,
                                     d=kcraw[par].ap[:, oc * 512:(oc + 1) * 512]:
                                     e.tensor_copy(d, pb)),
                          reads=[bank], writes=[kcraw[par]])
                if debug and kt == 0:
                    kb.do("gpsimd", (lambda e, s=kcraw[par].ap, d=dbg["kcraw"]:
                                     e.dma_start(out=d, in_=s)),
                          reads=[kcraw[par]], dma=True)
                kb.do("scalar", (lambda e, s=kcraw[par].ap, d=u_scr.ap:
                                 e.activation(d, s, AF.Square)),
                      reads=[kcraw[par]], writes=[u_scr])
                kb.do("vector", (lambda e, d=ss16.ap,
                                 s=u_scr.ap.rearrange("p (h d) -> p h d", h=16):
                                 e.tensor_reduce(d, s, AX.X, ALU.add)),
                      reads=[u_scr], writes=[ss16])
                kb.do("vector", (lambda e, q=dqs.ap[:, 1:2], d=dqs.ap[:, 0:1]:
                                 e.tensor_scalar(d, q, q, None, ALU.mult)),
                      reads=[dqs], writes=[dqs])
                kb.do("vector", (lambda e, q=dqs.ap[:, 0:1], d=ss16.ap:
                                 e.tensor_scalar(d, d, q, None, ALU.mult)),
                      reads=[ss16, dqs], writes=[ss16])
                kb.do("scalar", (lambda e, d=ss16.ap: e.activation(
                    d, d, AF.Sqrt, bias=C["eps6"].ap, scale=C["i128"].ap)),
                    reads=[ss16, C["eps6"], C["i128"]], writes=[ss16])
                kb.do("vector", (lambda e, d=cs16.ap, s=ss16.ap:
                                 e.reciprocal(d, s)),
                      reads=[ss16], writes=[cs16])
                kb.do("vector", (lambda e, q=dqs.ap[:, 1:2], d=cs16.ap:
                                 e.tensor_scalar(d, d, q, None, ALU.mult)),
                      reads=[cs16, dqs], writes=[cs16])
                for h in range(16):
                    kb.do("vector", (lambda e,
                                     s=kcraw[par].ap[:, h * 128:(h + 1) * 128],
                                     d=kcs[par].ap[:, h * 128:(h + 1) * 128],
                                     c=cs16.ap[:, h:h + 1]:
                                     e.tensor_scalar(d, s, c, None, ALU.mult)),
                          reads=[kcraw[par], cs16], writes=[kcs[par]])
                kb.do("gpsimd", (lambda e, s=kcs[par].ap,
                                 d=Kc_hbm[kt * 128:(kt + 1) * 128, :]:
                                 e.dma_start(out=d, in_=s)),
                      reads=[kcs[par]], dma=True, tag=f"A_kcst{kt}")
                if debug:
                    kb.do("gpsimd", (lambda e, s=kcs[par].ap,
                                     d=dbg["kc"][kt * 128:(kt + 1) * 128, :]:
                                     e.dma_start(out=d, in_=s)),
                          reads=[kcs[par]], dma=True)
                for oc in range(4):
                    bank = PS[4 + (oc % 2)]
                    for lc in range(4):
                        kb.do("tensor", (lambda e,
                                         l=latqT[par].ap[:, lc * 128:(lc + 1) * 128],
                                         r=wvu_sb.ap[:, lc * D + oc * 512: lc * D + (oc + 1) * 512],
                                         st=(lc == 0), sp=(lc == 3), pb=bank.ap:
                                         e.matmul(pb, l, r, start=st, stop=sp)),
                              reads=[latqT[par], wvu_sb], writes=[bank], tag="A_v")
                    kb.do("vector", (lambda e, pb=bank.ap,
                                     d=vst[par].ap[:, oc * 512:(oc + 1) * 512],
                                     q=dq_sc[par].ap[:, 2:3]:
                                     e.tensor_scalar(d, pb, q, None, ALU.mult)),
                          reads=[bank, dq_sc[par]], writes=[vst[par]])
                kb.do("gpsimd", (lambda e, s=vst[par].ap,
                                 d=V_hbm[kt * 128:(kt + 1) * 128, :]:
                                 e.dma_start(out=d, in_=s)),
                      reads=[vst[par]], dma=True, tag=f"A_vst{kt}")
                if debug:
                    kb.do("gpsimd", (lambda e, s=vst[par].ap,
                                     d=dbg["v"][kt * 128:(kt + 1) * 128, :]:
                                     e.dma_start(out=d, in_=s)),
                          reads=[vst[par]], dma=True)
                for oc in range(2):
                    bank = PS[4 + (oc % 2)]
                    for lc in range(4):
                        kb.do("tensor", (lambda e,
                                         l=latqT[par].ap[:, lc * 128:(lc + 1) * 128],
                                         r=wkr_sb.ap[:, lc * (H * DR) + oc * 512: lc * (H * DR) + (oc + 1) * 512],
                                         st=(lc == 0), sp=(lc == 3), pb=bank.ap:
                                         e.matmul(pb, l, r, start=st, stop=sp)),
                              reads=[latqT[par], wkr_sb], writes=[bank], tag="A_kr")
                    kb.do("vector", (lambda e, pb=bank.ap,
                                     d=krf.ap[:, oc * 512:(oc + 1) * 512],
                                     q=dq_sc[par].ap[:, 3:4]:
                                     e.tensor_scalar(d, pb, q, None, ALU.mult)),
                          reads=[bank, dq_sc[par]], writes=[krf])
                c3 = cosT_t[kt].ap.unsqueeze(1).unsqueeze(1).broadcast_to(
                    [128, 16, 2, 32])
                s3 = sinT_t[kt].ap.unsqueeze(1).unsqueeze(1).broadcast_to(
                    [128, 16, 2, 32])
                kr3 = krf.ap.rearrange("p (h r d) -> p h r d", h=16, r=2)
                krsw = krf.ap.rearrange("p (a b) -> p a b", b=2)[:, :, ::-1]
                kb.do("vector", (lambda e, tt=t1, aa=kr3, cc=c3: e.tensor_tensor(
                    tt.rearrange("p (h r d) -> p h r d", h=16, r=2), aa, cc,
                    ALU.mult)),
                      reads=[krf, cosT], writes=[ropa])
                kb.do("vector", (lambda e, tt=t2, aa=krsw, ss=s3: e.tensor_tensor(
                    tt.rearrange("p (h r d) -> p h r d", h=16, r=2), aa, ss,
                    ALU.mult)),
                      reads=[krf, sinT], writes=[ropa])
                kb.do("vector", (lambda e, d=krb[par].ap, a=t1, b=t2:
                                 e.tensor_tensor(d, a, b, ALU.add)),
                      reads=[ropa], writes=[krb[par]])
                if debug and kt == 2:
                    kb.do("gpsimd", (lambda e: e.dma_start(
                        out=dbg["trig"][:, 0:32], in_=cosT_t[2].ap)),
                          reads=[cosT], dma=True)
                    kb.do("gpsimd", (lambda e: e.dma_start(
                        out=dbg["trig"][:, 64:96], in_=sinT_t[2].ap)),
                          reads=[sinT], dma=True)
                    kb.do("gpsimd", (lambda e: e.dma_start(
                        out=dbg["trig"][:, 128:192], in_=alt.ap)),
                          reads=[alt], dma=True)
                    kb.do("gpsimd", (lambda e, s=krf.ap: e.dma_start(
                        out=dbg["krf"], in_=s)),
                          reads=[krf], dma=True)
                    kb.do("gpsimd", (lambda e, s=krb[par].ap: e.dma_start(
                        out=dbg["krb"], in_=s)),
                          reads=[krb[par]], dma=True)
                transposes(krb[par],
                           (lambda i, b=krb[par]: b.ap[:, i * 128:(i + 1) * 128]),
                           krTs[par], 0, 8)
                kb.do("gpsimd", (lambda e,
                                 s=krTs[par].ap.rearrange("p (a n) -> p a n", a=8),
                                 d=Kr_hbm.rearrange("h d t -> (h d) t")[
                                     :, kt * 128:(kt + 1) * 128].rearrange(
                                     "(a p) n -> p a n", p=128):
                                 e.dma_start(out=d, in_=s)),
                      reads=[krTs[par]], dma=True, tag=f"A_krst{kt}")
                if debug:
                    kb.do("gpsimd", (lambda e,
                                     s=krTs[par].ap.rearrange("p (a n) -> p a n", a=8),
                                     d=dbg["kr"].rearrange("h d t -> (h d) t")[
                                         :, kt * 128:(kt + 1) * 128].rearrange(
                                         "(a p) n -> p a n", p=128):
                                     e.dma_start(out=d, in_=s)),
                          reads=[krTs[par]], dma=True)


            for kt in range(NT + 1):
                if kt < NT:
                    emit_A_h1(kt)
                if kt >= 1:
                    emit_A_h2(kt - 1)
        kb.barrier()

        arena.reset()
        xqT_all = arena.sub("xqT_all", [128, NOWN * D], BF16)
        qcraw = arena.sub("qcraw", [128, NOWN * D], BF16)
        qrraw = arena.sub("qrraw", [128, NOWN * H * DR], BF16)
        wch = [arena.sub(f"wch{i}", [128, 16 * 512], BF16) for i in range(2)]
        xtileB = arena.sub("xtileB", [128, D], F32)
        u_scrB = arena.sub("u_scrB", [128, D], F32)
        xqB = arena.sub("xqB", [128, D], BF16)
        invsxB = arena.sub("invsxB", [128, NOWN], F32)
        mxB = arena.sub("mxB", [128, 1], F32)
        sB = arena.sub("sB", [128, 1], F32)
        ss16B = arena.sub("ss16B", [128, 16], F32)
        cs16B = arena.sub("cs16B", [128, 16], F32)
        dqB = arena.sub("dqB", [128, 2], F32)
        qrf = arena.sub("qrf", [128, H * DR], F32)
        ropaB = arena.sub("ropaB", [128, 2 * H * DR], F32)
        qrbB = arena.sub("qrbB", [128, H * DR], BF16)
        qstg = arena.sub("qstg", [128, D], BF16)
        qrstg = arena.sub("qrstg", [128, H * DR], BF16)
        cosB = arena.sub("cosB", [128, 32], F32)
        sinB = arena.sub("sinB", [128, 32], F32)
        tvB = arena.sub("tvB", [128, 1], F32)

        if stage >= 2:
            for j in range(NOWN):
                kb.do("sync", (lambda e, d=xtileB.ap,
                               s=x_own[j * 128:(j + 1) * 128, :]:
                               e.dma_start(out=d, in_=s)),
                      writes=[xtileB], dma=True, tag=f"B_x{j}")
                quant(xtileB, xtileB.ap, xqB, invsxB, j, D, mxB, sB, u_scrB)
                transposes(xqB, (lambda i, b=xqB: b.ap[:, i * 128:(i + 1) * 128]),
                           xqT_all, j * D, 16)
            for oc in range(4):
                wb = wch[oc % 2]
                kb.do("sync", (lambda e, d=wb.ap.rearrange("p (a n) -> p a n", a=16),
                               s=tern["wqc"][:, oc * 512:(oc + 1) * 512].rearrange(
                                   "(a p) n -> p a n", p=128):
                               e.dma_start(out=d, in_=s)),
                      writes=[wb], dma=True, tag=f"B_wqc{oc}")
                for j in range(NOWN):
                    bank = PS[2 + (j % 2)]
                    for ic in range(16):
                        kb.do("tensor", (lambda e,
                                         l=xqT_all.ap[:, j * D + ic * 128: j * D + (ic + 1) * 128],
                                         r=wb.ap[:, ic * 512:(ic + 1) * 512],
                                         st=(ic == 0), sp=(ic == 15), pb=bank.ap:
                                         e.matmul(pb, l, r, start=st, stop=sp)),
                              reads=[xqT_all, wb], writes=[bank], tag="B_qc")
                    kb.do("vector", (lambda e, pb=bank.ap,
                                     d=qcraw.ap[:, j * D + oc * 512: j * D + (oc + 1) * 512]:
                                     e.tensor_copy(d, pb)),
                          reads=[bank], writes=[qcraw])
            for oc in range(2):
                wb = wch[oc % 2]
                kb.do("sync", (lambda e, d=wb.ap.rearrange("p (a n) -> p a n", a=16),
                               s=tern["wqr"][:, oc * 512:(oc + 1) * 512].rearrange(
                                   "(a p) n -> p a n", p=128):
                               e.dma_start(out=d, in_=s)),
                      writes=[wb], dma=True, tag=f"B_wqr{oc}")
                for j in range(NOWN):
                    bank = PS[2 + (j % 2)]
                    for ic in range(16):
                        kb.do("tensor", (lambda e,
                                         l=xqT_all.ap[:, j * D + ic * 128: j * D + (ic + 1) * 128],
                                         r=wb.ap[:, ic * 512:(ic + 1) * 512],
                                         st=(ic == 0), sp=(ic == 15), pb=bank.ap:
                                         e.matmul(pb, l, r, start=st, stop=sp)),
                              reads=[xqT_all, wb], writes=[bank], tag="B_qr")
                    kb.do("vector", (lambda e, pb=bank.ap,
                                     d=qrraw.ap[:, j * (H * DR) + oc * 512: j * (H * DR) + (oc + 1) * 512]:
                                     e.tensor_copy(d, pb)),
                          reads=[bank], writes=[qrraw])
            for j in range(NOWN):
                kb.do("vector", (lambda e, d=dqB.ap[:, 0:1],
                                 h=hws.ap[:, WI["wqc"]:WI["wqc"] + 1],
                                 ix=invsxB.ap[:, j:j + 1]:
                                 e.tensor_scalar(d, h, ix, None, ALU.mult)),
                      reads=[hws, invsxB], writes=[dqB])
                kb.do("vector", (lambda e, d=dqB.ap[:, 1:2],
                                 h=hws.ap[:, WI["wqr"]:WI["wqr"] + 1],
                                 ix=invsxB.ap[:, j:j + 1]:
                                 e.tensor_scalar(d, h, ix, None, ALU.mult)),
                      reads=[hws, invsxB], writes=[dqB])
                qc_j = qcraw.ap[:, j * D:(j + 1) * D]
                kb.do("scalar", (lambda e, s=qc_j, d=u_scrB.ap:
                                 e.activation(d, s, AF.Square)),
                      reads=[qcraw], writes=[u_scrB])
                kb.do("vector", lambda e: e.tensor_reduce(
                    ss16B.ap, u_scrB.ap.rearrange("p (h d) -> p h d", h=16),
                    AX.X, ALU.add),
                    reads=[u_scrB], writes=[ss16B])
                kb.do("vector", (lambda e, q=dqB.ap[:, 0:1], d=mxB.ap:
                                 e.tensor_scalar(d, q, q, None, ALU.mult)),
                      reads=[dqB], writes=[mxB])
                kb.do("vector", (lambda e, q=mxB.ap:
                                 e.tensor_scalar(ss16B.ap, ss16B.ap, q, None,
                                                 ALU.mult)),
                      reads=[ss16B, mxB], writes=[ss16B])
                kb.do("scalar", lambda e: e.activation(
                    ss16B.ap, ss16B.ap, AF.Sqrt, bias=C["eps6"].ap,
                    scale=C["i128"].ap),
                    reads=[ss16B, C["eps6"], C["i128"]], writes=[ss16B])
                kb.do("vector", lambda e: e.reciprocal(cs16B.ap, ss16B.ap),
                      reads=[ss16B], writes=[cs16B])
                kb.do("vector", (lambda e, q=dqB.ap[:, 0:1]:
                                 e.tensor_scalar(cs16B.ap, cs16B.ap, q, None,
                                                 ALU.mult)),
                      reads=[cs16B, dqB], writes=[cs16B])
                for h in range(16):
                    kb.do("vector", (lambda e,
                                     s=qcraw.ap[:, j * D + h * 128: j * D + (h + 1) * 128],
                                     c=cs16B.ap[:, h:h + 1]:
                                     e.tensor_scalar(s, s, c, None, ALU.mult)),
                          reads=[qcraw, cs16B], writes=[qcraw])
                transposes(qcraw,
                           (lambda i, jj=j: qcraw.ap[:, jj * D + i * 128: jj * D + (i + 1) * 128]),
                           qstg, 0, 16, scale_col=0, scale_buf=g_col)
                kb.do("gpsimd", (lambda e, s=qstg.ap.rearrange("p (a n) -> p a n", a=16),
                                 d=Qc_hbm[:, j * 128:(j + 1) * 128].rearrange(
                                     "(a p) n -> p a n", p=128):
                                 e.dma_start(out=d, in_=s)),
                      reads=[qstg], dma=True, tag=f"B_qcst{j}")
                if debug:
                    kb.do("gpsimd", (lambda e, s=qstg.ap.rearrange("p (a n) -> p a n", a=16),
                                     d=dbg["qc"][:, j * 128:(j + 1) * 128].rearrange(
                                         "(a p) n -> p a n", p=128):
                                     e.dma_start(out=d, in_=s)),
                          reads=[qstg], dma=True)
                qr_j = qrraw.ap[:, j * H * DR:(j + 1) * H * DR]
                kb.do("vector", (lambda e, s=qr_j, d=qrf.ap, q=dqB.ap[:, 1:2]:
                                 e.tensor_scalar(d, s, q, None, ALU.mult)),
                      reads=[qrraw, dqB], writes=[qrf])
                emit_trig(None, j, cosB, sinB)
                c3 = cosB.ap.unsqueeze(1).unsqueeze(1).broadcast_to(
                    [128, 16, 2, 32])
                s3 = sinB.ap.unsqueeze(1).unsqueeze(1).broadcast_to(
                    [128, 16, 2, 32])
                qr3 = qrf.ap.rearrange("p (h r d) -> p h r d", h=16, r=2)
                qrsw = qrf.ap.rearrange("p (a b) -> p a b", b=2)[:, :, ::-1]
                t1 = ropaB.ap[:, 0:H * DR]
                t2 = ropaB.ap[:, H * DR:2 * H * DR]
                kb.do("vector", (lambda e, tt=t1, aa=qr3, cc=c3: e.tensor_tensor(
                    tt.rearrange("p (h r d) -> p h r d", h=16, r=2), aa, cc,
                    ALU.mult)),
                      reads=[qrf, cosB], writes=[ropaB])
                kb.do("vector", (lambda e, tt=t2, aa=qrsw, ss=s3: e.tensor_tensor(
                    tt.rearrange("p (h r d) -> p h r d", h=16, r=2), aa, ss,
                    ALU.mult)),
                      reads=[qrf, sinB], writes=[ropaB])
                kb.do("vector", (lambda e, d=qrbB.ap, a=t1, b=t2:
                                 e.tensor_tensor(d, a, b, ALU.add)),
                      reads=[ropaB], writes=[qrbB])
                transposes(qrbB,
                           (lambda i: qrbB.ap[:, i * 128:(i + 1) * 128]),
                           qrstg, 0, 8)
                kb.do("gpsimd", (lambda e,
                                 s=qrstg.ap.rearrange("p (a n) -> p a n", a=8),
                                 d=Qr_hbm.rearrange("h d t -> (h d) t")[
                                     :, j * 128:(j + 1) * 128].rearrange(
                                     "(a p) n -> p a n", p=128):
                                 e.dma_start(out=d, in_=s)),
                      reads=[qrstg], dma=True, tag=f"B_qrst{j}")
                if debug:
                    kb.do("gpsimd", (lambda e,
                                     s=qrstg.ap.rearrange("p (a n) -> p a n", a=8),
                                     d=dbg["qr"].rearrange("h d t -> (h d) t")[
                                         :, j * 128:(j + 1) * 128].rearrange(
                                         "(a p) n -> p a n", p=128):
                                     e.dma_start(out=d, in_=s)),
                          reads=[qrstg], dma=True)

        kb.barrier()

        arena.reset()
        u_fm = arena.sub("u_fm", [128, H * NOWN * 128], BF16)
        lrow = arena.sub("lrow", [1, 512], F32)
        invl_pre = arena.sub("invl_pre", [128, NOWN * 16], F32)
        qc_h = [arena.sub(f"qc_h{i}", [128, NOWN * 128], BF16) for i in range(2)]
        qr_h = [arena.sub(f"qr_h{i}", [65, NOWN * 128], BF16) for i in range(2)]
        kcb = [arena.sub(f"kcb{i}", [128, 512], BF16) for i in range(2)]
        krab = [arena.sub(f"krab{i}", [65, 512], BF16) for i in range(2)]
        vb = [arena.sub(f"vb{i}", [128, 512], BF16) for i in range(2)]
        ptst = [arena.sub(f"ptst{i}", [128, 512], BF16) for i in range(2)]
        sct_scr = arena.sub("sct_scr", [128, 256], F32)
        wtileC = [arena.sub(f"wtileC{i}", [128, 2048], F32) for i in range(2)]
        woutC = [arena.sub(f"woutC{i}", [128, 2048], BF16) for i in range(2)]
        sg1C = arena.sub("sg1C", [128, 2048], BF16)
        waccC = arena.sub("waccC", [128, 1], F32)
        wpartC = arena.sub("wpartC", [128, 1], F32)
        wscC = arena.sub("wscC", [1, 1], F32)
        mneg = arena.sub("mneg", [128, NOWN], F32)
        mnegb = arena.sub("mnegb", [128, 1], BF16)
        invl = arena.sub("invl", [128, NOWN * 16], F32)
        ltr = arena.sub("ltr", [128, 16], F32)

        EXT = [32 * (j // 4) + 32 for j in range(NOWN)]
        UP = [16 * p + 16 for p in range(4)]

        if stage >= 3:
            wgen = w_pipeline(W_LATE, wtileC, woutC, sg1C, waccC, wpartC,
                              wscC, arena, PS[7])
            wgen_live = [True]

            def pump_w(n):
                for _ in range(n):
                    try:
                        next(wgen)
                    except StopIteration:
                        wgen_live[0] = False
                        return

            for i in range(2):
                kb.do("vector", (lambda e, a=krab[i].ap[64:65, :]:
                                 e.memset(a, 1.0)),
                      writes=[krab[i]], tag="ones_r")
            for h in range(H):
                hp = h % 2
                qcb, qrb = qc_h[hp], qr_h[hp]
                kb.do("sync", (lambda e, d=qcb.ap,
                               s=Qc_hbm[h * 128:(h + 1) * 128, :]:
                               e.dma_start(out=d, in_=s)),
                      writes=[qcb], dma=True, tag=f"C_qc{h}")
                kb.do("sync", (lambda e, d=qrb.ap[0:64, :],
                               s=Qr_hbm[h, :, :]:
                               e.dma_start(out=d, in_=s)),
                      writes=[qrb], dma=True, tag=f"C_qr{h}")
                memset_step(mneg, float(3.0e38))
                for kc in range(16):
                    pump_w(1)
                    kpar = kc % 2
                    kcc, krc = kcb[kpar], krab[kpar]
                    kb.do("sync", (lambda e, d=kcc.ap,
                                   s=Kc_hbm[kc * 512:(kc + 1) * 512,
                                            h * 128:(h + 1) * 128]:
                                   e.dma_start(out=d, in_=s, transpose=True)),
                          writes=[kcc], dma=True, tag=f"C1_kc{h}_{kc}")
                    kb.do("sync", (lambda e, d=krc.ap[0:64, :],
                                   s=Kr_hbm[h, :, kc * 512:(kc + 1) * 512]:
                                   e.dma_start(out=d, in_=s)),
                          writes=[krc], dma=True, tag=f"C1_kr{h}_{kc}")
                    for j in range(NOWN):
                        if kc * 4 >= EXT[j]:
                            continue
                        bank = PS[kc % 2]
                        kb.do("tensor", (lambda e,
                                         l=qcb.ap[:, j * 128:(j + 1) * 128],
                                         r=kcc.ap, pb=bank.ap:
                                         e.matmul(pb, l, r, start=True, stop=False)),
                              reads=[qcb, kcc], writes=[bank], tag="C1mmc")
                        kb.do("tensor", (lambda e,
                                         l=qrb.ap[0:64, j * 128:(j + 1) * 128],
                                         r=krc.ap[0:64, :], pb=bank.ap:
                                         e.matmul(pb, l, r, start=False, stop=True)),
                              reads=[qrb, krc], writes=[bank], tag="C1mmr")
                        kb.do("vector", (lambda e, pb=bank.ap, d=ltr.ap[:, 0:1]:
                                         e.tensor_reduce(d, pb, AX.X, ALU.max,
                                                         negate=True)),
                              reads=[bank], writes=[ltr])
                        kb.do("vector", (lambda e, d=mneg.ap[:, j:j + 1],
                                         s=ltr.ap[:, 0:1]:
                                         e.tensor_tensor(d, d, s, ALU.min)),
                              reads=[mneg, ltr], writes=[mneg])
                if debug and h == 3:
                    kb.do("gpsimd", (lambda e: e.dma_start(
                        out=dbg["mneg"], in_=mneg.ap)),
                          reads=[mneg], dma=True)
                for j in range(NOWN):
                    kb.do("vector", (lambda e, s=mneg.ap[:, j:j + 1]:
                                     e.tensor_copy(mnegb.ap, s)),
                          reads=[mneg], writes=[mnegb])
                    bank = PSB[0]
                    kb.do("tensor", (lambda e, pb=bank.ap[:, 0:128]:
                                     e.transpose(pb[0:1, 0:128], mnegb.ap,
                                                 ident_b.ap)),
                          reads=[mnegb, ident_b], writes=[bank], tag="C_mt")
                    kb.do("vector", (lambda e, pb=bank.ap[0:1, 0:128],
                                     d=qrb.ap[64:65, j * 128:(j + 1) * 128]:
                                     e.tensor_copy(d, pb)),
                          reads=[bank], writes=[qrb])
                for half in range(2):
                    U = UP[2 * half + 1]
                    lbank = PS[6]
                    abank = PS[4]
                    qoff = half * 512
                    pending = None

                    def flush(ev):
                        kt_, pt_, vslice_ = ev
                        kb.do("tensor", (lambda e, r=pt_.ap, pb=lbank.ap[0:1, 0:512],
                                         st=(kt_ == 0), sp=(kt_ == U - 1):
                                         e.matmul(pb, ones_col_b.ap, r,
                                                  start=st, stop=sp)),
                              reads=[ones_col_b, pt_], writes=[lbank], tag="C2l")
                        kb.do("tensor", (lambda e, l=vslice_, r=pt_.ap,
                                         pb=abank.ap,
                                         st=(kt_ == 0), sp=(kt_ == U - 1):
                                         e.matmul(pb, l, r, start=st, stop=sp)),
                              reads=[vb[0], vb[1], pt_], writes=[abank],
                              tag="C2av")

                    for kc in range(U // 4):
                        pump_w(1)
                        kpar = kc % 2
                        kcc, krc, vcc = kcb[kpar], krab[kpar], vb[kpar]
                        kb.do("sync", (lambda e, d=kcc.ap,
                                       s=Kc_hbm[kc * 512:(kc + 1) * 512,
                                                h * 128:(h + 1) * 128]:
                                       e.dma_start(out=d, in_=s, transpose=True)),
                              writes=[kcc], dma=True, tag=f"C2_kc{h}_{half}_{kc}")
                        kb.do("sync", (lambda e, d=krc.ap[0:64, :],
                                       s=Kr_hbm[h, :, kc * 512:(kc + 1) * 512]:
                                       e.dma_start(out=d, in_=s)),
                              writes=[krc], dma=True, tag=f"C2_kr{h}_{half}_{kc}")
                        kb.do("sync", (lambda e,
                                       d=vcc.ap.rearrange("p (a n) -> p a n", a=4),
                                       s=V_hbm[kc * 512:(kc + 1) * 512,
                                               h * 128:(h + 1) * 128].rearrange(
                                           "(a p) n -> p a n", p=128):
                                       e.dma_start(out=d, in_=s)),
                              writes=[vcc], dma=True, tag=f"C2_v{h}_{half}_{kc}")
                        for kt_in in range(4):
                            kt = kc * 4 + kt_in
                            sbank = PS[2 + (kt % 2)]
                            pt = ptst[kt % 2]
                            kb.do("tensor", (lambda e,
                                             l=kcc.ap[:, kt_in * 128:(kt_in + 1) * 128],
                                             r=qcb.ap[:, qoff:qoff + 512],
                                             pb=sbank.ap:
                                             e.matmul(pb, l, r, start=True,
                                                      stop=False)),
                                  reads=[kcc, qcb], writes=[sbank], tag="C2mmc")
                            kb.do("tensor", (lambda e,
                                             l=krc.ap[0:65, kt_in * 128:(kt_in + 1) * 128],
                                             r=qrb.ap[0:65, qoff:qoff + 512],
                                             pb=sbank.ap:
                                             e.matmul(pb, l, r, start=False,
                                                      stop=True)),
                                  reads=[krc, qrb], writes=[sbank], tag="C2mmr")
                            if pending is not None:
                                flush(pending)
                            kb.do("scalar", (lambda e, pb=sbank.ap, d=pt.ap:
                                             e.activation(d, pb, AF.Exp,
                                                          scale=C["scl"].ap)),
                                  reads=[sbank, C["scl"]], writes=[pt],
                                  tag="C2exp")
                            for jj in range(4):
                                j = 4 * half + jj
                                if kt >= 8 * j:
                                    irel = kt - 8 * j
                                    kb.do("vector", (lambda e,
                                                     d=pt.ap[:, jj * 128:(jj + 1) * 128],
                                                     m=Mmask.ap[:, irel * 128:(irel + 1) * 128]:
                                                     e.tensor_tensor(d, d, m,
                                                                     ALU.mult)),
                                          reads=[pt, Mmask], writes=[pt],
                                          tag="C2msk")
                            pending = (kt, pt,
                                       vcc.ap[:, kt_in * 128:(kt_in + 1) * 128])
                    flush(pending)
                    kb.do("vector", (lambda e, pb=abank.ap,
                                     d=u_fm.ap[:, h * NOWN * 128 + half * 512: h * NOWN * 128 + half * 512 + 512]:
                                     e.tensor_copy(d, pb)),
                          reads=[abank], writes=[u_fm], tag="C2uev")
                    kb.do("vector", (lambda e, pb=lbank.ap[0:1, 0:512]:
                                     e.tensor_copy(lrow.ap, pb)),
                          reads=[lbank], writes=[lrow], tag="C2lev")
                    for qi in range(4):
                        j = 4 * half + qi
                        tb = PS[0]
                        kb.do("tensor", (lambda e, qq=qi, pb=tb.ap[:, 0:1]:
                                         e.transpose(
                                             pb, lrow.ap[0:1, qq * 128:(qq + 1) * 128],
                                             ones_row_f.ap[0:1, 0:1])),
                              reads=[lrow, ones_row_f], writes=[tb], tag="C2lt")
                        kb.do("vector", (lambda e, pb=tb.ap[:, 0:1],
                                         d=invl_pre.ap[:, j * 16 + h: j * 16 + h + 1]:
                                         e.tensor_copy(d, pb)),
                              reads=[tb], writes=[invl_pre], tag="C2ltev")
                pump_w(14)
            while wgen_live[0]:
                pump_w(50)
            for j in range(NOWN):
                kb.do("vector", (lambda e, s=invl_pre.ap[:, j * 16:(j + 1) * 16],
                                 d=ltr.ap[:, 0:16]:
                                 e.tensor_scalar(d, s, C["tiny"].ap, None,
                                                 ALU.max)),
                      reads=[invl_pre, C["tiny"]], writes=[ltr])
                kb.do("vector", (lambda e,
                                 d=invl.ap[:, j * 16:(j + 1) * 16]:
                                 e.reciprocal(d, ltr.ap[:, 0:16])),
                      reads=[ltr], writes=[invl])
            for j in range(NOWN):
                for h in range(H):
                    bank = PSB[(h % 2)]
                    kb.do("tensor", (lambda e,
                                     s=u_fm.ap[:, h * NOWN * 128 + j * 128: h * NOWN * 128 + (j + 1) * 128],
                                     pb=bank.ap[:, 0:128]:
                                     e.transpose(pb, s, ident_b.ap)),
                          reads=[u_fm, ident_b], writes=[bank], tag="Cf_ut")
                    kb.do("vector", (lambda e, pb=bank.ap[:, 0:128],
                                     d=attn_out.ap[:, j * D + h * 128: j * D + (h + 1) * 128],
                                     c=invl.ap[:, j * 16 + h: j * 16 + h + 1]:
                                     e.tensor_scalar(d, pb, c, None, ALU.mult)),
                          reads=[bank, invl], writes=[attn_out], tag="Cf_uev")
            if debug:
                for j in range(NOWN):
                    kb.do("gpsimd", (lambda e,
                                     s=attn_out.ap[:, j * D:(j + 1) * D],
                                     d=dbg["ao"][:, j * D:(j + 1) * D]:
                                     e.dma_start(out=d, in_=s)),
                          reads=[attn_out], dma=True)

        kb.barrier()

        arena.reset()
        subln_bc = arena.sub("subln_bc", [128, FF], BF16)
        subln_row = arena.sub("subln_row", [1, 512], F32)
        xo = [arena.sub(f"xo{i}", [128, D], F32) for i in range(2)]
        aoqT = [arena.sub(f"aoqT{i}", [128, D], BF16) for i in range(2)]
        h1 = [arena.sub(f"h1_{i}", [128, D], F32) for i in range(2)]
        act_b = [arena.sub(f"act{i}", [128, FF], BF16) for i in range(2)]
        h2 = xo[1]
        fqT = aoqT
        BqT = arena.sub("BqT", [128, FF], BF16)
        wdb = [arena.sub(f"wdb{i}", [128, 8 * 512], BF16) for i in range(4)]
        u_scrD = arena.sub("u_scrD", [128, D], F32)
        tq = arena.sub("tq", [128, D], BF16)
        rlu = arena.sub("rlu", [128, 512], BF16)
        rr2 = arena.sub("rr2", [128, 512], F32)
        scr512 = arena.sub("scr512", [128, 512], BF16)
        sscols = [arena.sub(f"sscols{i}", [128, 16], F32) for i in range(2)]
        mxD = arena.sub("mxD", [128, 1], F32)
        sD = arena.sub("sD", [128, 1], F32)
        sc1 = arena.sub("sc1", [128, 1], F32)
        sc2 = arena.sub("sc2", [128, 1], F32)
        dqD = [arena.sub(f"dqD{i}", [128, 4], F32) for i in range(2)]

        def ln_steps(hbuf, outbuf, eps_key="eps5"):
            kb.do("vector", (lambda e, s=hbuf.ap: e.tensor_reduce(
                mxD.ap, s, AX.X, ALU.add)),
                reads=[hbuf], writes=[mxD])
            kb.do("vector", lambda e: e.tensor_scalar(
                mxD.ap, mxD.ap, C["i2048"].ap, None, ALU.mult),
                reads=[mxD, C["i2048"]], writes=[mxD])
            kb.do("vector", (lambda e, s=hbuf.ap: e.tensor_scalar(
                s, s, mxD.ap, None, ALU.subtract)),
                reads=[hbuf, mxD], writes=[hbuf])
            kb.do("vector", (lambda e, s=hbuf.ap: e.scalar_tensor_tensor(
                u_scrD.ap, s, C["one"].ap, s, ALU.mult, ALU.mult,
                accum_out=sD.ap)),
                reads=[hbuf, C["one"]], writes=[u_scrD, sD])
            kb.do("scalar", (lambda e, ek=eps_key: e.activation(
                sD.ap, sD.ap, AF.Sqrt, bias=C[ek].ap, scale=C["i2048"].ap)),
                reads=[sD, C[eps_key], C["i2048"]], writes=[sD])
            kb.do("vector", lambda e: e.reciprocal(sc1.ap, sD.ap),
                  reads=[sD], writes=[sc1])
            kb.do("vector", (lambda e, s=hbuf.ap, d=outbuf.ap: e.tensor_scalar(
                d, s, sc1.ap, None, ALU.mult)),
                reads=[hbuf, sc1], writes=[outbuf])

        if stage >= 4:
            for i in range(16):
                bank = PS[6]
                kb.do("sync", (lambda e, s=subln_in[0:1, i * 512:(i + 1) * 512]:
                               e.dma_start(out=subln_row.ap, in_=s)),
                      writes=[subln_row], dma=True)
                kb.do("tensor", (lambda e,
                                 r=subln_row.ap[0:1, 0:512],
                                 pb=bank.ap:
                                 e.matmul(pb, ones_row_f.ap[0:1, :], r,
                                          start=True, stop=True)),
                      reads=[ones_row_f, subln_row], writes=[bank], tag="D_sb")
                kb.do("vector", (lambda e, pb=bank.ap,
                                 d=subln_bc.ap[:, i * 512:(i + 1) * 512]:
                                 e.tensor_copy(d, pb)),
                      reads=[bank], writes=[subln_bc])
            for grp in range(4):
                tiles = [2 * grp, 2 * grp + 1]
                for ti, t in enumerate(tiles):
                    kb.do("sync", (lambda e, d=xo[ti].ap,
                                   s=x_own[t * 128:(t + 1) * 128, :]:
                                   e.dma_start(out=d, in_=s)),
                          writes=[xo[ti]], dma=True, tag=f"D_x{t}")
                    ao = Buf(attn_out.ap[:, t * D:(t + 1) * D], "ao")
                    ao.res = attn_out.res
                    quant(ao, ao.ap, tq, dqD[ti], 0, D, mxD, sD, u_scrD)
                    kb.do("vector", (lambda e, d=dqD[ti].ap[:, 0:1],
                                     h=hws.ap[:, WI["wout"]:WI["wout"] + 1]:
                                     e.tensor_scalar(d, d, h, None, ALU.mult)),
                          reads=[dqD[ti], hws], writes=[dqD[ti]])
                    transposes(tq, (lambda i: tq.ap[:, i * 128:(i + 1) * 128]),
                               aoqT[ti], 0, 16)
                for oc in range(4):
                    for kh in range(2):
                        wb = wdb[(oc * 2 + kh) % 2]
                        kb.do("sync", (lambda e,
                                       d=wb.ap.rearrange("p (a n) -> p a n", a=8),
                                       s=tern["wout"][kh * 1024:(kh + 1) * 1024,
                                                      oc * 512:(oc + 1) * 512].rearrange(
                                           "(a p) n -> p a n", p=128):
                                       e.dma_start(out=d, in_=s)),
                              writes=[wb], dma=True, tag=f"D_wo{oc}_{kh}")
                        for ti in range(2):
                            bank = PS[2 + ti]
                            for ic in range(8):
                                kb.do("tensor", (lambda e,
                                                 l=aoqT[ti].ap[:, kh * 1024 + ic * 128: kh * 1024 + (ic + 1) * 128],
                                                 r=wb.ap[:, ic * 512:(ic + 1) * 512],
                                                 st=(kh == 0 and ic == 0),
                                                 sp=(kh == 1 and ic == 7),
                                                 pb=bank.ap:
                                                 e.matmul(pb, l, r, start=st,
                                                          stop=sp)),
                                      reads=[aoqT[ti], wb], writes=[bank],
                                      tag="D_womm")
                    for ti in range(2):
                        bank = PS[2 + ti]
                        kb.do("vector", (lambda e, pb=bank.ap,
                                         x=xo[ti].ap[:, oc * 512:(oc + 1) * 512],
                                         d=h1[ti].ap[:, oc * 512:(oc + 1) * 512],
                                         q=dqD[ti].ap[:, 0:1]:
                                         e.scalar_tensor_tensor(
                                             d, pb, q, x, ALU.mult, ALU.add)),
                              reads=[bank, xo[ti], dqD[ti]], writes=[h1[ti]],
                              tag="D_woev")
                for ti in range(2):
                    ln_steps(h1[ti], h1[ti])
                    quant(h1[ti], h1[ti].ap, tq, dqD[ti], 1, D, mxD, sD, u_scrD)
                    transposes(tq, (lambda i: tq.ap[:, i * 128:(i + 1) * 128]),
                               fqT[ti], 0, 16)
                    memset_step(sscols[ti], 0.0)
                for oc in range(16):
                    for kh in range(2):
                        wgb = wdb[kh % 2]
                        wub = wdb[2 + kh % 2]
                        kb.do("sync", (lambda e,
                                       d=wgb.ap.rearrange("p (a n) -> p a n", a=8),
                                       s=tern["wg"][kh * 1024:(kh + 1) * 1024,
                                                    oc * 512:(oc + 1) * 512].rearrange(
                                           "(a p) n -> p a n", p=128):
                                       e.dma_start(out=d, in_=s)),
                              writes=[wgb], dma=True, tag=f"D_wg{oc}_{kh}")
                        kb.do("sync", (lambda e,
                                       d=wub.ap.rearrange("p (a n) -> p a n", a=8),
                                       s=tern["wu"][kh * 1024:(kh + 1) * 1024,
                                                    oc * 512:(oc + 1) * 512].rearrange(
                                           "(a p) n -> p a n", p=128):
                                       e.dma_start(out=d, in_=s)),
                              writes=[wub], dma=True, tag=f"D_wu{oc}_{kh}")
                        for ti in range(2):
                            gbank = PS[2 + ti]
                            ubank = PS[4 + ti]
                            for ic in range(8):
                                kb.do("tensor", (lambda e,
                                                 l=fqT[ti].ap[:, kh * 1024 + ic * 128: kh * 1024 + (ic + 1) * 128],
                                                 r=wgb.ap[:, ic * 512:(ic + 1) * 512],
                                                 st=(kh == 0 and ic == 0),
                                                 sp=(kh == 1 and ic == 7),
                                                 pb=gbank.ap:
                                                 e.matmul(pb, l, r, start=st,
                                                          stop=sp)),
                                      reads=[fqT[ti], wgb], writes=[gbank],
                                      tag="D_gmm")
                                kb.do("tensor", (lambda e,
                                                 l=fqT[ti].ap[:, kh * 1024 + ic * 128: kh * 1024 + (ic + 1) * 128],
                                                 r=wub.ap[:, ic * 512:(ic + 1) * 512],
                                                 st=(kh == 0 and ic == 0),
                                                 sp=(kh == 1 and ic == 7),
                                                 pb=ubank.ap:
                                                 e.matmul(pb, l, r, start=st,
                                                          stop=sp)),
                                      reads=[fqT[ti], wub], writes=[ubank],
                                      tag="D_umm")
                    for ti in range(2):
                        gbank = PS[2 + ti]
                        ubank = PS[4 + ti]
                        kb.do("scalar", (lambda e, pb=gbank.ap, d=rlu.ap:
                                         e.activation(d, pb, AF.Relu)),
                              reads=[gbank], writes=[rlu], tag="D_relu")
                        kb.do("vector", lambda e: e.tensor_tensor(
                            rr2.ap, rlu.ap, rlu.ap, ALU.mult),
                            reads=[rlu], writes=[rr2], tag="D_rr")
                        kb.do("vector", (lambda e, pb=ubank.ap,
                                         d=act_b[ti].ap[:, oc * 512:(oc + 1) * 512]:
                                         e.tensor_tensor(d, rr2.ap, pb, ALU.mult)),
                              reads=[rr2, ubank], writes=[act_b[ti]], tag="D_act")
                        kb.do("vector", (lambda e,
                                         s=act_b[ti].ap[:, oc * 512:(oc + 1) * 512],
                                         acc=sscols[ti].ap[:, oc:oc + 1]:
                                         e.scalar_tensor_tensor(
                                             scr512.ap, s, C["one"].ap, s,
                                             ALU.mult, ALU.mult, accum_out=acc)),
                              reads=[act_b[ti], C["one"]],
                              writes=[scr512, sscols[ti]], tag="D_ss")
                for ti, t in enumerate(tiles):
                    kb.do("vector", (lambda e, d=sc1.ap,
                                     h=hws.ap[:, WI["wg"]:WI["wg"] + 1],
                                     ix=dqD[ti].ap[:, 1:2]:
                                     e.tensor_scalar(d, h, ix, None, ALU.mult)),
                          reads=[hws, dqD[ti]], writes=[sc1])
                    kb.do("vector", (lambda e, d=sc2.ap,
                                     h=hws.ap[:, WI["wu"]:WI["wu"] + 1],
                                     ix=dqD[ti].ap[:, 1:2]:
                                     e.tensor_scalar(d, h, ix, None, ALU.mult)),
                          reads=[hws, dqD[ti]], writes=[sc2])
                    kb.do("vector", lambda e: e.tensor_tensor(
                        sc1.ap, sc1.ap, sc1.ap, ALU.mult),
                        reads=[sc1], writes=[sc1])
                    kb.do("vector", (lambda e, d=dqD[ti].ap[:, 2:3]:
                                     e.tensor_tensor(d, sc1.ap, sc2.ap, ALU.mult)),
                          reads=[sc1, sc2], writes=[dqD[ti]])
                    kb.do("vector", (lambda e, s=sscols[ti].ap:
                                     e.tensor_reduce(mxD.ap, s, AX.X, ALU.add)),
                          reads=[sscols[ti]], writes=[mxD])
                    kb.do("vector", (lambda e, q=dqD[ti].ap[:, 2:3]:
                                     e.tensor_scalar(sc1.ap, q, q, None, ALU.mult)),
                          reads=[dqD[ti]], writes=[sc1])
                    kb.do("vector", lambda e: e.tensor_tensor(
                        mxD.ap, mxD.ap, sc1.ap, ALU.mult),
                        reads=[mxD, sc1], writes=[mxD])
                    kb.do("scalar", lambda e: e.activation(
                        mxD.ap, mxD.ap, AF.Sqrt, bias=C["eps6"].ap,
                        scale=C["i8192"].ap),
                        reads=[mxD, C["eps6"], C["i8192"]], writes=[mxD])
                    kb.do("vector", lambda e: e.reciprocal(sc2.ap, mxD.ap),
                          reads=[mxD], writes=[sc2])
                    kb.do("vector", (lambda e, q=dqD[ti].ap[:, 2:3]:
                                     e.tensor_tensor(sc2.ap, sc2.ap, q, ALU.mult)),
                          reads=[sc2, dqD[ti]], writes=[sc2])
                    for oc in range(16):
                        kb.do("vector", (lambda e,
                                         s=act_b[ti].ap[:, oc * 512:(oc + 1) * 512],
                                         g=subln_bc.ap[:, oc * 512:(oc + 1) * 512]:
                                         e.scalar_tensor_tensor(
                                             s, s, sc2.ap, g, ALU.mult, ALU.mult)),
                              reads=[act_b[ti], sc2, subln_bc],
                              writes=[act_b[ti]], tag="D_sub")
                    kb.do("vector", (lambda e, s=act_b[ti].ap:
                                     e.tensor_reduce(mxD.ap, s, AX.X, ALU.max,
                                                     apply_absolute_value=True)),
                          reads=[act_b[ti]], writes=[mxD])
                    kb.do("vector", lambda e: e.tensor_scalar(
                        mxD.ap, mxD.ap, C["eps5"].ap, None, ALU.max),
                        reads=[mxD, C["eps5"]], writes=[mxD])
                    kb.do("vector", (lambda e, d=dqD[ti].ap[:, 3:4]:
                                     e.tensor_scalar(d, mxD.ap, C["i127"].ap,
                                                     None, ALU.mult)),
                          reads=[mxD, C["i127"]], writes=[dqD[ti]])
                    kb.do("vector", (lambda e, d=dqD[ti].ap[:, 3:4],
                                     h=hws.ap[:, WI["wd"]:WI["wd"] + 1]:
                                     e.tensor_scalar(d, d, h, None, ALU.mult)),
                          reads=[dqD[ti], hws], writes=[dqD[ti]])
                    kb.do("vector", lambda e: e.reciprocal(sD.ap, mxD.ap),
                          reads=[mxD], writes=[sD])
                    kb.do("vector", lambda e: e.tensor_scalar(
                        sD.ap, sD.ap, C["c127"].ap, None, ALU.mult),
                        reads=[sD, C["c127"]], writes=[sD])
                    for oc in range(16):
                        kb.do("scalar", (lambda e,
                                         s=act_b[ti].ap[:, oc * 512:(oc + 1) * 512],
                                         d=u_scrD.ap[:, 0:512]:
                                         e.activation(d, s, AF.Identity,
                                                      bias=C["magic"].ap,
                                                      scale=sD.ap)),
                              reads=[act_b[ti], sD, C["magic"]],
                              writes=[u_scrD], tag="D_bq1")
                        kb.do("vector", (lambda e,
                                         d=act_b[ti].ap[:, oc * 512:(oc + 1) * 512],
                                         s=u_scrD.ap[:, 0:512]:
                                         e.tensor_scalar(d, s, C["magic"].ap,
                                                         None, ALU.subtract)),
                              reads=[u_scrD, C["magic"]], writes=[act_b[ti]],
                              tag="D_bq2")
                    transposes(act_b[ti],
                               (lambda i, tb=act_b[ti]: tb.ap[:, i * 128:(i + 1) * 128]),
                               BqT, 0, 64)
                    for oc2 in range(4):
                        bank = PS[2 + ti]
                        for kc in range(4):
                            wb = wdb[kc % 2]
                            kb.do("sync", (lambda e,
                                           d=wb.ap.rearrange("p (a n) -> p a n", a=8),
                                           s=tern["wd"][kc * 2048:(kc * 2048 + 1024),
                                                        oc2 * 512:(oc2 + 1) * 512].rearrange(
                                               "(a p) n -> p a n", p=128):
                                           e.dma_start(out=d, in_=s)),
                                  writes=[wb], dma=True, tag=f"D_wd")
                            wb2 = wdb[2 + kc % 2]
                            kb.do("sync", (lambda e,
                                           d=wb2.ap.rearrange("p (a n) -> p a n", a=8),
                                           s=tern["wd"][(kc * 2048 + 1024):(kc + 1) * 2048,
                                                        oc2 * 512:(oc2 + 1) * 512].rearrange(
                                               "(a p) n -> p a n", p=128):
                                           e.dma_start(out=d, in_=s)),
                                  writes=[wb2], dma=True, tag=f"D_wd2")
                            for hf, wbx in ((0, wb), (1, wb2)):
                                for ic in range(8):
                                    icg = kc * 16 + hf * 8 + ic
                                    kb.do("tensor", (lambda e,
                                                     l=BqT.ap[:, icg * 128:(icg + 1) * 128],
                                                     r=wbx.ap[:, ic * 512:(ic + 1) * 512],
                                                     st=(icg == 0), sp=(icg == 63),
                                                     pb=bank.ap:
                                                     e.matmul(pb, l, r, start=st,
                                                              stop=sp)),
                                          reads=[BqT, wbx], writes=[bank],
                                          tag="D_dmm")
                        kb.do("vector", (lambda e, pb=bank.ap,
                                         x=h1[ti].ap[:, oc2 * 512:(oc2 + 1) * 512],
                                         d=h2.ap[:, oc2 * 512:(oc2 + 1) * 512],
                                         q=dqD[ti].ap[:, 3:4]:
                                         e.scalar_tensor_tensor(
                                             d, pb, q, x, ALU.mult, ALU.add)),
                              reads=[bank, h1[ti], dqD[ti]], writes=[h2],
                              tag="D_dev")
                    ln_steps(h2, h2)
                    kb.do("gpsimd", (lambda e, s=h2.ap,
                                     d=y_out[t * 128:(t + 1) * 128, :]:
                                     e.dma_start(out=d, in_=s)),
                          reads=[h2], dma=True, tag=f"D_y{t}")

        last_q = {}
        for st in kb.steps:
            if st.dma:
                last_q[st.queue] = st
        kb.do("gpsimd", lambda e: e.nop(),
              extra=[s for s in last_q.values()], tag="final_wait")

        kb.emit(block)

    return nc


_nc_cache = {}


def _get_nc():
    key = (STAGE, DEBUG_DUMPS)
    if key not in _nc_cache:
        _nc_cache[key] = build_program(STAGE, DEBUG_DUMPS)
    return _nc_cache[key]


def kernel(L0_attn_x, L0_attn_wq_content, L0_attn_wq_rope, L0_attn_wkv_down,
           L0_attn_wk_up, L0_attn_wv_up, L0_attn_wkr_up, L0_attn_wout,
           L0_attn_qn_w, L0_attn_kn_w, L0_ffn_w_gate, L0_ffn_w_up,
           L0_ffn_w_down, L0_ffn_subln_weight):
    f32 = lambda a: np.ascontiguousarray(np.asarray(a, dtype=np.float32))
    x = f32(L0_attn_x).reshape(T, D)
    wT = {
        "wqc": f32(L0_attn_wq_content).T, "wqr": f32(L0_attn_wq_rope).T,
        "wkvd": f32(L0_attn_wkv_down).T, "wku": f32(L0_attn_wk_up).T,
        "wvu": f32(L0_attn_wv_up).T, "wkr": f32(L0_attn_wkr_up).T,
        "wout": f32(L0_attn_wout).T, "wg": f32(L0_ffn_w_gate).T,
        "wu": f32(L0_ffn_w_up).T, "wd": f32(L0_ffn_w_down).T,
    }
    wT = {k: np.ascontiguousarray(v) for k, v in wT.items()}
    qn = f32(L0_attn_qn_w).reshape(128, 1)
    kn = f32(L0_attn_kn_w).reshape(128, 1)
    subln = f32(L0_ffn_subln_weight).reshape(1, FF)
    xt = x.reshape(NT, 128, D)
    in_maps = []
    for c in range(NCORES):
        own = np.ascontiguousarray(
            xt[[8 * j + c for j in range(NOWN)]].reshape(NOWN * 128, D))
        m = {"x_full": x, "x_own": own,
             "cofs": np.full((128, 1), 128.0 * c, np.float32),
             "qn": qn, "kn": kn, "subln": subln}
        for nm in wT:
            m[nm + "T"] = wT[nm]
        in_maps.append(m)
    nc = _get_nc()
    res = run_bass_kernel_spmd(nc, in_maps, core_ids=list(range(NCORES)))
    y = np.empty((NT, 128, D), dtype=np.float32)
    for c in range(NCORES):
        yc = res.results[c]["y"].reshape(NOWN, 128, D)
        for j in range(NOWN):
            y[8 * j + c] = yc[j]
    return y.reshape(1, T, D)


